# version 03
# baseline (speedup 1.0000x reference)
import math
import numpy as np
import ml_dtypes
import concourse.bass as bass
import concourse.mybir as mybir
from concourse.bass_utils import run_bass_kernel_spmd
from concourse.bass_core_rust import IndirectOffsetOnAxis

F32 = mybir.dt.float32
BF16 = mybir.dt.bfloat16
I32 = mybir.dt.int32
AF = mybir.ActivationFunctionType
ALU = mybir.AluOpType
AX = mybir.AxisListType
BF = ml_dtypes.bfloat16

B = 8
T = 8192
D = 1024
KT = 8
HID = 4096
MT = 32
NE = 9
H = 16
HD = 64
CH = 512
NCH = 16
TPC = 4
EPS = 1e-5
QSCALE = float(HD) ** -0.5
NTAU = NCH * TPC
POOL_PC = 104


def hilbert_perm(M: int) -> np.ndarray:
    g = int(math.ceil(math.sqrt(M)))
    grid = np.arange(g * g).reshape(g, g)
    grid[1::2] = grid[1::2, ::-1]
    flat = grid.reshape(-1)
    flat = flat[flat < M][:M]
    return flat.astype(np.int32)


def fj_xgt(tt, k): return 8 * tt + k
def fj_qt(ft): return 32 + ft
def fj_kt(ft): return 40 + ft
def fj_vt(seg, nh): return 48 + 2 * seg + nh
def fj_wo(tt, dh): return 56 + 12 * tt + dh
def fj_y1t(tt, k): return 56 + 12 * tt + 2 + k
def fj_rt(tt): return 56 + 12 * tt + 10
def fj_gt(tt): return 56 + 12 * tt + 11


class CSem:
    def __init__(self, nc, name):
        self.h = nc.alloc_semaphore(name)
        self.n = 0

    def inc(self, inst, k=1):
        inst.then_inc(self.h, k)
        self.n += k
        return self.n


def build_program(test_ffn_only=False):
    nc = bass.Bass("TRN2", target_bir_lowering=False)

    x = nc.dram_tensor("x", [T, D], F32, kind="ExternalInput").ap()
    idxT = nc.dram_tensor("idxT", [128, NTAU], I32, kind="ExternalInput").ap()
    ident32 = nc.dram_tensor("ident32", [128, 128], F32, kind="ExternalInput").ap()
    identbf = nc.dram_tensor("identbf", [128, 128], BF16, kind="ExternalInput").ap()
    wqkvT32 = nc.dram_tensor("wqkvT32", [128, KT * 3072], F32, kind="ExternalInput").ap()
    woutT32 = nc.dram_tensor("woutT32", [128, KT * 1024], F32, kind="ExternalInput").ap()
    wrT32 = nc.dram_tensor("wrT32", [128, KT * 8], F32, kind="ExternalInput").ap()
    bfcT32 = nc.dram_tensor("bfcT32", [128, NE * MT], F32, kind="ExternalInput").ap()
    cst32 = nc.dram_tensor("cst32", [128, 8], F32, kind="ExternalInput").ap()
    bprojT32 = nc.dram_tensor("bprojT32", [9, D], F32, kind="ExternalInput").ap()
    WHALF = D * HID // 2
    w1t = nc.dram_tensor("w1t", [NE, 2, WHALF], F32, kind="ExternalInput").ap()
    w2t = nc.dram_tensor("w2t", [NE, 2, WHALF], F32, kind="ExternalInput").ap()
    out = nc.dram_tensor("out", [T, D], F32, kind="ExternalOutput").ap()

    w1s = nc.dram_tensor("w1s", [NE, 2, WHALF], BF16, kind="Internal").ap()
    w2s = nc.dram_tensor("w2s", [NE, 2, WHALF], BF16, kind="Internal").ap()

    def w1_chunk4(e, mc):
        h, ml = divmod(mc, 4)
        blk = 4 * KT * 128 * 128
        return w1s[e, h][ml * blk:(ml + 1) * blk].rearrange(
            "(m p kc) -> p m kc", m=4, p=128)

    def w2_slice(e, q):
        h, ql = divmod(q, 4)
        blk = 4 * 1024 * 128
        return w2s[e, h][ql * blk:(ql + 1) * blk].rearrange("(p c) -> p c", p=128)
    y1kind = "ExternalInput" if test_ffn_only else "Internal"
    y1Ts = nc.dram_tensor("y1Ts", [NCH, 128, KT * CH], BF16, kind=y1kind).ap()
    y1nat = nc.dram_tensor("y1nat", [T, D], F32, kind=y1kind).ap()
    if test_ffn_only:
        gate_in = nc.dram_tensor("gate_in", [128, NTAU * 8], F32, kind="ExternalInput").ap()
        gateT_in = nc.dram_tensor("gateT_in", [9, T], BF16, kind="ExternalInput").ap()

    SSETS = CSem(nc, "SSETS")
    SSETG = CSem(nc, "SSETG")
    SIDX = CSem(nc, "SIDX")
    SG = [CSem(nc, f"SG{i}") for i in range(3)]
    SXR = [CSem(nc, f"SXR{i}") for i in range(2)]
    SCAST = CSem(nc, "SCAST")
    SY1TW = CSem(nc, "SY1TW")
    SY1NW = [CSem(nc, f"SY1NW{i}") for i in range(2)]
    SW1 = [CSem(nc, f"SW1_{i}") for i in range(3)]
    SW2 = [CSem(nc, f"SW2_{i}") for i in range(4)]
    SY1G = [CSem(nc, f"SY1G{i}") for i in range(2)]
    SYN = CSem(nc, "SYN")
    SOUT = [CSem(nc, f"SOUT{i}") for i in range(2)]

    PE_F = CSem(nc, "PE_F")
    PE_SC = CSem(nc, "PE_SC")
    PE_AT = CSem(nc, "PE_AT")
    PE_OT = CSem(nc, "PE_OT")
    V_POOL = CSem(nc, "V_POOL")
    V_NM = CSem(nc, "V_NM")
    S_EXP = CSem(nc, "S_EXP")
    V_ABF = CSem(nc, "V_ABF")
    V_ATS = CSem(nc, "V_ATS")
    V_OTS = CSem(nc, "V_OTS")
    V_D = CSem(nc, "V_D")
    S_SQ = CSem(nc, "S_SQ")
    S_RSTD = CSem(nc, "S_RSTD")
    V_Y1 = CSem(nc, "V_Y1")
    V_NMR = CSem(nc, "V_NMR")
    S_EXPR = CSem(nc, "S_EXPR")
    V_GATE = CSem(nc, "V_GATE")
    VS = CSem(nc, "VS")
    SS = CSem(nc, "SS")

    PE_S1 = CSem(nc, "PE_S1")
    S_GELU = CSem(nc, "S_GELU")
    PE_S2 = CSem(nc, "PE_S2")
    V_FFE = CSem(nc, "V_FFE")
    W2FREE = CSem(nc, "W2FREE")
    V_D2 = CSem(nc, "V_D2")
    S_SQ2 = CSem(nc, "S_SQ2")
    S_RSTD2 = CSem(nc, "S_RSTD2")
    V_OUTR = CSem(nc, "V_OUTR")
    V_FIN = CSem(nc, "V_FIN")

    SETS_TGT = 16 * (6 if test_ffn_only else 4)
    SETG_TGT = 16 * (2 if test_ffn_only else 4)

    def wait_setup(eng):
        eng.wait_ge(SSETS.h, SETS_TGT)
        eng.wait_ge(SSETG.h, SETG_TGT)

    sb_ident32 = nc.alloc_sbuf_tensor("sb_ident32", [128, 128], F32)
    sb_identbf = nc.alloc_sbuf_tensor("sb_identbf", [128, 128], BF16)
    sb_idxT = nc.alloc_sbuf_tensor("sb_idxT", [128, NTAU], I32)
    sb_bfcT = nc.alloc_sbuf_tensor("sb_bfcT", [128, NE * MT], F32)
    sb_cst = nc.alloc_sbuf_tensor("sb_cst", [128, 8], F32)
    sb_bprojT = nc.alloc_sbuf_tensor("sb_bprojT", [9, D], BF16)
    sb_wr = nc.alloc_sbuf_tensor("sb_wr", [128, KT * 8], BF16)
    sb_gaten = nc.alloc_sbuf_tensor("sb_gaten", [128, NTAU * 8], F32)
    sb_gateT = nc.alloc_sbuf_tensor("sb_gateT", [9, T], BF16)

    def _flat2d(ap):
        return ap.rearrange("(r c) -> r c", c=65536)

    cast_jobs = []
    for e in range(NE):
        cast_jobs.append((_flat2d(w1s[e, 0]), _flat2d(w1t[e, 0])))
        cast_jobs.append((_flat2d(w1s[e, 1]), _flat2d(w1t[e, 1])))
        cast_jobs.append((_flat2d(w2s[e, 0]), _flat2d(w2t[e, 0])))
        cast_jobs.append((_flat2d(w2s[e, 1]), _flat2d(w2t[e, 1])))

    def fidx(c, j):
        return POOL_PC * c + j

    import contextlib

    with contextlib.ExitStack() as stA:
        if not test_ffn_only:
            sb_wqkvT = stA.enter_context(nc.sbuf_tensor("sb_wqkvT", [128, KT * 3072], BF16))
            sb_woutT = stA.enter_context(nc.sbuf_tensor("sb_woutT", [128, KT * 1024], BF16))
            sb_xg = [stA.enter_context(nc.sbuf_tensor(f"sb_xg{i}", [128, D], F32)) for i in range(3)]
            sb_xgT = stA.enter_context(nc.sbuf_tensor("sb_xgT", [128, KT * CH], BF16))
            sb_qT = stA.enter_context(nc.sbuf_tensor("sb_qT", [128, KT * CH], BF16))
            sb_kT = stA.enter_context(nc.sbuf_tensor("sb_kT", [128, KT * CH], BF16))
            sb_vnat = stA.enter_context(nc.sbuf_tensor("sb_vnat", [64, 4096], BF16))
            sb_pfp = [stA.enter_context(nc.sbuf_tensor(f"sb_pfp{i}", [128, 64], F32)) for i in range(2)]
            sb_abf = [stA.enter_context(nc.sbuf_tensor(f"sb_abf{i}", [128, 64], BF16)) for i in range(2)]
            sb_atT = [stA.enter_context(nc.sbuf_tensor(f"sb_atT{i}", [64, 128], BF16)) for i in range(2)]
            sb_oT = stA.enter_context(nc.sbuf_tensor("sb_oT", [128, KT * CH], BF16))
            sb_xr = [stA.enter_context(nc.sbuf_tensor(f"sb_xr{i}", [128, D], F32)) for i in range(2)]
            sb_junk = stA.enter_context(nc.sbuf_tensor("sb_junk", [128, D], F32))
            sb_y1 = [stA.enter_context(nc.sbuf_tensor(f"sb_y1{i}", [128, D], F32)) for i in range(2)]
            sb_y1T = stA.enter_context(nc.sbuf_tensor("sb_y1T", [128, KT * CH], BF16))
            sb_ast = [stA.enter_context(nc.sbuf_tensor(f"sb_ast{i}", [128, 4], F32)) for i in range(2)]
            sb_lst = [stA.enter_context(nc.sbuf_tensor(f"sb_lst{i}", [128, 8], F32)) for i in range(2)]
            sb_lg = [stA.enter_context(nc.sbuf_tensor(f"sb_lg{i}", [128, 8], F32)) for i in range(2)]
            sb_rexp = [stA.enter_context(nc.sbuf_tensor(f"sb_rexp{i}", [128, 8], F32)) for i in range(2)]
            sb_rtmp = [stA.enter_context(nc.sbuf_tensor(f"sb_rtmp{i}", [128, 8], F32)) for i in range(2)]
            sb_rst = [stA.enter_context(nc.sbuf_tensor(f"sb_rst{i}", [128, 8], F32)) for i in range(2)]
            sb_gaug = stA.enter_context(nc.sbuf_tensor("sb_gaug", [128, 4 * 9], F32))
            pool_ps = [stA.enter_context(nc.psum_tensor(f"pool_ps{i}", [128, 512], F32)) for i in range(2)]
            sc_ps = [stA.enter_context(nc.psum_tensor(f"sc_ps{i}", [128, 512], F32)) for i in range(2)]
            at_ps = [stA.enter_context(nc.psum_tensor(f"at_ps{i}", [128, 1024], BF16)) for i in range(2)]
            ot_ps = [stA.enter_context(nc.psum_tensor(f"ot_ps{i}", [128, 512], F32)) for i in range(2)]

        with nc.Block() as blk:
            @blk.sync
            def _(sy):
                SSETS.inc(sy.dma_start(sb_ident32[:], ident32), 16)
                SSETS.inc(sy.dma_start(sb_identbf[:], identbf), 16)
                SIDX.inc(sy.dma_start(sb_idxT[:], idxT), 16)
                SSETS.inc(sy.dma_start(sb_bfcT[:], bfcT32), 16)
                SSETS.inc(sy.dma_start(sb_cst[:], cst32), 16)
                if test_ffn_only:
                    SSETS.inc(sy.dma_start(sb_gaten[:], gate_in), 16)
                    SSETS.inc(sy.dma_start(sb_gateT[:], gateT_in), 16)
                    return
                for tau in range(2):
                    SXR[tau % 2].inc(sy.dma_start(sb_xr[tau % 2][:], x[tau * 128:(tau + 1) * 128, :]), 16)
                for c in range(NCH):
                    for half in range(2):
                        for tt in range(2):
                            tau = 4 * c + 2 * half + tt + 2
                            if tau < NTAU:
                                sy.wait_ge(S_SQ.h, tau - 1)
                                sy.wait_ge(V_Y1.h, tau - 1)
                                SXR[tau % 2].inc(sy.dma_start(sb_xr[tau % 2][:], x[tau * 128:(tau + 1) * 128, :]), 16)
                        for tt in range(2):
                            tau = 4 * c + 2 * half + tt
                            sy.wait_ge(V_Y1.h, tau + 1)
                            SY1NW[tau % 2].inc(sy.dma_start(y1nat[tau * 128:(tau + 1) * 128, :], sb_y1[tau % 2][:]), 16)
                    sy.wait_ge(V_POOL.h, fidx(c, fj_y1t(3, 7)) + 1)
                    SY1TW.inc(sy.dma_start(y1Ts[c], sb_y1T[:]), 16)
                if True:
                    for s in [SXR[0], SXR[1], SY1NW[0], SY1NW[1], SY1TW]:
                        sy.wait_ge(s.h, s.n)

            @blk.gpsimd
            def _(gp):
                if not test_ffn_only:
                    SSETG.inc(gp.dma_start(sb_wqkvT[:], wqkvT32), 16)
                    SSETG.inc(gp.dma_start(sb_woutT[:], woutT32), 16)
                SSETG.inc(gp.dma_start(sb_wr[:], wrT32), 16)
                SSETG.inc(gp.dma_start(sb_bprojT[:], bprojT32), 16)
                if test_ffn_only:
                    for dst, src in cast_jobs:
                        SCAST.inc(gp.dma_start(dst, src), 16)
                    return
                gp.wait_ge(SIDX.h, 16)
                cj = 0
                for tau in range(NTAU):
                    if tau >= 3:
                        tp = tau - 3
                        gp.wait_ge(PE_F.h, fidx(tp // 4, 8 * (tp % 4 + 1)))
                    SG[tau % 3].inc(gp.indirect_dma_start(
                        sb_xg[tau % 3][:], None, x,
                        IndirectOffsetOnAxis(sb_idxT[:, tau:tau + 1], 0)), 16)
                    ncast = 2 if tau < 12 else 1
                    for _ in range(ncast):
                        if cj < len(cast_jobs):
                            dst, src = cast_jobs[cj]
                            SCAST.inc(gp.dma_start(dst, src), 16)
                            cj += 1
                while cj < len(cast_jobs):
                    dst, src = cast_jobs[cj]
                    SCAST.inc(gp.dma_start(dst, src), 16)
                    cj += 1
                for s in SG:
                    gp.wait_ge(s.h, s.n)



            if not test_ffn_only:
                @blk.tensor
                def _(pe):
                    wait_setup(pe)
                    for c in range(NCH):
                        for tt in range(TPC):
                            tau = 4 * c + tt
                            pe.wait_ge(SG[tau % 3].h, 16 * (tau // 3 + 1))
                            for k in range(KT):
                                n = fidx(c, fj_xgt(tt, k))
                                if n >= 2:
                                    pe.wait_ge(V_POOL.h, n - 1)
                                PE_F.inc(pe.transpose(
                                    pool_ps[n % 2][:, 0:128],
                                    sb_xg[tau % 3][:, k * 128:(k + 1) * 128],
                                    sb_ident32[:]))
                        pe.wait_ge(V_POOL.h, fidx(c, 32))
                        for ft in range(16):
                            n = fidx(c, 32 + ft)
                            pe.wait_ge(V_POOL.h, n - 1)
                            for k in range(KT):
                                inst = pe.matmul(
                                    pool_ps[n % 2][:, :],
                                    sb_wqkvT[:, k * 3072 + ft * 128: k * 3072 + (ft + 1) * 128],
                                    sb_xgT[:, k * CH:(k + 1) * CH],
                                    start=(k == 0), stop=(k == KT - 1))
                            PE_F.inc(inst)
                        for seg in range(4):
                            for nh in range(2):
                                n = fidx(c, fj_vt(seg, nh))
                                pe.wait_ge(V_POOL.h, n - 1)
                                for k in range(KT):
                                    inst = pe.matmul(
                                        pool_ps[n % 2][0:64, :],
                                        sb_xgT[:, k * CH + seg * 128: k * CH + (seg + 1) * 128: 2],
                                        sb_wqkvT[:, k * 3072 + 2048 + nh * 512: k * 3072 + 2048 + (nh + 1) * 512],
                                        start=(k == 0), stop=(k == KT - 1))
                                PE_F.inc(inst)
                        pe.wait_ge(V_POOL.h, fidx(c, 56))
                        for sl in range(4):
                            for h in range(H):
                                i = 64 * c + 16 * sl + h
                                pb = (h % 2) * 64
                                qb = (h // 2) * CH + sl * 128
                                if i >= 2:
                                    pe.wait_ge(S_EXP.h, i - 1)
                                PE_SC.inc(pe.matmul(
                                    sc_ps[i % 2][:, 0:64],
                                    sb_qT[pb:pb + 64, qb:qb + 128],
                                    sb_kT[pb:pb + 64, qb:qb + 128:2],
                                    start=True, stop=True))
                                pe.wait_ge(V_ABF.h, i + 1)
                                if i >= 2:
                                    pe.wait_ge(V_ATS.h, i - 1)
                                PE_AT.inc(pe.transpose(
                                    at_ps[i % 2][0:64, 0:128], sb_abf[i % 2][:], sb_identbf[:]))
                                pe.wait_ge(V_ATS.h, i + 1)
                                if i >= 2:
                                    pe.wait_ge(V_OTS.h, i - 1)
                                PE_OT.inc(pe.matmul(
                                    ot_ps[i % 2][0:64, 0:128],
                                    sb_vnat[0:64, sl * 1024 + h * 64: sl * 1024 + (h + 1) * 64],
                                    sb_atT[i % 2][0:64, :],
                                    start=True, stop=True))
                        pe.wait_ge(V_OTS.h, 64 * (c + 1))
                        for tt in range(TPC):
                            tau = 4 * c + tt
                            for dh in range(2):
                                n = fidx(c, fj_wo(tt, dh))
                                pe.wait_ge(V_POOL.h, n - 1)
                                for k in range(KT):
                                    inst = pe.matmul(
                                        pool_ps[n % 2][:, :],
                                        sb_oT[:, k * CH + tt * 128: k * CH + (tt + 1) * 128],
                                        sb_woutT[:, k * 1024 + dh * 512: k * 1024 + (dh + 1) * 512],
                                        start=(k == 0), stop=(k == KT - 1))
                                PE_F.inc(inst)
                            pe.wait_ge(V_Y1.h, tau + 1)
                            for k in range(KT):
                                n = fidx(c, fj_y1t(tt, k))
                                pe.wait_ge(V_POOL.h, n - 1)
                                PE_F.inc(pe.transpose(
                                    pool_ps[n % 2][:, 0:128],
                                    sb_y1[tau % 2][:, k * 128:(k + 1) * 128],
                                    sb_ident32[:]))
                            n = fidx(c, fj_rt(tt))
                            pe.wait_ge(V_POOL.h, n)
                            for k in range(KT):
                                inst = pe.matmul(
                                    pool_ps[n % 2][:, 0:8],
                                    sb_y1T[:, k * CH + tt * 128: k * CH + (tt + 1) * 128],
                                    sb_wr[:, k * 8:(k + 1) * 8],
                                    start=(k == 0), stop=(k == KT - 1))
                            PE_F.inc(inst)
                            n = fidx(c, fj_gt(tt))
                            pe.wait_ge(V_POOL.h, n - 1)
                            pe.wait_ge(V_GATE.h, tau + 1)
                            PE_F.inc(pe.transpose(
                                pool_ps[n % 2][0:9, 0:128],
                                sb_gaug[:, tt * 9:(tt + 1) * 9],
                                sb_ident32[:]))

                @blk.vector
                def _(v):
                    wait_setup(v)
                    for tt in range(TPC):
                        v.memset(sb_gaug[:, tt * 9: tt * 9 + 1], 1.0)
                    for c in range(NCH):
                        for tt in range(TPC):
                            for k in range(KT):
                                n = fidx(c, fj_xgt(tt, k))
                                v.wait_ge(PE_F.h, n + 1)
                                V_POOL.inc(v.tensor_copy(
                                    sb_xgT[:, k * CH + tt * 128: k * CH + (tt + 1) * 128],
                                    pool_ps[n % 2][:, 0:128]))
                        for ft in range(8):
                            n = fidx(c, fj_qt(ft))
                            v.wait_ge(PE_F.h, n + 1)
                            V_POOL.inc(v.tensor_scalar(
                                sb_qT[:, ft * CH:(ft + 1) * CH], pool_ps[n % 2][:, :],
                                QSCALE, None, ALU.mult))
                        for ft in range(8):
                            n = fidx(c, fj_kt(ft))
                            v.wait_ge(PE_F.h, n + 1)
                            V_POOL.inc(v.tensor_copy(
                                sb_kT[:, ft * CH:(ft + 1) * CH], pool_ps[n % 2][:, :]))
                        for seg in range(4):
                            for nh in range(2):
                                n = fidx(c, fj_vt(seg, nh))
                                v.wait_ge(PE_F.h, n + 1)
                                V_POOL.inc(v.tensor_copy(
                                    sb_vnat[0:64, seg * 1024 + nh * 512: seg * 1024 + (nh + 1) * 512],
                                    pool_ps[n % 2][0:64, :]))
                        for sl in range(4):
                            for h in range(H):
                                i = 64 * c + 16 * sl + h
                                v.wait_ge(PE_SC.h, i + 1)
                                if i >= 2:
                                    v.wait_ge(V_ABF.h, i - 1)
                                V_NM.inc(v.tensor_reduce(
                                    sb_ast[i % 2][:, 0:1], sc_ps[i % 2][:, 0:64],
                                    AX.X, ALU.max, negate=True))
                                v.wait_ge(S_EXP.h, i + 1)
                                nvs = VS.inc(v.reciprocal(sb_ast[i % 2][:, 2:3], sb_ast[i % 2][:, 1:2]))
                                v.wait_ge(VS.h, nvs)
                                V_ABF.inc(v.tensor_scalar(
                                    sb_abf[i % 2][:], sb_pfp[i % 2][:],
                                    sb_ast[i % 2][:, 2:3], None, ALU.mult))
                                v.wait_ge(PE_AT.h, i + 1)
                                V_ATS.inc(v.tensor_copy(sb_atT[i % 2][0:64, :], at_ps[i % 2][0:64, 0:128]))
                                v.wait_ge(PE_OT.h, i + 1)
                                pb = (h % 2) * 64
                                V_OTS.inc(v.tensor_copy(
                                    sb_oT[pb:pb + 64, (h // 2) * CH + sl * 128: (h // 2) * CH + (sl + 1) * 128],
                                    ot_ps[i % 2][0:64, 0:128]))
                        for tt in range(TPC):
                            tau = 4 * c + tt
                            lp = tau % 2
                            rt = sb_xr[lp]
                            v.wait_ge(SXR[tau % 2].h, 16 * (tau // 2 + 1))
                            if tau >= 2:
                                v.wait_ge(S_SQ.h, tau - 1)
                                v.wait_ge(S_RSTD.h, tau - 1)
                            for dh in range(2):
                                n = fidx(c, fj_wo(tt, dh))
                                v.wait_ge(PE_F.h, n + 1)
                                V_POOL.inc(v.tensor_tensor(
                                    rt[:, dh * 512:(dh + 1) * 512],
                                    pool_ps[n % 2][:, :],
                                    rt[:, dh * 512:(dh + 1) * 512], ALU.add))
                            v.wait_ge(V_POOL.h, V_POOL.n)
                            nvs = VS.inc(v.tensor_reduce(sb_lst[lp][:, 0:1], rt[:], AX.X, ALU.add))
                            v.wait_ge(VS.h, nvs)
                            nvs = VS.inc(v.tensor_scalar(
                                sb_lst[lp][:, 1:2], sb_lst[lp][:, 0:1], -1.0 / D, None, ALU.mult))
                            v.wait_ge(VS.h, nvs)
                            V_D.inc(v.tensor_scalar(
                                rt[:], rt[:], sb_lst[lp][:, 1:2], None, ALU.add))
                            v.wait_ge(S_RSTD.h, tau + 1)
                            if tau >= 2:
                                pe_prev = fidx((tau - 2) // 4, fj_y1t((tau - 2) % 4, 7)) + 1
                                v.wait_ge(PE_F.h, pe_prev)
                                v.wait_ge(SY1NW[tau % 2].h, 16 * (tau // 2))
                            V_Y1.inc(v.tensor_scalar(
                                sb_y1[tau % 2][:], rt[:], sb_lst[lp][:, 4:5], None, ALU.mult))
                            for k in range(KT):
                                n = fidx(c, fj_y1t(tt, k))
                                v.wait_ge(PE_F.h, n + 1)
                                V_POOL.inc(v.tensor_copy(
                                    sb_y1T[:, k * CH + tt * 128: k * CH + (tt + 1) * 128],
                                    pool_ps[n % 2][:, 0:128]))
                            n = fidx(c, fj_rt(tt))
                            v.wait_ge(PE_F.h, n + 1)
                            if tau >= 2:
                                v.wait_ge(S_EXPR.h, tau - 1)
                            V_POOL.inc(v.tensor_copy(sb_lg[lp][:], pool_ps[n % 2][:, 0:8]))
                            v.wait_ge(V_POOL.h, V_POOL.n)
                            V_NMR.inc(v.tensor_reduce(
                                sb_rst[lp][:, 0:1], sb_lg[lp][:], AX.X, ALU.max, negate=True))
                            v.wait_ge(S_EXPR.h, tau + 1)
                            et = sb_rexp[lp]
                            rtmp = sb_rtmp[lp]
                            nvs = VS.inc(v.tensor_reduce(sb_rst[lp][:, 1:2], et[:], AX.X, ALU.max))
                            v.wait_ge(VS.h, nvs)
                            nvs = VS.inc(v.tensor_scalar(
                                rtmp[:], et[:], sb_rst[lp][:, 1:2], None, ALU.is_lt))
                            v.wait_ge(VS.h, nvs)
                            nvs = VS.inc(v.tensor_tensor(rtmp[:], rtmp[:], et[:], ALU.mult))
                            v.wait_ge(VS.h, nvs)
                            nvs = VS.inc(v.tensor_reduce(sb_rst[lp][:, 2:3], rtmp[:], AX.X, ALU.max))
                            v.wait_ge(VS.h, nvs)
                            nvs = VS.inc(v.tensor_scalar(
                                rtmp[:], et[:], sb_rst[lp][:, 2:3], None, ALU.is_ge))
                            v.wait_ge(VS.h, nvs)
                            nvs = VS.inc(v.tensor_tensor(rtmp[:], rtmp[:], et[:], ALU.mult))
                            v.wait_ge(VS.h, nvs)
                            nvs = VS.inc(v.tensor_reduce(sb_rst[lp][:, 3:4], rtmp[:], AX.X, ALU.add))
                            v.wait_ge(VS.h, nvs)
                            nvs = VS.inc(v.reciprocal(sb_rst[lp][:, 4:5], sb_rst[lp][:, 3:4]))
                            v.wait_ge(VS.h, nvs)
                            nvs = VS.inc(v.tensor_scalar(
                                sb_gaten[:, tau * 8:(tau + 1) * 8], rtmp[:],
                                sb_rst[lp][:, 4:5], None, ALU.mult))
                            v.wait_ge(VS.h, nvs)
                            V_GATE.inc(v.tensor_copy(
                                sb_gaug[:, tt * 9 + 1:(tt + 1) * 9],
                                sb_gaten[:, tau * 8:(tau + 1) * 8]))
                            n = fidx(c, fj_gt(tt))
                            v.wait_ge(PE_F.h, n + 1)
                            if c + 1 < NCH:
                                pass
                            V_POOL.inc(v.tensor_copy(
                                sb_gateT[0:9, c * CH + tt * 128: c * CH + (tt + 1) * 128],
                                pool_ps[n % 2][0:9, 0:128]))
                        if c + 1 < NCH:
                            v.wait_ge(SY1TW.h, 16 * (c + 1))

                @blk.scalar
                def _(s):
                    wait_setup(s)
                    for c in range(NCH):
                        for sl in range(4):
                            for h in range(H):
                                i = 64 * c + 16 * sl + h
                                s.wait_ge(V_NM.h, i + 1)
                                if i >= 2:
                                    s.wait_ge(V_ABF.h, i - 1)
                                S_EXP.inc(s.activation(
                                    sb_pfp[i % 2][:], sc_ps[i % 2][:, 0:64], AF.Exp,
                                    bias=sb_ast[i % 2][:, 0:1], scale=1.0,
                                    accum_out=sb_ast[i % 2][:, 1:2]))
                        for tt in range(TPC):
                            tau = 4 * c + tt
                            lp = tau % 2
                            s.wait_ge(V_D.h, tau + 1)
                            S_SQ.inc(s.activation(
                                sb_junk[:], sb_xr[lp][:], AF.Square,
                                accum_out=sb_lst[lp][:, 2:3]))
                            s.wait_ge(S_SQ.h, S_SQ.n)
                            nss = SS.inc(s.activation(
                                sb_lst[lp][:, 3:4], sb_lst[lp][:, 2:3], AF.Ln,
                                bias=sb_cst[:, 0:1], scale=sb_cst[:, 2:3]))
                            s.wait_ge(SS.h, nss)
                            S_RSTD.inc(s.activation(
                                sb_lst[lp][:, 4:5], sb_lst[lp][:, 3:4], AF.Exp,
                                scale=sb_cst[:, 1:2]))
                            s.wait_ge(V_NMR.h, tau + 1)
                            if tau >= 2:
                                s.wait_ge(V_GATE.h, tau - 1)
                            S_EXPR.inc(s.activation(
                                sb_rexp[lp][:], sb_lg[lp][:], AF.Exp,
                                bias=sb_rst[lp][:, 0:1], scale=1.0))

    nc.all_engine_barrier()

    with contextlib.ExitStack() as stB:
        sb_y1g = [stB.enter_context(nc.sbuf_tensor(f"sb_y1g{i}", [128, KT * CH], BF16)) for i in range(2)]
        sb_h = stB.enter_context(nc.sbuf_tensor("sb_h", [128, MT * CH], BF16))
        sb_acc = stB.enter_context(nc.sbuf_tensor("sb_acc", [128, TPC * D], F32))
        ring1 = [stB.enter_context(nc.sbuf_tensor(f"ring1_{i}", [128, 4 * KT * 128], BF16)) for i in range(3)]
        ring2 = [stB.enter_context(nc.sbuf_tensor(f"ring2_{i}", [128, 4 * 1024], BF16)) for i in range(4)]
        sb_y1n = stB.enter_context(nc.sbuf_tensor("sb_y1n", [128, TPC * D], F32))
        sb_rf = [stB.enter_context(nc.sbuf_tensor(f"sb_rf{i}", [128, D], F32)) for i in range(2)]
        sb_outt = [stB.enter_context(nc.sbuf_tensor(f"sb_outt{i}", [128, D], F32)) for i in range(2)]
        sb_junk2 = stB.enter_context(nc.sbuf_tensor("sb_junk2", [128, D], F32))
        sb_lst2 = [stB.enter_context(nc.sbuf_tensor(f"sb_lst2_{i}", [128, 8], F32)) for i in range(2)]
        ff_ps = [stB.enter_context(nc.psum_tensor(f"ff_ps{i}", [128, 512], F32)) for i in range(8)]

        with nc.Block() as blk:
            @blk.gpsimd
            def _(gp):
                gp.wait_ge(SCAST.h, 16 * 4 * NE)
                for g in range(NCH):
                    for e in range(NE):
                        P = 9 * g + e
                        for q in range(8):
                            w2seq = 8 * P + q
                            if w2seq >= 4:
                                gp.wait_ge(W2FREE.h, w2seq - 3)
                            SW2[w2seq % 4].inc(gp.dma_start(ring2[w2seq % 4][:], w2_slice(e, q)), 16)

            @blk.sync
            def _(sy):
                def emit_w1_loads(g, e, mc0, mc1):
                    P = 9 * g + e
                    for mc in range(mc0, mc1):
                        w1seq = 8 * P + mc
                        if w1seq >= 3:
                            sy.wait_ge(PE_S1.h, 4 * (w1seq - 3) + 4)
                        SW1[w1seq % 3].inc(sy.dma_start(
                            ring1[w1seq % 3][:].rearrange("p (m kc) -> p m kc", m=4),
                            w1_chunk4(e, mc)), 16)

                for g in range(NCH):
                    if g >= 2:
                        sy.wait_ge(PE_S1.h, 32 * 9 * (g - 1))
                    if not test_ffn_only:
                        sy.wait_ge(SY1TW.h, 16 * (g + 1))
                    SY1G[g % 2].inc(sy.dma_start(sb_y1g[g % 2][:], y1Ts[g]), 16)
                    if g == 0:
                        sy.wait_ge(SCAST.h, 16 * 4 * NE)
                    emit_w1_loads(g, 0, 0, 3)
                    if g >= 1:
                        for tt in range(TPC):
                            taup = 4 * (g - 1) + tt
                            sy.wait_ge(V_OUTR.h, taup + 1)
                            SOUT[taup % 2].inc(sy.dma_start(
                                out[taup * 128:(taup + 1) * 128, :], sb_outt[taup % 2][:]), 16)
                        sy.wait_ge(V_FIN.h, 4 * g)
                    if not test_ffn_only:
                        sy.wait_ge(SY1NW[0].h, 16 * (2 * g + 2))
                        sy.wait_ge(SY1NW[1].h, 16 * (2 * g + 2))
                    SYN.inc(sy.dma_start(
                        sb_y1n[:].rearrange("p (tt d) -> p tt d", tt=TPC),
                        y1nat[g * CH:(g + 1) * CH, :].rearrange("(tt p) d -> p tt d", p=128)), 16)
                    emit_w1_loads(g, 0, 3, 8)
                    for e in range(1, NE):
                        emit_w1_loads(g, e, 0, 8)
                g = NCH
                for tt in range(TPC):
                    taup = 4 * (g - 1) + tt
                    sy.wait_ge(V_OUTR.h, taup + 1)
                    SOUT[taup % 2].inc(sy.dma_start(
                        out[taup * 128:(taup + 1) * 128, :], sb_outt[taup % 2][:]), 16)
                sy.wait_ge(SOUT[0].h, 16 * (NTAU // 2))
                sy.wait_ge(SOUT[1].h, 16 * (NTAU // 2))
                for s in [SIDX, SY1TW, SY1NW[0], SY1NW[1], SXR[0], SXR[1],
                          SG[0], SG[1], SG[2], SCAST]:
                    if s.n > 0:
                        sy.wait_ge(s.h, s.n)

            @blk.tensor
            def _(pe):
                wait_setup(pe)
                for g in range(NCH):
                    for e in range(NE):
                        P = 9 * g + e
                        for m in range(MT):
                            if m % 4 == 0:
                                w1seq = 8 * P + m // 4
                                pe.wait_ge(SW1[w1seq % 3].h, 16 * (w1seq // 3 + 1))
                            if m == 0:
                                pe.wait_ge(SY1G[g % 2].h, 16 * (g // 2 + 1))
                            if m >= 8:
                                pe.wait_ge(S_GELU.h, 32 * P + m - 7)
                            elif P >= 1:
                                pe.wait_ge(V_FFE.h, 8 * (P - 1) + m + 1)
                            bank = (40 * P + m) % 8
                            w1seq = 8 * P + m // 4
                            for k in range(KT):
                                inst = pe.matmul(
                                    ff_ps[bank][:, :],
                                    ring1[w1seq % 3][:, (m % 4) * KT * 128 + k * 128: (m % 4) * KT * 128 + (k + 1) * 128],
                                    sb_y1g[g % 2][:, k * CH:(k + 1) * CH],
                                    start=(k == 0), stop=(k == KT - 1))
                                if m == 0 and k == 0 and P >= 1:
                                    W2FREE.inc(inst)
                            PE_S1.inc(inst)
                        pe.wait_ge(S_GELU.h, 32 * (P + 1))
                        if e == 0:
                            for j in range(8):
                                tt, dh = j // 2, j % 2
                                pe.matmul(
                                    ff_ps[j][:, :],
                                    sb_gateT[0:9, g * CH + tt * 128: g * CH + (tt + 1) * 128],
                                    sb_bprojT[0:9, dh * 512:(dh + 1) * 512],
                                    start=True, stop=False)
                        for k in range(32):
                            q, kk = k // 4, k % 4
                            if kk == 0:
                                w2seq = 8 * P + q
                                pe.wait_ge(SW2[w2seq % 4].h, 16 * (w2seq // 4 + 1))
                            for j in range(8):
                                tt, dh = j // 2, j % 2
                                inst = pe.matmul(
                                    ff_ps[j][:, :],
                                    sb_h[:, k * CH + tt * 128: k * CH + (tt + 1) * 128],
                                    ring2[(8 * P + q) % 4][:, kk * 1024 + dh * 512: kk * 1024 + (dh + 1) * 512],
                                    start=(e != 0 and k == 0), stop=(k == 31))
                                if k == 31:
                                    PE_S2.inc(inst)
                                elif kk == 0 and j == 0 and k > 0:
                                    W2FREE.inc(inst)

            @blk.scalar
            def _(s):
                wait_setup(s)
                for g in range(NCH):
                    for e in range(NE):
                        P = 9 * g + e
                        for m in range(MT):
                            s.wait_ge(PE_S1.h, 32 * P + m + 1)
                            bank = (40 * P + m) % 8
                            S_GELU.inc(s.activation(
                                sb_h[:, m * CH:(m + 1) * CH], ff_ps[bank][:, :],
                                AF.Gelu_apprx_tanh,
                                bias=sb_bfcT[:, e * MT + m: e * MT + m + 1], scale=1.0))
                    for tt in range(TPC):
                        tau = 4 * g + tt
                        lp = tau % 2
                        s.wait_ge(V_D2.h, tau + 1)
                        S_SQ2.inc(s.activation(
                            sb_junk2[:], sb_rf[lp][:], AF.Square,
                            accum_out=sb_lst2[lp][:, 2:3]))
                        s.wait_ge(S_SQ2.h, S_SQ2.n)
                        nss = SS.inc(s.activation(
                            sb_lst2[lp][:, 3:4], sb_lst2[lp][:, 2:3], AF.Ln,
                            bias=sb_cst[:, 0:1], scale=sb_cst[:, 2:3]))
                        s.wait_ge(SS.h, nss)
                        S_RSTD2.inc(s.activation(
                            sb_lst2[lp][:, 4:5], sb_lst2[lp][:, 3:4], AF.Exp,
                            scale=sb_cst[:, 1:2]))

            @blk.vector
            def _(v):
                wait_setup(v)
                acc_last = {}
                for g in range(NCH):
                    for e in range(NE):
                        P = 9 * g + e
                        for j in range(8):
                            tt, dh = j // 2, j % 2
                            v.wait_ge(PE_S2.h, 8 * P + j + 1)
                            accsl = sb_acc[:, tt * D + dh * 512: tt * D + (dh + 1) * 512]
                            if e == 0:
                                V_FFE.inc(v.tensor_copy(accsl, ff_ps[j][:, :]))
                            else:
                                v.wait_ge(V_FFE.h, acc_last[j])
                                tau = 4 * g + tt
                                V_FFE.inc(v.scalar_tensor_tensor(
                                    accsl, ff_ps[j][:, :],
                                    sb_gaten[:, tau * 8 + (e - 1): tau * 8 + e],
                                    accsl, ALU.mult, ALU.add))
                            acc_last[j] = V_FFE.n
                    v.wait_ge(SYN.h, 16 * (g + 1))
                    for tt in range(TPC):
                        tau = 4 * g + tt
                        lp = tau % 2
                        rf = sb_rf[lp]
                        v.wait_ge(V_FFE.h, max(acc_last[2 * tt], acc_last[2 * tt + 1]))
                        if tau >= 2:
                            v.wait_ge(S_SQ2.h, tau - 1)
                            v.wait_ge(S_RSTD2.h, tau - 1)
                            v.wait_ge(SOUT[tau % 2].h, 16 * (tau // 2))
                        nvs = VS.inc(v.tensor_tensor(
                            rf[:, 0:512], sb_acc[:, tt * D: tt * D + 512],
                            sb_y1n[:, tt * D: tt * D + 512], ALU.add))
                        V_FIN.inc(v.tensor_tensor(
                            rf[:, 512:1024], sb_acc[:, tt * D + 512: (tt + 1) * D],
                            sb_y1n[:, tt * D + 512: (tt + 1) * D], ALU.add))
                        v.wait_ge(VS.h, nvs)
                        v.wait_ge(V_FIN.h, V_FIN.n)
                        nvs = VS.inc(v.tensor_reduce(sb_lst2[lp][:, 0:1], rf[:], AX.X, ALU.add))
                        v.wait_ge(VS.h, nvs)
                        nvs = VS.inc(v.tensor_scalar(
                            sb_lst2[lp][:, 1:2], sb_lst2[lp][:, 0:1], -1.0 / D, None, ALU.mult))
                        v.wait_ge(VS.h, nvs)
                        V_D2.inc(v.tensor_scalar(
                            rf[:], rf[:], sb_lst2[lp][:, 1:2], None, ALU.add))
                        v.wait_ge(S_RSTD2.h, tau + 1)
                        V_OUTR.inc(v.tensor_scalar(
                            sb_outt[tau % 2][:], rf[:], sb_lst2[lp][:, 4:5], None, ALU.mult))

    return nc


def kernel(L0_attn_x, L0_attn_w_qkv, L0_attn_w_out, L0_ffn_Wfc_s, L0_ffn_bfc_s,
           L0_ffn_Wproj_s, L0_ffn_bproj_s, L0_ffn_Wr, L0_ffn_Wfc, L0_ffn_bfc,
           L0_ffn_Wproj, L0_ffn_bproj):
    xin = np.ascontiguousarray(np.asarray(L0_attn_x, dtype=np.float32))
    w_qkv = np.asarray(L0_attn_w_qkv, dtype=np.float32)
    w_out = np.asarray(L0_attn_w_out, dtype=np.float32)
    Wfc_s = np.asarray(L0_ffn_Wfc_s, dtype=np.float32)
    bfc_s = np.asarray(L0_ffn_bfc_s, dtype=np.float32)
    Wproj_s = np.asarray(L0_ffn_Wproj_s, dtype=np.float32)
    bproj_s = np.asarray(L0_ffn_bproj_s, dtype=np.float32)
    Wr = np.asarray(L0_ffn_Wr, dtype=np.float32)
    Wfc = np.asarray(L0_ffn_Wfc, dtype=np.float32)
    bfc = np.asarray(L0_ffn_bfc, dtype=np.float32)
    Wproj = np.asarray(L0_ffn_Wproj, dtype=np.float32)
    bproj = np.asarray(L0_ffn_bproj, dtype=np.float32)

    perm = hilbert_perm(T)
    idxT = np.ascontiguousarray(perm.reshape(NTAU, 128).T)
    ident32 = np.eye(128, dtype=np.float32)
    identbf = np.eye(128, dtype=BF)
    wqkvT32 = np.ascontiguousarray(
        w_qkv.T.reshape(KT, 128, 3072).transpose(1, 0, 2).reshape(128, KT * 3072))
    woutT32 = np.ascontiguousarray(
        w_out.T.reshape(KT, 128, 1024).transpose(1, 0, 2).reshape(128, KT * 1024))
    wrT32 = np.ascontiguousarray(
        Wr.reshape(KT, 128, 8).transpose(1, 0, 2).reshape(128, KT * 8))
    bfc_all = np.concatenate([bfc_s[None], bfc], axis=0)
    bfcT32 = np.ascontiguousarray(
        bfc_all.reshape(NE, MT, 128).transpose(2, 0, 1).reshape(128, NE * MT))
    bprojT32 = np.ascontiguousarray(np.concatenate([bproj_s[None], bproj], axis=0))
    W1_all = np.concatenate([Wfc_s[None], Wfc], axis=0)
    w1t = np.ascontiguousarray(
        W1_all.reshape(NE, KT, 128, MT, 128).transpose(0, 3, 2, 1, 4)).reshape(NE, 2, D * HID // 2)
    W2_all = np.concatenate([Wproj_s[None], Wproj], axis=0)
    w2t = np.ascontiguousarray(
        W2_all.reshape(NE, 8, 4, 128, 1024).transpose(0, 1, 3, 2, 4)).reshape(NE, 2, D * HID // 2)

    nc = build_program()
    cst = np.zeros((128, 8), dtype=np.float32)
    cst[:, 0] = EPS
    cst[:, 1] = -0.5
    cst[:, 2] = 1.0 / D
    shared = {
        "idxT": idxT, "ident32": ident32, "identbf": identbf, "cst32": cst,
        "wqkvT32": wqkvT32, "woutT32": woutT32, "wrT32": wrT32,
        "bfcT32": bfcT32, "bprojT32": bprojT32, "w1t": w1t, "w2t": w2t,
    }
    in_maps = []
    for b in range(B):
        m = dict(shared)
        m["x"] = np.ascontiguousarray(xin[b])
        in_maps.append(m)
    res = run_bass_kernel_spmd(nc, in_maps, core_ids=list(range(B)))
    return np.stack([r["out"] for r in res.results], axis=0)


# version 04
# speedup vs baseline: 1.0491x; 1.0491x over previous
import math
import numpy as np
import ml_dtypes
import concourse.bass as bass
import concourse.mybir as mybir
from concourse.bass_utils import run_bass_kernel_spmd
from concourse.bass_core_rust import IndirectOffsetOnAxis

F32 = mybir.dt.float32
BF16 = mybir.dt.bfloat16
I32 = mybir.dt.int32
AF = mybir.ActivationFunctionType
ALU = mybir.AluOpType
AX = mybir.AxisListType
BF = ml_dtypes.bfloat16

B = 8
T = 8192
D = 1024
KT = 8
HID = 4096
MT = 32
NE = 9
H = 16
HD = 64
CH = 512
NCH = 16
TPC = 4
EPS = 1e-5
QSCALE = float(HD) ** -0.5
NTAU = NCH * TPC
POOL_PC = 104


def hilbert_perm(M: int) -> np.ndarray:
    g = int(math.ceil(math.sqrt(M)))
    grid = np.arange(g * g).reshape(g, g)
    grid[1::2] = grid[1::2, ::-1]
    flat = grid.reshape(-1)
    flat = flat[flat < M][:M]
    return flat.astype(np.int32)


def fj_xgt(tt, k): return 8 * tt + k
def fj_qt(ft): return 32 + ft
def fj_kt(ft): return 40 + ft
def fj_vt(seg, nh): return 48 + 2 * seg + nh
def fj_wo(tt, dh): return 56 + 12 * tt + dh
def fj_y1t(tt, k): return 56 + 12 * tt + 2 + k
def fj_rt(tt): return 56 + 12 * tt + 10
def fj_gt(tt): return 56 + 12 * tt + 11


class CSem:
    def __init__(self, nc, name):
        self.h = nc.alloc_semaphore(name)
        self.n = 0

    def inc(self, inst, k=1):
        inst.then_inc(self.h, k)
        self.n += k
        return self.n


def build_program(test_ffn_only=False):
    nc = bass.Bass("TRN2", target_bir_lowering=False)

    x = nc.dram_tensor("x", [T, D], F32, kind="ExternalInput").ap()
    idxT = nc.dram_tensor("idxT", [128, NTAU], I32, kind="ExternalInput").ap()
    ident32 = nc.dram_tensor("ident32", [128, 128], F32, kind="ExternalInput").ap()
    identbf = nc.dram_tensor("identbf", [128, 128], BF16, kind="ExternalInput").ap()
    wqkvT32 = nc.dram_tensor("wqkvT32", [128, KT * 3072], F32, kind="ExternalInput").ap()
    woutT32 = nc.dram_tensor("woutT32", [128, KT * 1024], F32, kind="ExternalInput").ap()
    wrT32 = nc.dram_tensor("wrT32", [128, KT * 8], F32, kind="ExternalInput").ap()
    bfcT32 = nc.dram_tensor("bfcT32", [128, NE * MT], F32, kind="ExternalInput").ap()
    cst32 = nc.dram_tensor("cst32", [128, 8], F32, kind="ExternalInput").ap()
    bprojT32 = nc.dram_tensor("bprojT32", [9, D], F32, kind="ExternalInput").ap()
    WHALF = D * HID // 2
    w1t = nc.dram_tensor("w1t", [NE, 2, WHALF], F32, kind="ExternalInput").ap()
    w2t = nc.dram_tensor("w2t", [NE, 2, WHALF], F32, kind="ExternalInput").ap()
    out = nc.dram_tensor("out", [T, D], F32, kind="ExternalOutput").ap()

    w1s = nc.dram_tensor("w1s", [NE, 2, WHALF], BF16, kind="Internal").ap()
    w2s = nc.dram_tensor("w2s", [NE, 2, WHALF], BF16, kind="Internal").ap()

    def w1_chunk4(e, mc):
        h, ml = divmod(mc, 4)
        blk = 4 * KT * 128 * 128
        return w1s[e, h][ml * blk:(ml + 1) * blk].rearrange(
            "(m p kc) -> p m kc", m=4, p=128)

    def w2_slice(e, q):
        h, ql = divmod(q, 4)
        blk = 4 * 1024 * 128
        return w2s[e, h][ql * blk:(ql + 1) * blk].rearrange("(p c) -> p c", p=128)
    y1kind = "ExternalInput" if test_ffn_only else "Internal"
    y1Ts = nc.dram_tensor("y1Ts", [NCH, 128, KT * CH], BF16, kind=y1kind).ap()
    y1nat = nc.dram_tensor("y1nat", [T, D], F32, kind=y1kind).ap()
    if test_ffn_only:
        gate_in = nc.dram_tensor("gate_in", [128, NTAU * 8], F32, kind="ExternalInput").ap()
        gateT_in = nc.dram_tensor("gateT_in", [9, T], BF16, kind="ExternalInput").ap()

    SSETS = CSem(nc, "SSETS")
    SSETG = CSem(nc, "SSETG")
    SIDX = CSem(nc, "SIDX")
    SG = [CSem(nc, f"SG{i}") for i in range(3)]
    SXR = [CSem(nc, f"SXR{i}") for i in range(2)]
    SCAST = CSem(nc, "SCAST")
    SY1TW = CSem(nc, "SY1TW")
    SY1NW = [CSem(nc, f"SY1NW{i}") for i in range(2)]
    SW1 = [CSem(nc, f"SW1_{i}") for i in range(3)]
    SW2 = [CSem(nc, f"SW2_{i}") for i in range(4)]
    SY1G = [CSem(nc, f"SY1G{i}") for i in range(2)]
    SYN = CSem(nc, "SYN")
    SOUT = [CSem(nc, f"SOUT{i}") for i in range(2)]

    PE_F = CSem(nc, "PE_F")
    PE_SC = CSem(nc, "PE_SC")
    PE_AT = CSem(nc, "PE_AT")
    PE_OT = CSem(nc, "PE_OT")
    V_POOL = CSem(nc, "V_POOL")
    V_NM = CSem(nc, "V_NM")
    S_EXP = CSem(nc, "S_EXP")
    V_ABF = CSem(nc, "V_ABF")
    V_ATS = CSem(nc, "V_ATS")
    V_OTS = CSem(nc, "V_OTS")
    V_D = CSem(nc, "V_D")
    S_SQ = CSem(nc, "S_SQ")
    S_RSTD = CSem(nc, "S_RSTD")
    V_Y1 = CSem(nc, "V_Y1")
    V_NMR = CSem(nc, "V_NMR")
    S_EXPR = CSem(nc, "S_EXPR")
    V_GATE = CSem(nc, "V_GATE")
    VS = CSem(nc, "VS")
    SS = CSem(nc, "SS")

    PE_S1 = CSem(nc, "PE_S1")
    S_GELU = CSem(nc, "S_GELU")
    PE_S2 = CSem(nc, "PE_S2")
    V_FFE = CSem(nc, "V_FFE")
    W2FREE = CSem(nc, "W2FREE")
    V_D2 = CSem(nc, "V_D2")
    S_SQ2 = CSem(nc, "S_SQ2")
    S_RSTD2 = CSem(nc, "S_RSTD2")
    V_OUTR = CSem(nc, "V_OUTR")
    V_FIN = CSem(nc, "V_FIN")

    SETS_TGT = 16 * (6 if test_ffn_only else 4)
    SETG_TGT = 16 * (2 if test_ffn_only else 4)

    def wait_setup(eng):
        eng.wait_ge(SSETS.h, SETS_TGT)
        eng.wait_ge(SSETG.h, SETG_TGT)

    sb_ident32 = nc.alloc_sbuf_tensor("sb_ident32", [128, 128], F32)
    sb_identbf = nc.alloc_sbuf_tensor("sb_identbf", [128, 128], BF16)
    sb_idxT = nc.alloc_sbuf_tensor("sb_idxT", [128, NTAU], I32)
    sb_bfcT = nc.alloc_sbuf_tensor("sb_bfcT", [128, NE * MT], F32)
    sb_cst = nc.alloc_sbuf_tensor("sb_cst", [128, 8], F32)
    sb_bprojT = nc.alloc_sbuf_tensor("sb_bprojT", [9, D], BF16)
    sb_wr = nc.alloc_sbuf_tensor("sb_wr", [128, KT * 8], BF16)
    sb_gaten = nc.alloc_sbuf_tensor("sb_gaten", [128, NTAU * 8], F32)
    sb_gateT = nc.alloc_sbuf_tensor("sb_gateT", [9, T], BF16)

    def _flat2d(ap):
        return ap.rearrange("(r c) -> r c", c=65536)

    cast_jobs = []
    for e in range(NE):
        cast_jobs.append((_flat2d(w1s[e, 0]), _flat2d(w1t[e, 0])))
        cast_jobs.append((_flat2d(w1s[e, 1]), _flat2d(w1t[e, 1])))
        cast_jobs.append((_flat2d(w2s[e, 0]), _flat2d(w2t[e, 0])))
        cast_jobs.append((_flat2d(w2s[e, 1]), _flat2d(w2t[e, 1])))

    def fidx(c, j):
        return POOL_PC * c + j

    import contextlib

    with contextlib.ExitStack() as stA:
        if not test_ffn_only:
            sb_wqkvT = stA.enter_context(nc.sbuf_tensor("sb_wqkvT", [128, KT * 3072], BF16))
            sb_woutT = stA.enter_context(nc.sbuf_tensor("sb_woutT", [128, KT * 1024], BF16))
            sb_xg = [stA.enter_context(nc.sbuf_tensor(f"sb_xg{i}", [128, D], F32)) for i in range(3)]
            sb_xgT = stA.enter_context(nc.sbuf_tensor("sb_xgT", [128, KT * CH], BF16))
            sb_qT = stA.enter_context(nc.sbuf_tensor("sb_qT", [128, KT * CH], BF16))
            sb_kT = stA.enter_context(nc.sbuf_tensor("sb_kT", [128, KT * CH], BF16))
            sb_vnat = stA.enter_context(nc.sbuf_tensor("sb_vnat", [64, 4096], BF16))
            sb_pfp = [stA.enter_context(nc.sbuf_tensor(f"sb_pfp{i}", [128, 64], F32)) for i in range(2)]
            sb_abf = [stA.enter_context(nc.sbuf_tensor(f"sb_abf{i}", [128, 64], BF16)) for i in range(2)]
            sb_atT = [stA.enter_context(nc.sbuf_tensor(f"sb_atT{i}", [64, 128], BF16)) for i in range(2)]
            sb_oT = stA.enter_context(nc.sbuf_tensor("sb_oT", [128, KT * CH], BF16))
            sb_xr = [stA.enter_context(nc.sbuf_tensor(f"sb_xr{i}", [128, D], F32)) for i in range(2)]
            sb_junk = stA.enter_context(nc.sbuf_tensor("sb_junk", [128, D], F32))
            sb_y1 = [stA.enter_context(nc.sbuf_tensor(f"sb_y1{i}", [128, D], F32)) for i in range(2)]
            sb_y1T = stA.enter_context(nc.sbuf_tensor("sb_y1T", [128, KT * CH], BF16))
            sb_ast = [stA.enter_context(nc.sbuf_tensor(f"sb_ast{i}", [128, 4], F32)) for i in range(2)]
            sb_lst = [stA.enter_context(nc.sbuf_tensor(f"sb_lst{i}", [128, 8], F32)) for i in range(2)]
            sb_lg = [stA.enter_context(nc.sbuf_tensor(f"sb_lg{i}", [128, 8], F32)) for i in range(2)]
            sb_rexp = [stA.enter_context(nc.sbuf_tensor(f"sb_rexp{i}", [128, 8], F32)) for i in range(2)]
            sb_rtmp = [stA.enter_context(nc.sbuf_tensor(f"sb_rtmp{i}", [128, 8], F32)) for i in range(2)]
            sb_rst = [stA.enter_context(nc.sbuf_tensor(f"sb_rst{i}", [128, 8], F32)) for i in range(2)]
            sb_gaug = stA.enter_context(nc.sbuf_tensor("sb_gaug", [128, 4 * 9], F32))
            pool_ps = [stA.enter_context(nc.psum_tensor(f"pool_ps{i}", [128, 512], F32)) for i in range(2)]
            sc_ps = [stA.enter_context(nc.psum_tensor(f"sc_ps{i}", [128, 512], F32)) for i in range(2)]
            at_ps = [stA.enter_context(nc.psum_tensor(f"at_ps{i}", [128, 1024], BF16)) for i in range(2)]
            ot_ps = [stA.enter_context(nc.psum_tensor(f"ot_ps{i}", [128, 512], F32)) for i in range(2)]

        with nc.Block() as blk:
            @blk.sync
            def _(sy):
                SSETS.inc(sy.dma_start(sb_ident32[:], ident32), 16)
                SSETS.inc(sy.dma_start(sb_identbf[:], identbf), 16)
                SIDX.inc(sy.dma_start(sb_idxT[:], idxT), 16)
                SSETS.inc(sy.dma_start(sb_bfcT[:], bfcT32), 16)
                SSETS.inc(sy.dma_start(sb_cst[:], cst32), 16)
                if test_ffn_only:
                    SSETS.inc(sy.dma_start(sb_gaten[:], gate_in), 16)
                    SSETS.inc(sy.dma_start(sb_gateT[:], gateT_in), 16)
                    return
                for tau in range(2):
                    SXR[tau % 2].inc(sy.dma_start(sb_xr[tau % 2][:], x[tau * 128:(tau + 1) * 128, :]), 16)
                for c in range(NCH):
                    for half in range(2):
                        for tt in range(2):
                            tau = 4 * c + 2 * half + tt + 2
                            if tau < NTAU:
                                sy.wait_ge(S_SQ.h, tau - 1)
                                sy.wait_ge(V_Y1.h, tau - 1)
                                SXR[tau % 2].inc(sy.dma_start(sb_xr[tau % 2][:], x[tau * 128:(tau + 1) * 128, :]), 16)
                        for tt in range(2):
                            tau = 4 * c + 2 * half + tt
                            sy.wait_ge(V_Y1.h, tau + 1)
                            SY1NW[tau % 2].inc(sy.dma_start(y1nat[tau * 128:(tau + 1) * 128, :], sb_y1[tau % 2][:]), 16)
                    sy.wait_ge(V_POOL.h, fidx(c, fj_y1t(3, 7)) + 1)
                    SY1TW.inc(sy.dma_start(y1Ts[c], sb_y1T[:]), 16)
                if True:
                    for s in [SXR[0], SXR[1], SY1NW[0], SY1NW[1], SY1TW]:
                        sy.wait_ge(s.h, s.n)

            @blk.gpsimd
            def _(gp):
                if not test_ffn_only:
                    SSETG.inc(gp.dma_start(sb_wqkvT[:], wqkvT32), 16)
                    SSETG.inc(gp.dma_start(sb_woutT[:], woutT32), 16)
                SSETG.inc(gp.dma_start(sb_wr[:], wrT32), 16)
                SSETG.inc(gp.dma_start(sb_bprojT[:], bprojT32), 16)
                if test_ffn_only:
                    for dst, src in cast_jobs:
                        SCAST.inc(gp.dma_start(dst, src), 16)
                    return
                gp.wait_ge(SIDX.h, 16)
                cj = 0
                for tau in range(NTAU):
                    if tau >= 3:
                        tp = tau - 3
                        gp.wait_ge(PE_F.h, fidx(tp // 4, 8 * (tp % 4 + 1)))
                    SG[tau % 3].inc(gp.indirect_dma_start(
                        sb_xg[tau % 3][:], None, x,
                        IndirectOffsetOnAxis(sb_idxT[:, tau:tau + 1], 0)), 16)
                    ncast = 2 if tau < 12 else 1
                    for _ in range(ncast):
                        if cj < len(cast_jobs):
                            dst, src = cast_jobs[cj]
                            SCAST.inc(gp.dma_start(dst, src), 16)
                            cj += 1
                while cj < len(cast_jobs):
                    dst, src = cast_jobs[cj]
                    SCAST.inc(gp.dma_start(dst, src), 16)
                    cj += 1
                for s in SG:
                    gp.wait_ge(s.h, s.n)



            if not test_ffn_only:
                @blk.tensor
                def _(pe):
                    wait_setup(pe)
                    for c in range(NCH):
                        for tt in range(TPC):
                            tau = 4 * c + tt
                            pe.wait_ge(SG[tau % 3].h, 16 * (tau // 3 + 1))
                            for k in range(KT):
                                n = fidx(c, fj_xgt(tt, k))
                                if n >= 2:
                                    pe.wait_ge(V_POOL.h, n - 1)
                                PE_F.inc(pe.transpose(
                                    pool_ps[n % 2][:, 0:128],
                                    sb_xg[tau % 3][:, k * 128:(k + 1) * 128],
                                    sb_ident32[:]))
                        pe.wait_ge(V_POOL.h, fidx(c, 32))
                        for ft in range(16):
                            n = fidx(c, 32 + ft)
                            pe.wait_ge(V_POOL.h, n - 1)
                            for k in range(KT):
                                inst = pe.matmul(
                                    pool_ps[n % 2][:, :],
                                    sb_wqkvT[:, k * 3072 + ft * 128: k * 3072 + (ft + 1) * 128],
                                    sb_xgT[:, k * CH:(k + 1) * CH],
                                    start=(k == 0), stop=(k == KT - 1))
                            PE_F.inc(inst)
                        for seg in range(4):
                            for nh in range(2):
                                n = fidx(c, fj_vt(seg, nh))
                                pe.wait_ge(V_POOL.h, n - 1)
                                for k in range(KT):
                                    inst = pe.matmul(
                                        pool_ps[n % 2][0:64, :],
                                        sb_xgT[:, k * CH + seg * 128: k * CH + (seg + 1) * 128: 2],
                                        sb_wqkvT[:, k * 3072 + 2048 + nh * 512: k * 3072 + 2048 + (nh + 1) * 512],
                                        start=(k == 0), stop=(k == KT - 1))
                                PE_F.inc(inst)
                        pe.wait_ge(V_POOL.h, fidx(c, 56))
                        for sl in range(4):
                            for h in range(H):
                                i = 64 * c + 16 * sl + h
                                pb = (h % 2) * 64
                                qb = (h // 2) * CH + sl * 128
                                if i >= 2:
                                    pe.wait_ge(S_EXP.h, i - 1)
                                PE_SC.inc(pe.matmul(
                                    sc_ps[i % 2][:, 0:64],
                                    sb_qT[pb:pb + 64, qb:qb + 128],
                                    sb_kT[pb:pb + 64, qb:qb + 128:2],
                                    start=True, stop=True))
                                pe.wait_ge(V_ABF.h, i + 1)
                                if i >= 2:
                                    pe.wait_ge(V_ATS.h, i - 1)
                                PE_AT.inc(pe.transpose(
                                    at_ps[i % 2][0:64, 0:128], sb_abf[i % 2][:], sb_identbf[:]))
                                pe.wait_ge(V_ATS.h, i + 1)
                                if i >= 2:
                                    pe.wait_ge(V_OTS.h, i - 1)
                                PE_OT.inc(pe.matmul(
                                    ot_ps[i % 2][0:64, 0:128],
                                    sb_vnat[0:64, sl * 1024 + h * 64: sl * 1024 + (h + 1) * 64],
                                    sb_atT[i % 2][0:64, :],
                                    start=True, stop=True))
                        pe.wait_ge(V_OTS.h, 64 * (c + 1))
                        for tt in range(TPC):
                            tau = 4 * c + tt
                            for dh in range(2):
                                n = fidx(c, fj_wo(tt, dh))
                                pe.wait_ge(V_POOL.h, n - 1)
                                for k in range(KT):
                                    inst = pe.matmul(
                                        pool_ps[n % 2][:, :],
                                        sb_oT[:, k * CH + tt * 128: k * CH + (tt + 1) * 128],
                                        sb_woutT[:, k * 1024 + dh * 512: k * 1024 + (dh + 1) * 512],
                                        start=(k == 0), stop=(k == KT - 1))
                                PE_F.inc(inst)
                            pe.wait_ge(V_Y1.h, tau + 1)
                            for k in range(KT):
                                n = fidx(c, fj_y1t(tt, k))
                                pe.wait_ge(V_POOL.h, n - 1)
                                PE_F.inc(pe.transpose(
                                    pool_ps[n % 2][:, 0:128],
                                    sb_y1[tau % 2][:, k * 128:(k + 1) * 128],
                                    sb_ident32[:]))
                            n = fidx(c, fj_rt(tt))
                            pe.wait_ge(V_POOL.h, n)
                            for k in range(KT):
                                inst = pe.matmul(
                                    pool_ps[n % 2][:, 0:8],
                                    sb_y1T[:, k * CH + tt * 128: k * CH + (tt + 1) * 128],
                                    sb_wr[:, k * 8:(k + 1) * 8],
                                    start=(k == 0), stop=(k == KT - 1))
                            PE_F.inc(inst)
                            n = fidx(c, fj_gt(tt))
                            pe.wait_ge(V_POOL.h, n - 1)
                            pe.wait_ge(V_GATE.h, tau + 1)
                            PE_F.inc(pe.transpose(
                                pool_ps[n % 2][0:9, 0:128],
                                sb_gaug[:, tt * 9:(tt + 1) * 9],
                                sb_ident32[:]))

                @blk.vector
                def _(v):
                    wait_setup(v)
                    for tt in range(TPC):
                        v.memset(sb_gaug[:, tt * 9: tt * 9 + 1], 1.0)
                    for c in range(NCH):
                        for tt in range(TPC):
                            for k in range(KT):
                                n = fidx(c, fj_xgt(tt, k))
                                v.wait_ge(PE_F.h, n + 1)
                                V_POOL.inc(v.tensor_copy(
                                    sb_xgT[:, k * CH + tt * 128: k * CH + (tt + 1) * 128],
                                    pool_ps[n % 2][:, 0:128]))
                        for ft in range(8):
                            n = fidx(c, fj_qt(ft))
                            v.wait_ge(PE_F.h, n + 1)
                            V_POOL.inc(v.tensor_scalar(
                                sb_qT[:, ft * CH:(ft + 1) * CH], pool_ps[n % 2][:, :],
                                QSCALE, None, ALU.mult))
                        for ft in range(8):
                            n = fidx(c, fj_kt(ft))
                            v.wait_ge(PE_F.h, n + 1)
                            V_POOL.inc(v.tensor_copy(
                                sb_kT[:, ft * CH:(ft + 1) * CH], pool_ps[n % 2][:, :]))
                        for seg in range(4):
                            for nh in range(2):
                                n = fidx(c, fj_vt(seg, nh))
                                v.wait_ge(PE_F.h, n + 1)
                                V_POOL.inc(v.tensor_copy(
                                    sb_vnat[0:64, seg * 1024 + nh * 512: seg * 1024 + (nh + 1) * 512],
                                    pool_ps[n % 2][0:64, :]))
                        for sl in range(4):
                            for h in range(H):
                                i = 64 * c + 16 * sl + h
                                v.wait_ge(PE_SC.h, i + 1)
                                if i >= 2:
                                    v.wait_ge(V_ABF.h, i - 1)
                                V_NM.inc(v.tensor_reduce(
                                    sb_ast[i % 2][:, 0:1], sc_ps[i % 2][:, 0:64],
                                    AX.X, ALU.max, negate=True))
                                v.wait_ge(S_EXP.h, i + 1)
                                nvs = VS.inc(v.reciprocal(sb_ast[i % 2][:, 2:3], sb_ast[i % 2][:, 1:2]))
                                v.wait_ge(VS.h, nvs)
                                V_ABF.inc(v.tensor_scalar(
                                    sb_abf[i % 2][:], sb_pfp[i % 2][:],
                                    sb_ast[i % 2][:, 2:3], None, ALU.mult))
                                v.wait_ge(PE_AT.h, i + 1)
                                V_ATS.inc(v.tensor_copy(sb_atT[i % 2][0:64, :], at_ps[i % 2][0:64, 0:128]))
                                v.wait_ge(PE_OT.h, i + 1)
                                pb = (h % 2) * 64
                                V_OTS.inc(v.tensor_copy(
                                    sb_oT[pb:pb + 64, (h // 2) * CH + sl * 128: (h // 2) * CH + (sl + 1) * 128],
                                    ot_ps[i % 2][0:64, 0:128]))
                        for tt in range(TPC):
                            tau = 4 * c + tt
                            lp = tau % 2
                            rt = sb_xr[lp]
                            v.wait_ge(SXR[tau % 2].h, 16 * (tau // 2 + 1))
                            if tau >= 2:
                                v.wait_ge(S_SQ.h, tau - 1)
                                v.wait_ge(S_RSTD.h, tau - 1)
                            for dh in range(2):
                                n = fidx(c, fj_wo(tt, dh))
                                v.wait_ge(PE_F.h, n + 1)
                                V_POOL.inc(v.tensor_tensor(
                                    rt[:, dh * 512:(dh + 1) * 512],
                                    pool_ps[n % 2][:, :],
                                    rt[:, dh * 512:(dh + 1) * 512], ALU.add))
                            v.wait_ge(V_POOL.h, V_POOL.n)
                            nvs = VS.inc(v.tensor_reduce(sb_lst[lp][:, 0:1], rt[:], AX.X, ALU.add))
                            v.wait_ge(VS.h, nvs)
                            nvs = VS.inc(v.tensor_scalar(
                                sb_lst[lp][:, 1:2], sb_lst[lp][:, 0:1], -1.0 / D, None, ALU.mult))
                            v.wait_ge(VS.h, nvs)
                            V_D.inc(v.tensor_scalar(
                                rt[:], rt[:], sb_lst[lp][:, 1:2], None, ALU.add))
                            v.wait_ge(S_RSTD.h, tau + 1)
                            if tau >= 2:
                                pe_prev = fidx((tau - 2) // 4, fj_y1t((tau - 2) % 4, 7)) + 1
                                v.wait_ge(PE_F.h, pe_prev)
                                v.wait_ge(SY1NW[tau % 2].h, 16 * (tau // 2))
                            V_Y1.inc(v.tensor_scalar(
                                sb_y1[tau % 2][:], rt[:], sb_lst[lp][:, 4:5], None, ALU.mult))
                            for k in range(KT):
                                n = fidx(c, fj_y1t(tt, k))
                                v.wait_ge(PE_F.h, n + 1)
                                V_POOL.inc(v.tensor_copy(
                                    sb_y1T[:, k * CH + tt * 128: k * CH + (tt + 1) * 128],
                                    pool_ps[n % 2][:, 0:128]))
                            n = fidx(c, fj_rt(tt))
                            v.wait_ge(PE_F.h, n + 1)
                            if tau >= 2:
                                v.wait_ge(S_EXPR.h, tau - 1)
                            V_POOL.inc(v.tensor_copy(sb_lg[lp][:], pool_ps[n % 2][:, 0:8]))
                            v.wait_ge(V_POOL.h, V_POOL.n)
                            V_NMR.inc(v.tensor_reduce(
                                sb_rst[lp][:, 0:1], sb_lg[lp][:], AX.X, ALU.max, negate=True))
                            v.wait_ge(S_EXPR.h, tau + 1)
                            et = sb_rexp[lp]
                            rtmp = sb_rtmp[lp]
                            nvs = VS.inc(v.tensor_reduce(sb_rst[lp][:, 1:2], et[:], AX.X, ALU.max))
                            v.wait_ge(VS.h, nvs)
                            nvs = VS.inc(v.tensor_scalar(
                                rtmp[:], et[:], sb_rst[lp][:, 1:2], None, ALU.is_lt))
                            v.wait_ge(VS.h, nvs)
                            nvs = VS.inc(v.tensor_tensor(rtmp[:], rtmp[:], et[:], ALU.mult))
                            v.wait_ge(VS.h, nvs)
                            nvs = VS.inc(v.tensor_reduce(sb_rst[lp][:, 2:3], rtmp[:], AX.X, ALU.max))
                            v.wait_ge(VS.h, nvs)
                            nvs = VS.inc(v.tensor_scalar(
                                rtmp[:], et[:], sb_rst[lp][:, 2:3], None, ALU.is_ge))
                            v.wait_ge(VS.h, nvs)
                            nvs = VS.inc(v.tensor_tensor(rtmp[:], rtmp[:], et[:], ALU.mult))
                            v.wait_ge(VS.h, nvs)
                            nvs = VS.inc(v.tensor_reduce(sb_rst[lp][:, 3:4], rtmp[:], AX.X, ALU.add))
                            v.wait_ge(VS.h, nvs)
                            nvs = VS.inc(v.reciprocal(sb_rst[lp][:, 4:5], sb_rst[lp][:, 3:4]))
                            v.wait_ge(VS.h, nvs)
                            nvs = VS.inc(v.tensor_scalar(
                                sb_gaten[:, tau * 8:(tau + 1) * 8], rtmp[:],
                                sb_rst[lp][:, 4:5], None, ALU.mult))
                            v.wait_ge(VS.h, nvs)
                            V_GATE.inc(v.tensor_copy(
                                sb_gaug[:, tt * 9 + 1:(tt + 1) * 9],
                                sb_gaten[:, tau * 8:(tau + 1) * 8]))
                            n = fidx(c, fj_gt(tt))
                            v.wait_ge(PE_F.h, n + 1)
                            if c + 1 < NCH:
                                pass
                            V_POOL.inc(v.tensor_copy(
                                sb_gateT[0:9, c * CH + tt * 128: c * CH + (tt + 1) * 128],
                                pool_ps[n % 2][0:9, 0:128]))
                        if c + 1 < NCH:
                            v.wait_ge(SY1TW.h, 16 * (c + 1))

                @blk.scalar
                def _(s):
                    wait_setup(s)
                    for c in range(NCH):
                        for sl in range(4):
                            for h in range(H):
                                i = 64 * c + 16 * sl + h
                                s.wait_ge(V_NM.h, i + 1)
                                if i >= 2:
                                    s.wait_ge(V_ABF.h, i - 1)
                                S_EXP.inc(s.activation(
                                    sb_pfp[i % 2][:], sc_ps[i % 2][:, 0:64], AF.Exp,
                                    bias=sb_ast[i % 2][:, 0:1], scale=1.0,
                                    accum_out=sb_ast[i % 2][:, 1:2]))
                        for tt in range(TPC):
                            tau = 4 * c + tt
                            lp = tau % 2
                            s.wait_ge(V_D.h, tau + 1)
                            S_SQ.inc(s.activation(
                                sb_junk[:], sb_xr[lp][:], AF.Square,
                                accum_out=sb_lst[lp][:, 2:3]))
                            s.wait_ge(S_SQ.h, S_SQ.n)
                            nss = SS.inc(s.activation(
                                sb_lst[lp][:, 3:4], sb_lst[lp][:, 2:3], AF.Ln,
                                bias=sb_cst[:, 0:1], scale=sb_cst[:, 2:3]))
                            s.wait_ge(SS.h, nss)
                            S_RSTD.inc(s.activation(
                                sb_lst[lp][:, 4:5], sb_lst[lp][:, 3:4], AF.Exp,
                                scale=sb_cst[:, 1:2]))
                            s.wait_ge(V_NMR.h, tau + 1)
                            if tau >= 2:
                                s.wait_ge(V_GATE.h, tau - 1)
                            S_EXPR.inc(s.activation(
                                sb_rexp[lp][:], sb_lg[lp][:], AF.Exp,
                                bias=sb_rst[lp][:, 0:1], scale=1.0))

    nc.all_engine_barrier()

    with contextlib.ExitStack() as stB:
        sb_y1g = [stB.enter_context(nc.sbuf_tensor(f"sb_y1g{i}", [128, KT * CH], BF16)) for i in range(2)]
        sb_h = stB.enter_context(nc.sbuf_tensor("sb_h", [128, MT * CH], BF16))
        sb_acc = stB.enter_context(nc.sbuf_tensor("sb_acc", [128, TPC * D], F32))
        ring1 = [stB.enter_context(nc.sbuf_tensor(f"ring1_{i}", [128, 4 * KT * 128], BF16)) for i in range(3)]
        ring2 = [stB.enter_context(nc.sbuf_tensor(f"ring2_{i}", [128, 4 * 1024], BF16)) for i in range(4)]
        sb_y1n = stB.enter_context(nc.sbuf_tensor("sb_y1n", [128, TPC * D], F32))
        sb_rf = [stB.enter_context(nc.sbuf_tensor(f"sb_rf{i}", [128, D], F32)) for i in range(2)]
        sb_outt = [stB.enter_context(nc.sbuf_tensor(f"sb_outt{i}", [128, D], F32)) for i in range(2)]
        sb_junk2 = stB.enter_context(nc.sbuf_tensor("sb_junk2", [128, D], F32))
        sb_lst2 = [stB.enter_context(nc.sbuf_tensor(f"sb_lst2_{i}", [128, 8], F32)) for i in range(2)]
        ff_ps = [stB.enter_context(nc.psum_tensor(f"ff_ps{i}", [128, 512], F32)) for i in range(8)]

        with nc.Block() as blk:
            @blk.gpsimd
            def _(gp):
                gp.wait_ge(SCAST.h, 16 * 4 * NE)
                for g in range(NCH):
                    for e in range(NE):
                        P = 9 * g + e
                        for q in range(8):
                            w2seq = 8 * P + q
                            if w2seq >= 4:
                                gp.wait_ge(W2FREE.h, w2seq - 3)
                            SW2[w2seq % 4].inc(gp.dma_start(ring2[w2seq % 4][:], w2_slice(e, q)), 16)

            @blk.sync
            def _(sy):
                def emit_w1_loads(g, e, mc0, mc1):
                    P = 9 * g + e
                    for mc in range(mc0, mc1):
                        w1seq = 8 * P + mc
                        if w1seq >= 3:
                            sy.wait_ge(PE_S1.h, 4 * (w1seq - 3) + 4)
                        SW1[w1seq % 3].inc(sy.dma_start(
                            ring1[w1seq % 3][:].rearrange("p (m kc) -> p m kc", m=4),
                            w1_chunk4(e, mc)), 16)

                for g in range(NCH):
                    if g >= 2:
                        sy.wait_ge(PE_S1.h, 32 * 9 * (g - 1))
                    if not test_ffn_only:
                        sy.wait_ge(SY1TW.h, 16 * (g + 1))
                    SY1G[g % 2].inc(sy.dma_start(sb_y1g[g % 2][:], y1Ts[g]), 16)
                    if g == 0:
                        sy.wait_ge(SCAST.h, 16 * 4 * NE)
                    emit_w1_loads(g, 0, 0, 8)
                    if g >= 1:
                        for tt in range(TPC):
                            taup = 4 * (g - 1) + tt
                            sy.wait_ge(V_OUTR.h, taup + 1)
                            SOUT[taup % 2].inc(sy.dma_start(
                                out[taup * 128:(taup + 1) * 128, :], sb_outt[taup % 2][:]), 16)
                        sy.wait_ge(V_FIN.h, 4 * g)
                    if not test_ffn_only:
                        sy.wait_ge(SY1NW[0].h, 16 * (2 * g + 2))
                        sy.wait_ge(SY1NW[1].h, 16 * (2 * g + 2))
                    SYN.inc(sy.dma_start(
                        sb_y1n[:].rearrange("p (tt d) -> p tt d", tt=TPC),
                        y1nat[g * CH:(g + 1) * CH, :].rearrange("(tt p) d -> p tt d", p=128)), 16)
                    for e in range(1, NE):
                        emit_w1_loads(g, e, 0, 8)
                g = NCH
                for tt in range(TPC):
                    taup = 4 * (g - 1) + tt
                    sy.wait_ge(V_OUTR.h, taup + 1)
                    SOUT[taup % 2].inc(sy.dma_start(
                        out[taup * 128:(taup + 1) * 128, :], sb_outt[taup % 2][:]), 16)
                sy.wait_ge(SOUT[0].h, 16 * (NTAU // 2))
                sy.wait_ge(SOUT[1].h, 16 * (NTAU // 2))
                for s in [SIDX, SY1TW, SY1NW[0], SY1NW[1], SXR[0], SXR[1],
                          SG[0], SG[1], SG[2], SCAST]:
                    if s.n > 0:
                        sy.wait_ge(s.h, s.n)

            @blk.tensor
            def _(pe):
                wait_setup(pe)
                for g in range(NCH):
                    for e in range(NE):
                        P = 9 * g + e
                        for m in range(MT):
                            if m % 4 == 0:
                                w1seq = 8 * P + m // 4
                                pe.wait_ge(SW1[w1seq % 3].h, 16 * (w1seq // 3 + 1))
                            if m == 0:
                                pe.wait_ge(SY1G[g % 2].h, 16 * (g // 2 + 1))
                            if m >= 8:
                                pe.wait_ge(S_GELU.h, 32 * P + m - 7)
                            elif P >= 1:
                                pe.wait_ge(V_FFE.h, 8 * (P - 1) + m + 1)
                            bank = (40 * P + m) % 8
                            w1seq = 8 * P + m // 4
                            for k in range(KT):
                                inst = pe.matmul(
                                    ff_ps[bank][:, :],
                                    ring1[w1seq % 3][:, (m % 4) * KT * 128 + k * 128: (m % 4) * KT * 128 + (k + 1) * 128],
                                    sb_y1g[g % 2][:, k * CH:(k + 1) * CH],
                                    start=(k == 0), stop=(k == KT - 1))
                                if m == 0 and k == 0 and P >= 1:
                                    W2FREE.inc(inst)
                            PE_S1.inc(inst)
                        pe.wait_ge(S_GELU.h, 32 * (P + 1))
                        if e == 0:
                            for j in range(8):
                                tt, dh = j // 2, j % 2
                                pe.matmul(
                                    ff_ps[j][:, :],
                                    sb_gateT[0:9, g * CH + tt * 128: g * CH + (tt + 1) * 128],
                                    sb_bprojT[0:9, dh * 512:(dh + 1) * 512],
                                    start=True, stop=False)
                        for k in range(32):
                            q, kk = k // 4, k % 4
                            if kk == 0:
                                w2seq = 8 * P + q
                                pe.wait_ge(SW2[w2seq % 4].h, 16 * (w2seq // 4 + 1))
                            for j in range(8):
                                tt, dh = j // 2, j % 2
                                inst = pe.matmul(
                                    ff_ps[j][:, :],
                                    sb_h[:, k * CH + tt * 128: k * CH + (tt + 1) * 128],
                                    ring2[(8 * P + q) % 4][:, kk * 1024 + dh * 512: kk * 1024 + (dh + 1) * 512],
                                    start=(e != 0 and k == 0), stop=(k == 31))
                                if k == 31:
                                    PE_S2.inc(inst)
                                elif kk == 0 and j == 0 and k > 0:
                                    W2FREE.inc(inst)

            @blk.scalar
            def _(s):
                wait_setup(s)

                def emit_ln2(gp_):
                    for tt in range(TPC):
                        tau = 4 * gp_ + tt
                        lp = tau % 2
                        s.wait_ge(V_D2.h, tau + 1)
                        S_SQ2.inc(s.activation(
                            sb_junk2[:], sb_rf[lp][:], AF.Square,
                            accum_out=sb_lst2[lp][:, 2:3]))
                        s.wait_ge(S_SQ2.h, S_SQ2.n)
                        nss = SS.inc(s.activation(
                            sb_lst2[lp][:, 3:4], sb_lst2[lp][:, 2:3], AF.Ln,
                            bias=sb_cst[:, 0:1], scale=sb_cst[:, 2:3]))
                        s.wait_ge(SS.h, nss)
                        S_RSTD2.inc(s.activation(
                            sb_lst2[lp][:, 4:5], sb_lst2[lp][:, 3:4], AF.Exp,
                            scale=sb_cst[:, 1:2]))

                for g in range(NCH):
                    for e in range(NE):
                        P = 9 * g + e
                        for m in range(MT):
                            s.wait_ge(PE_S1.h, 32 * P + m + 1)
                            bank = (40 * P + m) % 8
                            S_GELU.inc(s.activation(
                                sb_h[:, m * CH:(m + 1) * CH], ff_ps[bank][:, :],
                                AF.Gelu_apprx_tanh,
                                bias=sb_bfcT[:, e * MT + m: e * MT + m + 1], scale=1.0))
                        if e == 0 and g >= 1:
                            emit_ln2(g - 1)
                emit_ln2(NCH - 1)

            @blk.vector
            def _(v):
                wait_setup(v)
                acc_last = {}
                for g in range(NCH):
                    for e in range(NE):
                        P = 9 * g + e
                        for j in range(8):
                            tt, dh = j // 2, j % 2
                            v.wait_ge(PE_S2.h, 8 * P + j + 1)
                            accsl = sb_acc[:, tt * D + dh * 512: tt * D + (dh + 1) * 512]
                            if e == 0:
                                V_FFE.inc(v.tensor_copy(accsl, ff_ps[j][:, :]))
                            else:
                                v.wait_ge(V_FFE.h, acc_last[j])
                                tau = 4 * g + tt
                                V_FFE.inc(v.scalar_tensor_tensor(
                                    accsl, ff_ps[j][:, :],
                                    sb_gaten[:, tau * 8 + (e - 1): tau * 8 + e],
                                    accsl, ALU.mult, ALU.add))
                            acc_last[j] = V_FFE.n
                    v.wait_ge(SYN.h, 16 * (g + 1))
                    for tt in range(TPC):
                        tau = 4 * g + tt
                        lp = tau % 2
                        rf = sb_rf[lp]
                        v.wait_ge(V_FFE.h, max(acc_last[2 * tt], acc_last[2 * tt + 1]))
                        if tau >= 2:
                            v.wait_ge(S_SQ2.h, tau - 1)
                            v.wait_ge(S_RSTD2.h, tau - 1)
                            v.wait_ge(SOUT[tau % 2].h, 16 * (tau // 2))
                        nvs = VS.inc(v.tensor_tensor(
                            rf[:, 0:512], sb_acc[:, tt * D: tt * D + 512],
                            sb_y1n[:, tt * D: tt * D + 512], ALU.add))
                        V_FIN.inc(v.tensor_tensor(
                            rf[:, 512:1024], sb_acc[:, tt * D + 512: (tt + 1) * D],
                            sb_y1n[:, tt * D + 512: (tt + 1) * D], ALU.add))
                        v.wait_ge(VS.h, nvs)
                        v.wait_ge(V_FIN.h, V_FIN.n)
                        nvs = VS.inc(v.tensor_reduce(sb_lst2[lp][:, 0:1], rf[:], AX.X, ALU.add))
                        v.wait_ge(VS.h, nvs)
                        nvs = VS.inc(v.tensor_scalar(
                            sb_lst2[lp][:, 1:2], sb_lst2[lp][:, 0:1], -1.0 / D, None, ALU.mult))
                        v.wait_ge(VS.h, nvs)
                        V_D2.inc(v.tensor_scalar(
                            rf[:], rf[:], sb_lst2[lp][:, 1:2], None, ALU.add))
                        v.wait_ge(S_RSTD2.h, tau + 1)
                        V_OUTR.inc(v.tensor_scalar(
                            sb_outt[tau % 2][:], rf[:], sb_lst2[lp][:, 4:5], None, ALU.mult))

    return nc


def kernel(L0_attn_x, L0_attn_w_qkv, L0_attn_w_out, L0_ffn_Wfc_s, L0_ffn_bfc_s,
           L0_ffn_Wproj_s, L0_ffn_bproj_s, L0_ffn_Wr, L0_ffn_Wfc, L0_ffn_bfc,
           L0_ffn_Wproj, L0_ffn_bproj):
    xin = np.ascontiguousarray(np.asarray(L0_attn_x, dtype=np.float32))
    w_qkv = np.asarray(L0_attn_w_qkv, dtype=np.float32)
    w_out = np.asarray(L0_attn_w_out, dtype=np.float32)
    Wfc_s = np.asarray(L0_ffn_Wfc_s, dtype=np.float32)
    bfc_s = np.asarray(L0_ffn_bfc_s, dtype=np.float32)
    Wproj_s = np.asarray(L0_ffn_Wproj_s, dtype=np.float32)
    bproj_s = np.asarray(L0_ffn_bproj_s, dtype=np.float32)
    Wr = np.asarray(L0_ffn_Wr, dtype=np.float32)
    Wfc = np.asarray(L0_ffn_Wfc, dtype=np.float32)
    bfc = np.asarray(L0_ffn_bfc, dtype=np.float32)
    Wproj = np.asarray(L0_ffn_Wproj, dtype=np.float32)
    bproj = np.asarray(L0_ffn_bproj, dtype=np.float32)

    perm = hilbert_perm(T)
    idxT = np.ascontiguousarray(perm.reshape(NTAU, 128).T)
    ident32 = np.eye(128, dtype=np.float32)
    identbf = np.eye(128, dtype=BF)
    wqkvT32 = np.ascontiguousarray(
        w_qkv.T.reshape(KT, 128, 3072).transpose(1, 0, 2).reshape(128, KT * 3072))
    woutT32 = np.ascontiguousarray(
        w_out.T.reshape(KT, 128, 1024).transpose(1, 0, 2).reshape(128, KT * 1024))
    wrT32 = np.ascontiguousarray(
        Wr.reshape(KT, 128, 8).transpose(1, 0, 2).reshape(128, KT * 8))
    bfc_all = np.concatenate([bfc_s[None], bfc], axis=0)
    bfcT32 = np.ascontiguousarray(
        bfc_all.reshape(NE, MT, 128).transpose(2, 0, 1).reshape(128, NE * MT))
    bprojT32 = np.ascontiguousarray(np.concatenate([bproj_s[None], bproj], axis=0))
    W1_all = np.concatenate([Wfc_s[None], Wfc], axis=0)
    w1t = np.ascontiguousarray(
        W1_all.reshape(NE, KT, 128, MT, 128).transpose(0, 3, 2, 1, 4)).reshape(NE, 2, D * HID // 2)
    W2_all = np.concatenate([Wproj_s[None], Wproj], axis=0)
    w2t = np.ascontiguousarray(
        W2_all.reshape(NE, 8, 4, 128, 1024).transpose(0, 1, 3, 2, 4)).reshape(NE, 2, D * HID // 2)

    nc = build_program()
    cst = np.zeros((128, 8), dtype=np.float32)
    cst[:, 0] = EPS
    cst[:, 1] = -0.5
    cst[:, 2] = 1.0 / D
    shared = {
        "idxT": idxT, "ident32": ident32, "identbf": identbf, "cst32": cst,
        "wqkvT32": wqkvT32, "woutT32": woutT32, "wrT32": wrT32,
        "bfcT32": bfcT32, "bprojT32": bprojT32, "w1t": w1t, "w2t": w2t,
    }
    in_maps = []
    for b in range(B):
        m = dict(shared)
        m["x"] = np.ascontiguousarray(xin[b])
        in_maps.append(m)
    res = run_bass_kernel_spmd(nc, in_maps, core_ids=list(range(B)))
    return np.stack([r["out"] for r in res.results], axis=0)


# version 06
# speedup vs baseline: 1.0973x; 1.0459x over previous
import math
import numpy as np
import ml_dtypes
import concourse.bass as bass
import concourse.mybir as mybir
from concourse.bass_utils import run_bass_kernel_spmd
from concourse.bass_core_rust import IndirectOffsetOnAxis

F32 = mybir.dt.float32
BF16 = mybir.dt.bfloat16
I32 = mybir.dt.int32
AF = mybir.ActivationFunctionType
ALU = mybir.AluOpType
AX = mybir.AxisListType
BF = ml_dtypes.bfloat16

B = 8
T = 8192
D = 1024
KT = 8
HID = 4096
MT = 32
NE = 9
H = 16
HD = 64
CH = 512
NCH = 16
TPC = 4
EPS = 1e-5
QSCALE = float(HD) ** -0.5
NTAU = NCH * TPC
POOL_PC = 104


def hilbert_perm(M: int) -> np.ndarray:
    g = int(math.ceil(math.sqrt(M)))
    grid = np.arange(g * g).reshape(g, g)
    grid[1::2] = grid[1::2, ::-1]
    flat = grid.reshape(-1)
    flat = flat[flat < M][:M]
    return flat.astype(np.int32)


def fj_xgt(tt, k): return 8 * tt + k
def fj_qt(ft): return 32 + ft
def fj_kt(ft): return 40 + ft
def fj_vt(seg, nh): return 48 + 2 * seg + nh
def fj_wo(tt, dh): return 56 + 12 * tt + dh
def fj_y1t(tt, k): return 56 + 12 * tt + 2 + k
def fj_rt(tt): return 56 + 12 * tt + 10
def fj_gt(tt): return 56 + 12 * tt + 11


class CSem:
    def __init__(self, nc, name):
        self.h = nc.alloc_semaphore(name)
        self.n = 0

    def inc(self, inst, k=1):
        inst.then_inc(self.h, k)
        self.n += k
        return self.n


def build_program(test_ffn_only=False):
    nc = bass.Bass("TRN2", target_bir_lowering=False)

    x = nc.dram_tensor("x", [T, D], F32, kind="ExternalInput").ap()
    idxT = nc.dram_tensor("idxT", [128, NTAU], I32, kind="ExternalInput").ap()
    ident32 = nc.dram_tensor("ident32", [128, 128], F32, kind="ExternalInput").ap()
    identbf = nc.dram_tensor("identbf", [128, 128], BF16, kind="ExternalInput").ap()
    wqkvT32 = nc.dram_tensor("wqkvT32", [128, KT * 3072], F32, kind="ExternalInput").ap()
    woutT32 = nc.dram_tensor("woutT32", [128, KT * 1024], F32, kind="ExternalInput").ap()
    wrT32 = nc.dram_tensor("wrT32", [128, KT * 8], F32, kind="ExternalInput").ap()
    bfcT32 = nc.dram_tensor("bfcT32", [128, NE * MT], F32, kind="ExternalInput").ap()
    cst32 = nc.dram_tensor("cst32", [128, 8], F32, kind="ExternalInput").ap()
    bprojT32 = nc.dram_tensor("bprojT32", [9, D], F32, kind="ExternalInput").ap()
    WHALF = D * HID // 2
    w1t = nc.dram_tensor("w1t", [NE, 2, WHALF], F32, kind="ExternalInput").ap()
    w2t = nc.dram_tensor("w2t", [NE, 2, WHALF], F32, kind="ExternalInput").ap()
    out = nc.dram_tensor("out", [T, D], F32, kind="ExternalOutput").ap()

    w1s = nc.dram_tensor("w1s", [NE, 2, WHALF], BF16, kind="Internal").ap()
    w2s = nc.dram_tensor("w2s", [NE, 2, WHALF], BF16, kind="Internal").ap()

    def w1_chunk4(e, mc):
        h, ml = divmod(mc, 4)
        blk = 4 * KT * 128 * 128
        return w1s[e, h][ml * blk:(ml + 1) * blk].rearrange(
            "(m p kc) -> p m kc", m=4, p=128)

    def w2_slice(e, q):
        h, ql = divmod(q, 4)
        blk = 4 * 1024 * 128
        return w2s[e, h][ql * blk:(ql + 1) * blk].rearrange("(p c) -> p c", p=128)
    y1kind = "ExternalInput" if test_ffn_only else "Internal"
    y1Ts = nc.dram_tensor("y1Ts", [NCH, 128, KT * CH], BF16, kind=y1kind).ap()
    y1nat = nc.dram_tensor("y1nat", [T, D], F32, kind=y1kind).ap()
    if test_ffn_only:
        gate_in = nc.dram_tensor("gate_in", [128, NTAU * 8], F32, kind="ExternalInput").ap()
        gateT_in = nc.dram_tensor("gateT_in", [9, T], BF16, kind="ExternalInput").ap()

    SSETS = CSem(nc, "SSETS")
    SSETG = CSem(nc, "SSETG")
    SIDX = CSem(nc, "SIDX")
    SG = [CSem(nc, f"SG{i}") for i in range(3)]
    SXR = [CSem(nc, f"SXR{i}") for i in range(2)]
    SCAST = CSem(nc, "SCAST")
    SY1TW = CSem(nc, "SY1TW")
    SY1NW = [CSem(nc, f"SY1NW{i}") for i in range(2)]
    SW1 = [CSem(nc, f"SW1_{i}") for i in range(3)]
    SW2 = [CSem(nc, f"SW2_{i}") for i in range(4)]
    SY1G = [CSem(nc, f"SY1G{i}") for i in range(2)]
    SYN = CSem(nc, "SYN")
    SOUT = [CSem(nc, f"SOUT{i}") for i in range(2)]

    PE_F = CSem(nc, "PE_F")
    PE_SC = CSem(nc, "PE_SC")
    PE_AT = CSem(nc, "PE_AT")
    PE_OT = CSem(nc, "PE_OT")
    V_POOL = CSem(nc, "V_POOL")
    V_SUB = CSem(nc, "V_SUB")
    S_EXP = CSem(nc, "S_EXP")
    V_ABF = CSem(nc, "V_ABF")
    V_ATS = CSem(nc, "V_ATS")
    V_OTS = CSem(nc, "V_OTS")
    V_D = CSem(nc, "V_D")
    S_SQ = CSem(nc, "S_SQ")
    S_RSTD = CSem(nc, "S_RSTD")
    V_Y1 = CSem(nc, "V_Y1")
    V_NMR = CSem(nc, "V_NMR")
    S_EXPR = CSem(nc, "S_EXPR")
    V_GATE = CSem(nc, "V_GATE")
    VS = CSem(nc, "VS")
    SS = CSem(nc, "SS")

    PE_S1 = CSem(nc, "PE_S1")
    S_GELU = CSem(nc, "S_GELU")
    PE_S2 = CSem(nc, "PE_S2")
    V_FFE = CSem(nc, "V_FFE")
    W2FREE = CSem(nc, "W2FREE")
    V_D2 = CSem(nc, "V_D2")
    S_SQ2 = CSem(nc, "S_SQ2")
    S_RSTD2 = CSem(nc, "S_RSTD2")
    V_OUTR = CSem(nc, "V_OUTR")
    V_FIN = CSem(nc, "V_FIN")

    SETS_TGT = 16 * (6 if test_ffn_only else 4)
    SETG_TGT = 16 * (2 if test_ffn_only else 4)

    def wait_setup(eng):
        eng.wait_ge(SSETS.h, SETS_TGT)
        eng.wait_ge(SSETG.h, SETG_TGT)

    sb_ident32 = nc.alloc_sbuf_tensor("sb_ident32", [128, 128], F32)
    sb_identbf = nc.alloc_sbuf_tensor("sb_identbf", [128, 128], BF16)
    sb_idxT = nc.alloc_sbuf_tensor("sb_idxT", [128, NTAU], I32)
    sb_bfcT = nc.alloc_sbuf_tensor("sb_bfcT", [128, NE * MT], F32)
    sb_cst = nc.alloc_sbuf_tensor("sb_cst", [128, 8], F32)
    sb_bprojT = nc.alloc_sbuf_tensor("sb_bprojT", [9, D], BF16)
    sb_wr = nc.alloc_sbuf_tensor("sb_wr", [128, KT * 8], BF16)
    sb_gaten = nc.alloc_sbuf_tensor("sb_gaten", [128, NTAU * 8], F32)
    sb_gateT = nc.alloc_sbuf_tensor("sb_gateT", [9, T], BF16)

    def _flat2d(ap):
        return ap.rearrange("(r c) -> r c", c=65536)

    cast_jobs = []
    for e in range(NE):
        cast_jobs.append((_flat2d(w1s[e, 0]), _flat2d(w1t[e, 0])))
        cast_jobs.append((_flat2d(w1s[e, 1]), _flat2d(w1t[e, 1])))
        cast_jobs.append((_flat2d(w2s[e, 0]), _flat2d(w2t[e, 0])))
        cast_jobs.append((_flat2d(w2s[e, 1]), _flat2d(w2t[e, 1])))

    def fidx(c, j):
        return POOL_PC * c + j

    import contextlib

    with contextlib.ExitStack() as stA:
        if not test_ffn_only:
            sb_wqkvT = stA.enter_context(nc.sbuf_tensor("sb_wqkvT", [128, KT * 3072], BF16))
            sb_woutT = stA.enter_context(nc.sbuf_tensor("sb_woutT", [128, KT * 1024], BF16))
            sb_xg = [stA.enter_context(nc.sbuf_tensor(f"sb_xg{i}", [128, D], F32)) for i in range(3)]
            sb_xgT = stA.enter_context(nc.sbuf_tensor("sb_xgT", [128, KT * CH], BF16))
            sb_qT = stA.enter_context(nc.sbuf_tensor("sb_qT", [128, KT * CH], BF16))
            sb_kT = stA.enter_context(nc.sbuf_tensor("sb_kT", [128, KT * CH], BF16))
            sb_vnat = stA.enter_context(nc.sbuf_tensor("sb_vnat", [64, 4096], BF16))
            sb_psub = [stA.enter_context(nc.sbuf_tensor(f"sb_psub{i}", [128, 512], F32)) for i in range(2)]
            sb_afp8 = [stA.enter_context(nc.sbuf_tensor(f"sb_afp8{i}", [128, 512], F32)) for i in range(2)]
            sb_nm8 = [stA.enter_context(nc.sbuf_tensor(f"sb_nm8{i}", [128, 8], F32)) for i in range(2)]
            sb_zs8 = [stA.enter_context(nc.sbuf_tensor(f"sb_zs8{i}", [128, 8], F32)) for i in range(2)]
            sb_rz8 = [stA.enter_context(nc.sbuf_tensor(f"sb_rz8{i}", [128, 8], F32)) for i in range(2)]
            sb_atT = [stA.enter_context(nc.sbuf_tensor(f"sb_atT{i}", [64, 128], BF16)) for i in range(2)]
            sb_oT = stA.enter_context(nc.sbuf_tensor("sb_oT", [128, KT * CH], BF16))
            sb_xr = [stA.enter_context(nc.sbuf_tensor(f"sb_xr{i}", [128, D], F32)) for i in range(2)]
            sb_junk = stA.enter_context(nc.sbuf_tensor("sb_junk", [128, D], F32))
            sb_y1 = [stA.enter_context(nc.sbuf_tensor(f"sb_y1{i}", [128, D], F32)) for i in range(2)]
            sb_y1T = stA.enter_context(nc.sbuf_tensor("sb_y1T", [128, KT * CH], BF16))
            sb_lst = [stA.enter_context(nc.sbuf_tensor(f"sb_lst{i}", [128, 8], F32)) for i in range(2)]
            sb_lg = [stA.enter_context(nc.sbuf_tensor(f"sb_lg{i}", [128, 8], F32)) for i in range(2)]
            sb_rexp = [stA.enter_context(nc.sbuf_tensor(f"sb_rexp{i}", [128, 8], F32)) for i in range(2)]
            sb_rtmp = [stA.enter_context(nc.sbuf_tensor(f"sb_rtmp{i}", [128, 8], F32)) for i in range(2)]
            sb_rst = [stA.enter_context(nc.sbuf_tensor(f"sb_rst{i}", [128, 8], F32)) for i in range(2)]
            sb_gaug = stA.enter_context(nc.sbuf_tensor("sb_gaug", [128, 4 * 9], F32))
            pool_ps = [stA.enter_context(nc.psum_tensor(f"pool_ps{i}", [128, 512], F32)) for i in range(2)]
            scE_ps = [stA.enter_context(nc.psum_tensor(f"scE_ps{i}", [128, 512], F32)) for i in range(2)]
            scO_ps = [stA.enter_context(nc.psum_tensor(f"scO_ps{i}", [128, 512], F32)) for i in range(2)]
            atot_ps = [stA.enter_context(nc.psum_tensor(f"atot_ps{i}", [128, 512], F32)) for i in range(2)]

        with nc.Block() as blk:
            @blk.sync
            def _(sy):
                SSETS.inc(sy.dma_start(sb_ident32[:], ident32), 16)
                SSETS.inc(sy.dma_start(sb_identbf[:], identbf), 16)
                SIDX.inc(sy.dma_start(sb_idxT[:], idxT), 16)
                SSETS.inc(sy.dma_start(sb_bfcT[:], bfcT32), 16)
                SSETS.inc(sy.dma_start(sb_cst[:], cst32), 16)
                if test_ffn_only:
                    SSETS.inc(sy.dma_start(sb_gaten[:], gate_in), 16)
                    SSETS.inc(sy.dma_start(sb_gateT[:], gateT_in), 16)
                    return
                for tau in range(2):
                    SXR[tau % 2].inc(sy.dma_start(sb_xr[tau % 2][:], x[tau * 128:(tau + 1) * 128, :]), 16)
                for c in range(NCH):
                    for half in range(2):
                        for tt in range(2):
                            tau = 4 * c + 2 * half + tt + 2
                            if tau < NTAU:
                                sy.wait_ge(S_SQ.h, tau - 1)
                                sy.wait_ge(V_Y1.h, tau - 1)
                                SXR[tau % 2].inc(sy.dma_start(sb_xr[tau % 2][:], x[tau * 128:(tau + 1) * 128, :]), 16)
                        for tt in range(2):
                            tau = 4 * c + 2 * half + tt
                            sy.wait_ge(V_Y1.h, tau + 1)
                            SY1NW[tau % 2].inc(sy.dma_start(y1nat[tau * 128:(tau + 1) * 128, :], sb_y1[tau % 2][:]), 16)
                    sy.wait_ge(V_POOL.h, fidx(c, fj_y1t(3, 7)) + 1)
                    SY1TW.inc(sy.dma_start(y1Ts[c], sb_y1T[:]), 16)
                if True:
                    for s in [SXR[0], SXR[1], SY1NW[0], SY1NW[1], SY1TW]:
                        sy.wait_ge(s.h, s.n)

            @blk.gpsimd
            def _(gp):
                if not test_ffn_only:
                    SSETG.inc(gp.dma_start(sb_wqkvT[:], wqkvT32), 16)
                    SSETG.inc(gp.dma_start(sb_woutT[:], woutT32), 16)
                SSETG.inc(gp.dma_start(sb_wr[:], wrT32), 16)
                SSETG.inc(gp.dma_start(sb_bprojT[:], bprojT32), 16)
                if test_ffn_only:
                    for dst, src in cast_jobs:
                        SCAST.inc(gp.dma_start(dst, src), 16)
                    return
                gp.wait_ge(SIDX.h, 16)
                cj = 0
                for tau in range(NTAU):
                    if tau >= 3:
                        tp = tau - 3
                        gp.wait_ge(PE_F.h, fidx(tp // 4, 8 * (tp % 4 + 1)))
                    SG[tau % 3].inc(gp.indirect_dma_start(
                        sb_xg[tau % 3][:], None, x,
                        IndirectOffsetOnAxis(sb_idxT[:, tau:tau + 1], 0)), 16)
                    ncast = 2 if tau < 12 else 1
                    for _ in range(ncast):
                        if cj < len(cast_jobs):
                            dst, src = cast_jobs[cj]
                            SCAST.inc(gp.dma_start(dst, src), 16)
                            cj += 1
                while cj < len(cast_jobs):
                    dst, src = cast_jobs[cj]
                    SCAST.inc(gp.dma_start(dst, src), 16)
                    cj += 1
                for s in SG:
                    gp.wait_ge(s.h, s.n)



            if not test_ffn_only:
                @blk.tensor
                def _(pe):
                    wait_setup(pe)
                    for c in range(NCH):
                        for tt in range(TPC):
                            tau = 4 * c + tt
                            pe.wait_ge(SG[tau % 3].h, 16 * (tau // 3 + 1))
                            for k in range(KT):
                                n = fidx(c, fj_xgt(tt, k))
                                if n >= 2:
                                    pe.wait_ge(V_POOL.h, n - 1)
                                PE_F.inc(pe.transpose(
                                    pool_ps[n % 2][:, 0:128],
                                    sb_xg[tau % 3][:, k * 128:(k + 1) * 128],
                                    sb_ident32[:]))
                        pe.wait_ge(V_POOL.h, fidx(c, 32))
                        for ft in range(16):
                            n = fidx(c, 32 + ft)
                            pe.wait_ge(V_POOL.h, n - 1)
                            for k in range(KT):
                                inst = pe.matmul(
                                    pool_ps[n % 2][:, :],
                                    sb_wqkvT[:, k * 3072 + ft * 128: k * 3072 + (ft + 1) * 128],
                                    sb_xgT[:, k * CH:(k + 1) * CH],
                                    start=(k == 0), stop=(k == KT - 1))
                            PE_F.inc(inst)
                        for seg in range(4):
                            for nh in range(2):
                                n = fidx(c, fj_vt(seg, nh))
                                pe.wait_ge(V_POOL.h, n - 1)
                                for k in range(KT):
                                    inst = pe.matmul(
                                        pool_ps[n % 2][0:64, :],
                                        sb_xgT[:, k * CH + seg * 128: k * CH + (seg + 1) * 128: 2],
                                        sb_wqkvT[:, k * 3072 + 2048 + nh * 512: k * 3072 + 2048 + (nh + 1) * 512],
                                        start=(k == 0), stop=(k == KT - 1))
                                PE_F.inc(inst)
                        pe.wait_ge(V_POOL.h, fidx(c, 56))
                        for sl in range(4):
                            for oc in range(2):
                                o = 8 * c + 2 * sl + oc
                                par = o % 2
                                if o >= 2:
                                    pe.wait_ge(V_SUB.h, o - 1)
                                for pg, ps_t in ((0, scE_ps[par]), (1, scO_ps[par])):
                                    for he in range(4):
                                        h = 8 * oc + 2 * he + pg
                                        pb = pg * 64
                                        qb = (h // 2) * CH + sl * 128
                                        inst = pe.matmul(
                                            ps_t[:, he * 64:(he + 1) * 64],
                                            sb_qT[pb:pb + 64, qb:qb + 128],
                                            sb_kT[pb:pb + 64, qb:qb + 128:2],
                                            start=True, stop=True)
                                PE_SC.inc(inst)
                                pe.wait_ge(V_ABF.h, o + 1)
                                def emit_T(hh):
                                    h = 8 * oc + hh
                                    blk_ = (h % 2) * 4 + (hh // 2)
                                    i = 8 * o + hh
                                    if i >= 2:
                                        pe.wait_ge(V_ATS.h, i - 1)
                                        pe.wait_ge(V_OTS.h, i - 1)
                                    PE_AT.inc(pe.transpose(
                                        atot_ps[i % 2][0:64, 0:128],
                                        sb_afp8[par][:, blk_ * 64:(blk_ + 1) * 64],
                                        sb_ident32[:]))
                                def emit_O(hh):
                                    h = 8 * oc + hh
                                    i = 8 * o + hh
                                    pe.wait_ge(V_ATS.h, i + 1)
                                    PE_OT.inc(pe.matmul(
                                        atot_ps[i % 2][0:64, 128:256],
                                        sb_vnat[0:64, sl * 1024 + h * 64: sl * 1024 + (h + 1) * 64],
                                        sb_atT[i % 2][0:64, :],
                                        start=True, stop=True))
                                for hh in range(8):
                                    emit_T(hh)
                                    emit_O(hh)
                        pe.wait_ge(V_OTS.h, 64 * (c + 1))
                        for tt in range(TPC):
                            tau = 4 * c + tt
                            for dh in range(2):
                                n = fidx(c, fj_wo(tt, dh))
                                pe.wait_ge(V_POOL.h, n - 1)
                                for k in range(KT):
                                    inst = pe.matmul(
                                        pool_ps[n % 2][:, :],
                                        sb_oT[:, k * CH + tt * 128: k * CH + (tt + 1) * 128],
                                        sb_woutT[:, k * 1024 + dh * 512: k * 1024 + (dh + 1) * 512],
                                        start=(k == 0), stop=(k == KT - 1))
                                PE_F.inc(inst)
                            pe.wait_ge(V_Y1.h, tau + 1)
                            for k in range(KT):
                                n = fidx(c, fj_y1t(tt, k))
                                pe.wait_ge(V_POOL.h, n - 1)
                                PE_F.inc(pe.transpose(
                                    pool_ps[n % 2][:, 0:128],
                                    sb_y1[tau % 2][:, k * 128:(k + 1) * 128],
                                    sb_ident32[:]))
                            n = fidx(c, fj_rt(tt))
                            pe.wait_ge(V_POOL.h, n)
                            for k in range(KT):
                                inst = pe.matmul(
                                    pool_ps[n % 2][:, 0:8],
                                    sb_y1T[:, k * CH + tt * 128: k * CH + (tt + 1) * 128],
                                    sb_wr[:, k * 8:(k + 1) * 8],
                                    start=(k == 0), stop=(k == KT - 1))
                            PE_F.inc(inst)
                            n = fidx(c, fj_gt(tt))
                            pe.wait_ge(V_POOL.h, n - 1)
                            pe.wait_ge(V_GATE.h, tau + 1)
                            PE_F.inc(pe.transpose(
                                pool_ps[n % 2][0:9, 0:128],
                                sb_gaug[:, tt * 9:(tt + 1) * 9],
                                sb_ident32[:]))

                @blk.vector
                def _(v):
                    wait_setup(v)
                    for tt in range(TPC):
                        v.memset(sb_gaug[:, tt * 9: tt * 9 + 1], 1.0)
                    for c in range(NCH):
                        for tt in range(TPC):
                            for k in range(KT):
                                n = fidx(c, fj_xgt(tt, k))
                                v.wait_ge(PE_F.h, n + 1)
                                V_POOL.inc(v.tensor_copy(
                                    sb_xgT[:, k * CH + tt * 128: k * CH + (tt + 1) * 128],
                                    pool_ps[n % 2][:, 0:128]))
                        for ft in range(8):
                            n = fidx(c, fj_qt(ft))
                            v.wait_ge(PE_F.h, n + 1)
                            V_POOL.inc(v.tensor_scalar(
                                sb_qT[:, ft * CH:(ft + 1) * CH], pool_ps[n % 2][:, :],
                                QSCALE, None, ALU.mult))
                        for ft in range(8):
                            n = fidx(c, fj_kt(ft))
                            v.wait_ge(PE_F.h, n + 1)
                            V_POOL.inc(v.tensor_copy(
                                sb_kT[:, ft * CH:(ft + 1) * CH], pool_ps[n % 2][:, :]))
                        for seg in range(4):
                            for nh in range(2):
                                n = fidx(c, fj_vt(seg, nh))
                                v.wait_ge(PE_F.h, n + 1)
                                V_POOL.inc(v.tensor_copy(
                                    sb_vnat[0:64, seg * 1024 + nh * 512: seg * 1024 + (nh + 1) * 512],
                                    pool_ps[n % 2][0:64, :]))
                        for sl in range(4):
                            for oc in range(2):
                                o = 8 * c + 2 * sl + oc
                                par = o % 2
                                v.wait_ge(PE_SC.h, o + 1)
                                first = True
                                for half, ps_t in ((0, scE_ps[par]), (1, scO_ps[par])):
                                    sc3 = ps_t[:, 0:256].rearrange("p (h k) -> p h k", h=4)
                                    ps3 = sb_psub[par][:, half * 256:(half + 1) * 256].rearrange(
                                        "p (h k) -> p h k", h=4)
                                    nm = sb_nm8[par][:, half * 4:(half + 1) * 4]
                                    nvs = VS.inc(v.tensor_reduce(nm, sc3, AX.X, ALU.max, negate=True))
                                    v.wait_ge(VS.h, nvs)
                                    if first and o >= 2:
                                        v.wait_ge(S_EXP.h, o - 1)
                                    first = False
                                    a, b = bass.broadcast_tensor_aps(
                                        sc3, nm.rearrange("p (h z) -> p h z", z=1))
                                    inst = v.tensor_tensor(ps3, a, b, ALU.add)
                                    if half == 1:
                                        V_SUB.inc(inst)
                                v.wait_ge(S_EXP.h, o + 1)
                                full3 = sb_psub[par][:].rearrange("p (h k) -> p h k", h=8)
                                nvs = VS.inc(v.tensor_reduce(sb_zs8[par][:], full3, AX.X, ALU.add))
                                v.wait_ge(VS.h, nvs)
                                nvs = VS.inc(v.reciprocal(sb_rz8[par][:], sb_zs8[par][:]))
                                v.wait_ge(VS.h, nvs)
                                if o >= 2:
                                    v.wait_ge(PE_AT.h, 8 * (o - 1))
                                a, b = bass.broadcast_tensor_aps(
                                    full3, sb_rz8[par][:].rearrange("p (h z) -> p h z", z=1))
                                V_ABF.inc(v.tensor_tensor(
                                    sb_afp8[par][:].rearrange("p (h k) -> p h k", h=8), a, b, ALU.mult))
                                for hh in range(8):
                                    h = 8 * oc + hh
                                    i = 8 * o + hh
                                    v.wait_ge(PE_AT.h, i + 1)
                                    V_ATS.inc(v.tensor_copy(sb_atT[i % 2][0:64, :], atot_ps[i % 2][0:64, 0:128]))
                                    v.wait_ge(PE_OT.h, i + 1)
                                    pb = (h % 2) * 64
                                    V_OTS.inc(v.tensor_copy(
                                        sb_oT[pb:pb + 64, (h // 2) * CH + sl * 128: (h // 2) * CH + (sl + 1) * 128],
                                        atot_ps[i % 2][0:64, 128:256]))
                        for tt in range(TPC):
                            tau = 4 * c + tt
                            lp = tau % 2
                            rt = sb_xr[lp]
                            v.wait_ge(SXR[tau % 2].h, 16 * (tau // 2 + 1))
                            if tau >= 2:
                                v.wait_ge(S_SQ.h, tau - 1)
                                v.wait_ge(S_RSTD.h, tau - 1)
                            for dh in range(2):
                                n = fidx(c, fj_wo(tt, dh))
                                v.wait_ge(PE_F.h, n + 1)
                                V_POOL.inc(v.tensor_tensor(
                                    rt[:, dh * 512:(dh + 1) * 512],
                                    pool_ps[n % 2][:, :],
                                    rt[:, dh * 512:(dh + 1) * 512], ALU.add))
                            v.wait_ge(V_POOL.h, V_POOL.n)
                            nvs = VS.inc(v.tensor_reduce(sb_lst[lp][:, 0:1], rt[:], AX.X, ALU.add))
                            v.wait_ge(VS.h, nvs)
                            nvs = VS.inc(v.tensor_scalar(
                                sb_lst[lp][:, 1:2], sb_lst[lp][:, 0:1], -1.0 / D, None, ALU.mult))
                            v.wait_ge(VS.h, nvs)
                            V_D.inc(v.tensor_scalar(
                                rt[:], rt[:], sb_lst[lp][:, 1:2], None, ALU.add))
                            v.wait_ge(S_RSTD.h, tau + 1)
                            if tau >= 2:
                                pe_prev = fidx((tau - 2) // 4, fj_y1t((tau - 2) % 4, 7)) + 1
                                v.wait_ge(PE_F.h, pe_prev)
                                v.wait_ge(SY1NW[tau % 2].h, 16 * (tau // 2))
                            V_Y1.inc(v.tensor_scalar(
                                sb_y1[tau % 2][:], rt[:], sb_lst[lp][:, 4:5], None, ALU.mult))
                            for k in range(KT):
                                n = fidx(c, fj_y1t(tt, k))
                                v.wait_ge(PE_F.h, n + 1)
                                V_POOL.inc(v.tensor_copy(
                                    sb_y1T[:, k * CH + tt * 128: k * CH + (tt + 1) * 128],
                                    pool_ps[n % 2][:, 0:128]))
                            n = fidx(c, fj_rt(tt))
                            v.wait_ge(PE_F.h, n + 1)
                            if tau >= 2:
                                v.wait_ge(S_EXPR.h, tau - 1)
                            V_POOL.inc(v.tensor_copy(sb_lg[lp][:], pool_ps[n % 2][:, 0:8]))
                            v.wait_ge(V_POOL.h, V_POOL.n)
                            V_NMR.inc(v.tensor_reduce(
                                sb_rst[lp][:, 0:1], sb_lg[lp][:], AX.X, ALU.max, negate=True))
                            v.wait_ge(S_EXPR.h, tau + 1)
                            et = sb_rexp[lp]
                            rtmp = sb_rtmp[lp]
                            nvs = VS.inc(v.tensor_reduce(sb_rst[lp][:, 1:2], et[:], AX.X, ALU.max))
                            v.wait_ge(VS.h, nvs)
                            nvs = VS.inc(v.tensor_scalar(
                                rtmp[:], et[:], sb_rst[lp][:, 1:2], None, ALU.is_lt))
                            v.wait_ge(VS.h, nvs)
                            nvs = VS.inc(v.tensor_tensor(rtmp[:], rtmp[:], et[:], ALU.mult))
                            v.wait_ge(VS.h, nvs)
                            nvs = VS.inc(v.tensor_reduce(sb_rst[lp][:, 2:3], rtmp[:], AX.X, ALU.max))
                            v.wait_ge(VS.h, nvs)
                            nvs = VS.inc(v.tensor_scalar(
                                rtmp[:], et[:], sb_rst[lp][:, 2:3], None, ALU.is_ge))
                            v.wait_ge(VS.h, nvs)
                            nvs = VS.inc(v.tensor_tensor(rtmp[:], rtmp[:], et[:], ALU.mult))
                            v.wait_ge(VS.h, nvs)
                            nvs = VS.inc(v.tensor_reduce(sb_rst[lp][:, 3:4], rtmp[:], AX.X, ALU.add))
                            v.wait_ge(VS.h, nvs)
                            nvs = VS.inc(v.reciprocal(sb_rst[lp][:, 4:5], sb_rst[lp][:, 3:4]))
                            v.wait_ge(VS.h, nvs)
                            nvs = VS.inc(v.tensor_scalar(
                                sb_gaten[:, tau * 8:(tau + 1) * 8], rtmp[:],
                                sb_rst[lp][:, 4:5], None, ALU.mult))
                            v.wait_ge(VS.h, nvs)
                            V_GATE.inc(v.tensor_copy(
                                sb_gaug[:, tt * 9 + 1:(tt + 1) * 9],
                                sb_gaten[:, tau * 8:(tau + 1) * 8]))
                            n = fidx(c, fj_gt(tt))
                            v.wait_ge(PE_F.h, n + 1)
                            if c + 1 < NCH:
                                pass
                            V_POOL.inc(v.tensor_copy(
                                sb_gateT[0:9, c * CH + tt * 128: c * CH + (tt + 1) * 128],
                                pool_ps[n % 2][0:9, 0:128]))
                        if c + 1 < NCH:
                            v.wait_ge(SY1TW.h, 16 * (c + 1))

                @blk.scalar
                def _(s):
                    wait_setup(s)
                    for c in range(NCH):
                        for sl in range(4):
                            for oc in range(2):
                                o = 8 * c + 2 * sl + oc
                                par = o % 2
                                s.wait_ge(V_SUB.h, o + 1)
                                if o >= 2:
                                    s.wait_ge(V_ABF.h, o - 1)
                                S_EXP.inc(s.activation(
                                    sb_psub[par][:], sb_psub[par][:], AF.Exp, scale=1.0))
                        for tt in range(TPC):
                            tau = 4 * c + tt
                            lp = tau % 2
                            s.wait_ge(V_D.h, tau + 1)
                            S_SQ.inc(s.activation(
                                sb_junk[:], sb_xr[lp][:], AF.Square,
                                accum_out=sb_lst[lp][:, 2:3]))
                            s.wait_ge(S_SQ.h, S_SQ.n)
                            nss = SS.inc(s.activation(
                                sb_lst[lp][:, 3:4], sb_lst[lp][:, 2:3], AF.Ln,
                                bias=sb_cst[:, 0:1], scale=sb_cst[:, 2:3]))
                            s.wait_ge(SS.h, nss)
                            S_RSTD.inc(s.activation(
                                sb_lst[lp][:, 4:5], sb_lst[lp][:, 3:4], AF.Exp,
                                scale=sb_cst[:, 1:2]))
                            s.wait_ge(V_NMR.h, tau + 1)
                            if tau >= 2:
                                s.wait_ge(V_GATE.h, tau - 1)
                            S_EXPR.inc(s.activation(
                                sb_rexp[lp][:], sb_lg[lp][:], AF.Exp,
                                bias=sb_rst[lp][:, 0:1], scale=1.0))

    nc.all_engine_barrier()

    with contextlib.ExitStack() as stB:
        sb_y1g = [stB.enter_context(nc.sbuf_tensor(f"sb_y1g{i}", [128, KT * CH], BF16)) for i in range(2)]
        sb_h = stB.enter_context(nc.sbuf_tensor("sb_h", [128, MT * CH], BF16))
        sb_acc = stB.enter_context(nc.sbuf_tensor("sb_acc", [128, TPC * D], F32))
        ring1 = [stB.enter_context(nc.sbuf_tensor(f"ring1_{i}", [128, 4 * KT * 128], BF16)) for i in range(3)]
        ring2 = [stB.enter_context(nc.sbuf_tensor(f"ring2_{i}", [128, 4 * 1024], BF16)) for i in range(4)]
        sb_y1n = stB.enter_context(nc.sbuf_tensor("sb_y1n", [128, TPC * D], F32))
        sb_rf = [stB.enter_context(nc.sbuf_tensor(f"sb_rf{i}", [128, D], F32)) for i in range(2)]
        sb_outt = [stB.enter_context(nc.sbuf_tensor(f"sb_outt{i}", [128, D], F32)) for i in range(2)]
        sb_junk2 = stB.enter_context(nc.sbuf_tensor("sb_junk2", [128, D], F32))
        sb_lst2 = [stB.enter_context(nc.sbuf_tensor(f"sb_lst2_{i}", [128, 8], F32)) for i in range(2)]
        ff_ps = [stB.enter_context(nc.psum_tensor(f"ff_ps{i}", [128, 512], F32)) for i in range(8)]

        with nc.Block() as blk:
            @blk.gpsimd
            def _(gp):
                gp.wait_ge(SCAST.h, 16 * 4 * NE)
                for g in range(NCH):
                    for e in range(NE):
                        P = 9 * g + e
                        for q in range(8):
                            w2seq = 8 * P + q
                            if w2seq >= 4:
                                gp.wait_ge(W2FREE.h, w2seq - 3)
                            SW2[w2seq % 4].inc(gp.dma_start(ring2[w2seq % 4][:], w2_slice(e, q)), 16)

            @blk.sync
            def _(sy):
                def emit_w1_loads(g, e, mc0, mc1):
                    P = 9 * g + e
                    for mc in range(mc0, mc1):
                        w1seq = 8 * P + mc
                        if w1seq >= 3:
                            sy.wait_ge(PE_S1.h, 4 * (w1seq - 3) + 4)
                        SW1[w1seq % 3].inc(sy.dma_start(
                            ring1[w1seq % 3][:].rearrange("p (m kc) -> p m kc", m=4),
                            w1_chunk4(e, mc)), 16)

                for g in range(NCH):
                    if g >= 2:
                        sy.wait_ge(PE_S1.h, 32 * 9 * (g - 1))
                    if not test_ffn_only:
                        sy.wait_ge(SY1TW.h, 16 * (g + 1))
                    SY1G[g % 2].inc(sy.dma_start(sb_y1g[g % 2][:], y1Ts[g]), 16)
                    if g == 0:
                        sy.wait_ge(SCAST.h, 16 * 4 * NE)
                    emit_w1_loads(g, 0, 0, 8)
                    if g >= 1:
                        for tt in range(TPC):
                            taup = 4 * (g - 1) + tt
                            sy.wait_ge(V_OUTR.h, taup + 1)
                            SOUT[taup % 2].inc(sy.dma_start(
                                out[taup * 128:(taup + 1) * 128, :], sb_outt[taup % 2][:]), 16)
                        sy.wait_ge(V_FIN.h, 4 * g)
                    if not test_ffn_only:
                        sy.wait_ge(SY1NW[0].h, 16 * (2 * g + 2))
                        sy.wait_ge(SY1NW[1].h, 16 * (2 * g + 2))
                    SYN.inc(sy.dma_start(
                        sb_y1n[:].rearrange("p (tt d) -> p tt d", tt=TPC),
                        y1nat[g * CH:(g + 1) * CH, :].rearrange("(tt p) d -> p tt d", p=128)), 16)
                    for e in range(1, NE):
                        emit_w1_loads(g, e, 0, 8)
                g = NCH
                for tt in range(TPC):
                    taup = 4 * (g - 1) + tt
                    sy.wait_ge(V_OUTR.h, taup + 1)
                    SOUT[taup % 2].inc(sy.dma_start(
                        out[taup * 128:(taup + 1) * 128, :], sb_outt[taup % 2][:]), 16)
                sy.wait_ge(SOUT[0].h, 16 * (NTAU // 2))
                sy.wait_ge(SOUT[1].h, 16 * (NTAU // 2))
                for s in [SIDX, SY1TW, SY1NW[0], SY1NW[1], SXR[0], SXR[1],
                          SG[0], SG[1], SG[2], SCAST]:
                    if s.n > 0:
                        sy.wait_ge(s.h, s.n)

            @blk.tensor
            def _(pe):
                wait_setup(pe)
                for g in range(NCH):
                    for e in range(NE):
                        P = 9 * g + e
                        for m in range(MT):
                            if m % 4 == 0:
                                w1seq = 8 * P + m // 4
                                pe.wait_ge(SW1[w1seq % 3].h, 16 * (w1seq // 3 + 1))
                            if m == 0:
                                pe.wait_ge(SY1G[g % 2].h, 16 * (g // 2 + 1))
                            if m >= 8:
                                pe.wait_ge(S_GELU.h, 32 * P + m - 7)
                            elif P >= 1:
                                pe.wait_ge(V_FFE.h, 8 * (P - 1) + m + 1)
                            bank = (40 * P + m) % 8
                            w1seq = 8 * P + m // 4
                            for k in range(KT):
                                inst = pe.matmul(
                                    ff_ps[bank][:, :],
                                    ring1[w1seq % 3][:, (m % 4) * KT * 128 + k * 128: (m % 4) * KT * 128 + (k + 1) * 128],
                                    sb_y1g[g % 2][:, k * CH:(k + 1) * CH],
                                    start=(k == 0), stop=(k == KT - 1))
                                if m == 0 and k == 0 and P >= 1:
                                    W2FREE.inc(inst)
                            PE_S1.inc(inst)
                        pe.wait_ge(S_GELU.h, 32 * (P + 1))
                        if e == 0:
                            for j in range(8):
                                tt, dh = j // 2, j % 2
                                pe.matmul(
                                    ff_ps[j][:, :],
                                    sb_gateT[0:9, g * CH + tt * 128: g * CH + (tt + 1) * 128],
                                    sb_bprojT[0:9, dh * 512:(dh + 1) * 512],
                                    start=True, stop=False)
                        for k in range(32):
                            q, kk = k // 4, k % 4
                            if kk == 0:
                                w2seq = 8 * P + q
                                pe.wait_ge(SW2[w2seq % 4].h, 16 * (w2seq // 4 + 1))
                            for j in range(8):
                                tt, dh = j // 2, j % 2
                                inst = pe.matmul(
                                    ff_ps[j][:, :],
                                    sb_h[:, k * CH + tt * 128: k * CH + (tt + 1) * 128],
                                    ring2[(8 * P + q) % 4][:, kk * 1024 + dh * 512: kk * 1024 + (dh + 1) * 512],
                                    start=(e != 0 and k == 0), stop=(k == 31))
                                if k == 31:
                                    PE_S2.inc(inst)
                                elif kk == 0 and j == 0 and k > 0:
                                    W2FREE.inc(inst)

            @blk.scalar
            def _(s):
                wait_setup(s)

                def emit_ln2(gp_):
                    for tt in range(TPC):
                        tau = 4 * gp_ + tt
                        lp = tau % 2
                        s.wait_ge(V_D2.h, tau + 1)
                        S_SQ2.inc(s.activation(
                            sb_junk2[:], sb_rf[lp][:], AF.Square,
                            accum_out=sb_lst2[lp][:, 2:3]))
                        s.wait_ge(S_SQ2.h, S_SQ2.n)
                        nss = SS.inc(s.activation(
                            sb_lst2[lp][:, 3:4], sb_lst2[lp][:, 2:3], AF.Ln,
                            bias=sb_cst[:, 0:1], scale=sb_cst[:, 2:3]))
                        s.wait_ge(SS.h, nss)
                        S_RSTD2.inc(s.activation(
                            sb_lst2[lp][:, 4:5], sb_lst2[lp][:, 3:4], AF.Exp,
                            scale=sb_cst[:, 1:2]))

                for g in range(NCH):
                    for e in range(NE):
                        P = 9 * g + e
                        for m in range(MT):
                            s.wait_ge(PE_S1.h, 32 * P + m + 1)
                            bank = (40 * P + m) % 8
                            S_GELU.inc(s.activation(
                                sb_h[:, m * CH:(m + 1) * CH], ff_ps[bank][:, :],
                                AF.Gelu_apprx_tanh,
                                bias=sb_bfcT[:, e * MT + m: e * MT + m + 1], scale=1.0))
                        if e == 0 and g >= 1:
                            emit_ln2(g - 1)
                emit_ln2(NCH - 1)

            @blk.vector
            def _(v):
                wait_setup(v)
                acc_last = {}
                for g in range(NCH):
                    for e in range(NE):
                        P = 9 * g + e
                        for j in range(8):
                            tt, dh = j // 2, j % 2
                            v.wait_ge(PE_S2.h, 8 * P + j + 1)
                            accsl = sb_acc[:, tt * D + dh * 512: tt * D + (dh + 1) * 512]
                            if e == 0:
                                V_FFE.inc(v.tensor_copy(accsl, ff_ps[j][:, :]))
                            else:
                                v.wait_ge(V_FFE.h, acc_last[j])
                                tau = 4 * g + tt
                                V_FFE.inc(v.scalar_tensor_tensor(
                                    accsl, ff_ps[j][:, :],
                                    sb_gaten[:, tau * 8 + (e - 1): tau * 8 + e],
                                    accsl, ALU.mult, ALU.add))
                            acc_last[j] = V_FFE.n
                    v.wait_ge(SYN.h, 16 * (g + 1))
                    for tt in range(TPC):
                        tau = 4 * g + tt
                        lp = tau % 2
                        rf = sb_rf[lp]
                        v.wait_ge(V_FFE.h, max(acc_last[2 * tt], acc_last[2 * tt + 1]))
                        if tau >= 2:
                            v.wait_ge(S_SQ2.h, tau - 1)
                            v.wait_ge(S_RSTD2.h, tau - 1)
                            v.wait_ge(SOUT[tau % 2].h, 16 * (tau // 2))
                        nvs = VS.inc(v.tensor_tensor(
                            rf[:, 0:512], sb_acc[:, tt * D: tt * D + 512],
                            sb_y1n[:, tt * D: tt * D + 512], ALU.add))
                        V_FIN.inc(v.tensor_tensor(
                            rf[:, 512:1024], sb_acc[:, tt * D + 512: (tt + 1) * D],
                            sb_y1n[:, tt * D + 512: (tt + 1) * D], ALU.add))
                        v.wait_ge(VS.h, nvs)
                        v.wait_ge(V_FIN.h, V_FIN.n)
                        nvs = VS.inc(v.tensor_reduce(sb_lst2[lp][:, 0:1], rf[:], AX.X, ALU.add))
                        v.wait_ge(VS.h, nvs)
                        nvs = VS.inc(v.tensor_scalar(
                            sb_lst2[lp][:, 1:2], sb_lst2[lp][:, 0:1], -1.0 / D, None, ALU.mult))
                        v.wait_ge(VS.h, nvs)
                        V_D2.inc(v.tensor_scalar(
                            rf[:], rf[:], sb_lst2[lp][:, 1:2], None, ALU.add))
                        v.wait_ge(S_RSTD2.h, tau + 1)
                        V_OUTR.inc(v.tensor_scalar(
                            sb_outt[tau % 2][:], rf[:], sb_lst2[lp][:, 4:5], None, ALU.mult))

    return nc


def kernel(L0_attn_x, L0_attn_w_qkv, L0_attn_w_out, L0_ffn_Wfc_s, L0_ffn_bfc_s,
           L0_ffn_Wproj_s, L0_ffn_bproj_s, L0_ffn_Wr, L0_ffn_Wfc, L0_ffn_bfc,
           L0_ffn_Wproj, L0_ffn_bproj):
    xin = np.ascontiguousarray(np.asarray(L0_attn_x, dtype=np.float32))
    w_qkv = np.asarray(L0_attn_w_qkv, dtype=np.float32)
    w_out = np.asarray(L0_attn_w_out, dtype=np.float32)
    Wfc_s = np.asarray(L0_ffn_Wfc_s, dtype=np.float32)
    bfc_s = np.asarray(L0_ffn_bfc_s, dtype=np.float32)
    Wproj_s = np.asarray(L0_ffn_Wproj_s, dtype=np.float32)
    bproj_s = np.asarray(L0_ffn_bproj_s, dtype=np.float32)
    Wr = np.asarray(L0_ffn_Wr, dtype=np.float32)
    Wfc = np.asarray(L0_ffn_Wfc, dtype=np.float32)
    bfc = np.asarray(L0_ffn_bfc, dtype=np.float32)
    Wproj = np.asarray(L0_ffn_Wproj, dtype=np.float32)
    bproj = np.asarray(L0_ffn_bproj, dtype=np.float32)

    perm = hilbert_perm(T)
    idxT = np.ascontiguousarray(perm.reshape(NTAU, 128).T)
    ident32 = np.eye(128, dtype=np.float32)
    identbf = np.eye(128, dtype=BF)
    wqkvT32 = np.ascontiguousarray(
        w_qkv.T.reshape(KT, 128, 3072).transpose(1, 0, 2).reshape(128, KT * 3072))
    woutT32 = np.ascontiguousarray(
        w_out.T.reshape(KT, 128, 1024).transpose(1, 0, 2).reshape(128, KT * 1024))
    wrT32 = np.ascontiguousarray(
        Wr.reshape(KT, 128, 8).transpose(1, 0, 2).reshape(128, KT * 8))
    bfc_all = np.concatenate([bfc_s[None], bfc], axis=0)
    bfcT32 = np.ascontiguousarray(
        bfc_all.reshape(NE, MT, 128).transpose(2, 0, 1).reshape(128, NE * MT))
    bprojT32 = np.ascontiguousarray(np.concatenate([bproj_s[None], bproj], axis=0))
    W1_all = np.concatenate([Wfc_s[None], Wfc], axis=0)
    w1t = np.ascontiguousarray(
        W1_all.reshape(NE, KT, 128, MT, 128).transpose(0, 3, 2, 1, 4)).reshape(NE, 2, D * HID // 2)
    W2_all = np.concatenate([Wproj_s[None], Wproj], axis=0)
    w2t = np.ascontiguousarray(
        W2_all.reshape(NE, 8, 4, 128, 1024).transpose(0, 1, 3, 2, 4)).reshape(NE, 2, D * HID // 2)

    nc = build_program()
    cst = np.zeros((128, 8), dtype=np.float32)
    cst[:, 0] = EPS
    cst[:, 1] = -0.5
    cst[:, 2] = 1.0 / D
    shared = {
        "idxT": idxT, "ident32": ident32, "identbf": identbf, "cst32": cst,
        "wqkvT32": wqkvT32, "woutT32": woutT32, "wrT32": wrT32,
        "bfcT32": bfcT32, "bprojT32": bprojT32, "w1t": w1t, "w2t": w2t,
    }
    in_maps = []
    for b in range(B):
        m = dict(shared)
        m["x"] = np.ascontiguousarray(xin[b])
        in_maps.append(m)
    res = run_bass_kernel_spmd(nc, in_maps, core_ids=list(range(B)))
    return np.stack([r["out"] for r in res.results], axis=0)


# version 08 (final)
# speedup vs baseline: 1.8320x; 1.6696x over previous
import math
import numpy as np
import ml_dtypes
import concourse.bass as bass
import concourse.mybir as mybir
from concourse.bass_utils import run_bass_kernel_spmd
from concourse.bass_core_rust import IndirectOffsetOnAxis

F32 = mybir.dt.float32
BF16 = mybir.dt.bfloat16
I32 = mybir.dt.int32
AF = mybir.ActivationFunctionType
ALU = mybir.AluOpType
AX = mybir.AxisListType
BF = ml_dtypes.bfloat16

B = 8
T = 8192
D = 1024
KT = 8
HID = 4096
MT = 32
NE = 9
H = 16
HD = 64
CH = 512
NCH = 16
TPC = 4
EPS = 1e-5
QSCALE = float(HD) ** -0.5
NTAU = NCH * TPC
POOL_PC = 104


def hilbert_perm(M: int) -> np.ndarray:
    g = int(math.ceil(math.sqrt(M)))
    grid = np.arange(g * g).reshape(g, g)
    grid[1::2] = grid[1::2, ::-1]
    flat = grid.reshape(-1)
    flat = flat[flat < M][:M]
    return flat.astype(np.int32)


def fj_xgt(tt, k): return 8 * tt + k
def fj_qt(ft): return 32 + ft
def fj_kt(ft): return 40 + ft
def fj_vt(seg, nh): return 48 + 2 * seg + nh
def fj_wo(tt, dh): return 56 + 12 * tt + dh
def fj_y1t(tt, k): return 56 + 12 * tt + 2 + k
def fj_rt(tt): return 56 + 12 * tt + 10
def fj_gt(tt): return 56 + 12 * tt + 11


class CSem:
    def __init__(self, nc, name):
        self.h = nc.alloc_semaphore(name)
        self.n = 0

    def inc(self, inst, k=1):
        inst.then_inc(self.h, k)
        self.n += k
        return self.n


def build_program(test_ffn_only=False, nu_rt=None, skip_b2=False, skip_slsc=False, skip_b4=False, skip_b1=False, nsh=None, nos2=False, nobias=False, nomm=False, noreg=False):
    nc = bass.Bass("TRN2", target_bir_lowering=False)

    x = nc.dram_tensor("x", [T, D], F32, kind="ExternalInput").ap()
    idxT = nc.dram_tensor("idxT", [128, NTAU], I32, kind="ExternalInput").ap()
    tokid = nc.dram_tensor("tokid", [128, NTAU], I32, kind="ExternalInput").ap()
    ident32 = nc.dram_tensor("ident32", [128, 128], F32, kind="ExternalInput").ap()
    identbf = nc.dram_tensor("identbf", [128, 128], BF16, kind="ExternalInput").ap()
    wqkvT32 = nc.dram_tensor("wqkvT32", [128, KT * 3072], F32, kind="ExternalInput").ap()
    woutT32 = nc.dram_tensor("woutT32", [128, KT * 1024], F32, kind="ExternalInput").ap()
    wrT32 = nc.dram_tensor("wrT32", [128, KT * 8], F32, kind="ExternalInput").ap()
    bfcT32 = nc.dram_tensor("bfcT32", [128, NE * MT], F32, kind="ExternalInput").ap()
    cst32 = nc.dram_tensor("cst32", [128, 8], F32, kind="ExternalInput").ap()
    bprojT32 = nc.dram_tensor("bprojT32", [9, D], F32, kind="ExternalInput").ap()
    WHALF = D * HID // 2
    w1t = nc.dram_tensor("w1t", [NE, 2, WHALF], F32, kind="ExternalInput").ap()
    w2t = nc.dram_tensor("w2t", [NE, 2, WHALF], F32, kind="ExternalInput").ap()
    out = nc.dram_tensor("out", [T, D], F32, kind="ExternalOutput").ap()

    w1s = nc.dram_tensor("w1s", [NE, 2, WHALF], BF16, kind="Internal").ap()
    w2s = nc.dram_tensor("w2s", [NE, 2, WHALF], BF16, kind="Internal").ap()

    def w1_chunk4(e, mc):
        h, ml = divmod(mc, 4)
        blk = 4 * KT * 128 * 128
        return w1s[e, h][ml * blk:(ml + 1) * blk].rearrange(
            "(m p kc) -> p m kc", m=4, p=128)

    def w2_slice(e, q):
        h, ql = divmod(q, 4)
        blk = 4 * 1024 * 128
        return w2s[e, h][ql * blk:(ql + 1) * blk].rearrange("(p c) -> p c", p=128)
    y1kind = "ExternalInput" if test_ffn_only else "Internal"
    y1Ts = nc.dram_tensor("y1Ts", [NCH, 128, KT * CH], BF16, kind=y1kind).ap()
    y1g8 = nc.dram_tensor("y1g8", [T, D + 8], F32, kind=y1kind).ap()
    acc_hbm = nc.dram_tensor("acc_hbm", [T, D], F32, kind="Internal").ap()
    lists = nc.dram_tensor("lists", [8, T], I32, kind="Internal").ap()
    if test_ffn_only:
        gate_in = nc.dram_tensor("gate_in", [128, NTAU * 8], F32, kind="ExternalInput").ap()
        gateT_in = nc.dram_tensor("gateT_in", [9, T], BF16, kind="ExternalInput").ap()

    SSETS = CSem(nc, "SSETS")
    SSETG = CSem(nc, "SSETG")
    SIDX = CSem(nc, "SIDX")
    SG = [CSem(nc, f"SG{i}") for i in range(3)]
    SXR = [CSem(nc, f"SXR{i}") for i in range(2)]
    SCAST = CSem(nc, "SCAST")
    SY1TW = CSem(nc, "SY1TW")
    SY1NW = [CSem(nc, f"SY1NW{i}") for i in range(2)]
    SW1 = [CSem(nc, f"SW1_{i}") for i in range(3)]
    SW2 = [CSem(nc, f"SW2_{i}") for i in range(4)]
    SY1G = [CSem(nc, f"SY1G{i}") for i in range(2)]
    SYN = CSem(nc, "SYN")
    SOUT = [CSem(nc, f"SOUT{i}") for i in range(2)]

    PE_F = CSem(nc, "PE_F")
    PE_SC = CSem(nc, "PE_SC")
    PE_AT = CSem(nc, "PE_AT")
    PE_OT = CSem(nc, "PE_OT")
    V_POOL = CSem(nc, "V_POOL")
    V_SUB = CSem(nc, "V_SUB")
    S_EXP = CSem(nc, "S_EXP")
    V_ABF = CSem(nc, "V_ABF")
    V_ATS = CSem(nc, "V_ATS")
    V_OTS = CSem(nc, "V_OTS")
    V_D = CSem(nc, "V_D")
    S_SQ = CSem(nc, "S_SQ")
    S_RSTD = CSem(nc, "S_RSTD")
    V_Y1 = CSem(nc, "V_Y1")
    V_NMR = CSem(nc, "V_NMR")
    S_EXPR = CSem(nc, "S_EXPR")
    V_GATE = CSem(nc, "V_GATE")
    VS = CSem(nc, "VS")
    SS = CSem(nc, "SS")

    PE_S1 = CSem(nc, "PE_S1")
    S_GELU = CSem(nc, "S_GELU")
    PE_S2 = CSem(nc, "PE_S2")
    V_FFE = CSem(nc, "V_FFE")
    W2FREE = CSem(nc, "W2FREE")
    V_D2 = CSem(nc, "V_D2")
    S_SQ2 = CSem(nc, "S_SQ2")
    S_RSTD2 = CSem(nc, "S_RSTD2")
    V_OUTR = CSem(nc, "V_OUTR")
    V_FIN = CSem(nc, "V_FIN")
    SGW = CSem(nc, "SGW")
    SACCW = [CSem(nc, f"SACCW{i}") for i in range(2)]
    SCAT = CSem(nc, "SCAT")
    SGATH = [CSem(nc, f"SGATH{i}") for i in range(4)]
    SLSC = CSem(nc, "SLSC")
    SLINIT = CSem(nc, "SLINIT")
    SLIST = [CSem(nc, f"SLIST{i}") for i in range(2)]
    SB4A = CSem(nc, "SB4A")
    SB4Y = CSem(nc, "SB4Y")
    PE_TT = CSem(nc, "PE_TT")
    V_TE = CSem(nc, "V_TE")
    V_B2 = CSem(nc, "V_B2")
    V_GC = CSem(nc, "V_GC")
    V_ACC = CSem(nc, "V_ACC")
    S_MUL = CSem(nc, "S_MUL")

    SETS_TGT = 16 * (7 if test_ffn_only else 5)
    SETG_TGT = 16 * (2 if test_ffn_only else 4)

    def wait_setup(eng):
        eng.wait_ge(SSETS.h, SETS_TGT)
        eng.wait_ge(SSETG.h, SETG_TGT)

    sb_ident32 = nc.alloc_sbuf_tensor("sb_ident32", [128, 128], F32)
    sb_identbf = nc.alloc_sbuf_tensor("sb_identbf", [128, 128], BF16)
    sb_idxT = nc.alloc_sbuf_tensor("sb_idxT", [128, NTAU], I32)
    sb_tokid = nc.alloc_sbuf_tensor("sb_tokid", [128, NTAU], I32)
    sb_bfcT = nc.alloc_sbuf_tensor("sb_bfcT", [128, NE * MT], F32)
    sb_cst = nc.alloc_sbuf_tensor("sb_cst", [128, 8], F32)
    sb_bprojT = nc.alloc_sbuf_tensor("sb_bprojT", [9, D], BF16)
    sb_wr = nc.alloc_sbuf_tensor("sb_wr", [128, KT * 8], BF16)
    sb_gaten = nc.alloc_sbuf_tensor("sb_gaten", [128, NTAU * 8], F32)
    sb_gateT = nc.alloc_sbuf_tensor("sb_gateT", [9, T], BF16)
    sb_listE = [nc.alloc_sbuf_tensor(f"sb_listE{i}", [128, 4096 // 128], I32) for i in range(2)]
    sb_sidx = nc.alloc_sbuf_tensor("sb_sidx", [128, NTAU * 8], I32)

    def _flat2d(ap):
        return ap.rearrange("(r c) -> r c", c=65536)

    cast_jobs = []
    for e in range(NE):
        cast_jobs.append((_flat2d(w1s[e, 0]), _flat2d(w1t[e, 0])))
        cast_jobs.append((_flat2d(w1s[e, 1]), _flat2d(w1t[e, 1])))
        cast_jobs.append((_flat2d(w2s[e, 0]), _flat2d(w2t[e, 0])))
        cast_jobs.append((_flat2d(w2s[e, 1]), _flat2d(w2t[e, 1])))

    def fidx(c, j):
        return POOL_PC * c + j

    import contextlib

    with contextlib.ExitStack() as stA:
        if not test_ffn_only:
            sb_wqkvT = stA.enter_context(nc.sbuf_tensor("sb_wqkvT", [128, KT * 3072], BF16))
            sb_woutT = stA.enter_context(nc.sbuf_tensor("sb_woutT", [128, KT * 1024], BF16))
            sb_xg = [stA.enter_context(nc.sbuf_tensor(f"sb_xg{i}", [128, D], F32)) for i in range(3)]
            sb_xgT = stA.enter_context(nc.sbuf_tensor("sb_xgT", [128, KT * CH], BF16))
            sb_qT = stA.enter_context(nc.sbuf_tensor("sb_qT", [128, KT * CH], BF16))
            sb_kT = stA.enter_context(nc.sbuf_tensor("sb_kT", [128, KT * CH], BF16))
            sb_vnat = stA.enter_context(nc.sbuf_tensor("sb_vnat", [64, 4096], BF16))
            sb_psub = [stA.enter_context(nc.sbuf_tensor(f"sb_psub{i}", [128, 512], F32)) for i in range(2)]
            sb_afp8 = [stA.enter_context(nc.sbuf_tensor(f"sb_afp8{i}", [128, 512], F32)) for i in range(2)]
            sb_nm8 = [stA.enter_context(nc.sbuf_tensor(f"sb_nm8{i}", [128, 8], F32)) for i in range(2)]
            sb_zs8 = [stA.enter_context(nc.sbuf_tensor(f"sb_zs8{i}", [128, 8], F32)) for i in range(2)]
            sb_rz8 = [stA.enter_context(nc.sbuf_tensor(f"sb_rz8{i}", [128, 8], F32)) for i in range(2)]
            sb_atT = [stA.enter_context(nc.sbuf_tensor(f"sb_atT{i}", [64, 128], BF16)) for i in range(2)]
            sb_oT = stA.enter_context(nc.sbuf_tensor("sb_oT", [128, KT * CH], BF16))
            sb_xr = [stA.enter_context(nc.sbuf_tensor(f"sb_xr{i}", [128, D], F32)) for i in range(2)]
            sb_junk = stA.enter_context(nc.sbuf_tensor("sb_junk", [128, D], F32))
            sb_y1 = [stA.enter_context(nc.sbuf_tensor(f"sb_y1{i}", [128, D], F32)) for i in range(2)]
            sb_y1T = stA.enter_context(nc.sbuf_tensor("sb_y1T", [128, KT * CH], BF16))
            sb_lst = [stA.enter_context(nc.sbuf_tensor(f"sb_lst{i}", [128, 8], F32)) for i in range(2)]
            sb_lg = [stA.enter_context(nc.sbuf_tensor(f"sb_lg{i}", [128, 8], F32)) for i in range(2)]
            sb_rexp = [stA.enter_context(nc.sbuf_tensor(f"sb_rexp{i}", [128, 8], F32)) for i in range(2)]
            sb_rtmp = [stA.enter_context(nc.sbuf_tensor(f"sb_rtmp{i}", [128, 8], F32)) for i in range(2)]
            sb_rst = [stA.enter_context(nc.sbuf_tensor(f"sb_rst{i}", [128, 8], F32)) for i in range(2)]
            sb_gaug = stA.enter_context(nc.sbuf_tensor("sb_gaug", [128, 4 * 9], F32))
            pool_ps = [stA.enter_context(nc.psum_tensor(f"pool_ps{i}", [128, 512], F32)) for i in range(2)]
            scE_ps = [stA.enter_context(nc.psum_tensor(f"scE_ps{i}", [128, 512], F32)) for i in range(2)]
            scO_ps = [stA.enter_context(nc.psum_tensor(f"scO_ps{i}", [128, 512], F32)) for i in range(2)]
            atot_ps = [stA.enter_context(nc.psum_tensor(f"atot_ps{i}", [128, 512], F32)) for i in range(2)]

        with nc.Block() as blk:
            @blk.sync
            def _(sy):
                SSETS.inc(sy.dma_start(sb_ident32[:], ident32), 16)
                SSETS.inc(sy.dma_start(sb_identbf[:], identbf), 16)
                SIDX.inc(sy.dma_start(sb_idxT[:], idxT), 16)
                SSETS.inc(sy.dma_start(sb_bfcT[:], bfcT32), 16)
                SSETS.inc(sy.dma_start(sb_cst[:], cst32), 16)
                SSETS.inc(sy.dma_start(sb_tokid[:], tokid), 16)
                if test_ffn_only:
                    SSETS.inc(sy.dma_start(sb_gaten[:], gate_in), 16)
                    SSETS.inc(sy.dma_start(sb_gateT[:], gateT_in), 16)
                    return
                for tau in range(2):
                    SXR[tau % 2].inc(sy.dma_start(sb_xr[tau % 2][:], x[tau * 128:(tau + 1) * 128, :]), 16)
                for c in range(NCH):
                    for half in range(2):
                        for tt in range(2):
                            tau = 4 * c + 2 * half + tt + 2
                            if tau < NTAU:
                                sy.wait_ge(S_SQ.h, tau - 1)
                                sy.wait_ge(V_Y1.h, tau - 1)
                                SXR[tau % 2].inc(sy.dma_start(sb_xr[tau % 2][:], x[tau * 128:(tau + 1) * 128, :]), 16)
                        for tt in range(2):
                            tau = 4 * c + 2 * half + tt
                            sy.wait_ge(V_Y1.h, tau + 1)
                            SY1NW[tau % 2].inc(sy.dma_start(
                                y1g8[tau * 128:(tau + 1) * 128, 0:D], sb_y1[tau % 2][:]), 16)
                            sy.wait_ge(V_GATE.h, tau + 1)
                            SGW.inc(sy.dma_start(
                                y1g8[tau * 128:(tau + 1) * 128, D:D + 8],
                                sb_gaten[:, tau * 8:(tau + 1) * 8]), 16)
                    sy.wait_ge(V_POOL.h, fidx(c, fj_y1t(3, 7)) + 1)
                    SY1TW.inc(sy.dma_start(y1Ts[c], sb_y1T[:]), 16)
                if True:
                    for s in [SXR[0], SXR[1], SY1NW[0], SY1NW[1], SY1TW, SGW]:
                        sy.wait_ge(s.h, s.n)

            @blk.gpsimd
            def _(gp):
                if not test_ffn_only:
                    SSETG.inc(gp.dma_start(sb_wqkvT[:], wqkvT32), 16)
                    SSETG.inc(gp.dma_start(sb_woutT[:], woutT32), 16)
                SSETG.inc(gp.dma_start(sb_wr[:], wrT32), 16)
                SSETG.inc(gp.dma_start(sb_bprojT[:], bprojT32), 16)
                if test_ffn_only:
                    for dst, src in cast_jobs:
                        SCAST.inc(gp.dma_start(dst, src), 16)
                    return
                gp.wait_ge(SIDX.h, 16)
                cj = 0
                for tau in range(NTAU):
                    if tau >= 3:
                        tp = tau - 3
                        gp.wait_ge(PE_F.h, fidx(tp // 4, 8 * (tp % 4 + 1)))
                    SG[tau % 3].inc(gp.indirect_dma_start(
                        sb_xg[tau % 3][:], None, x,
                        IndirectOffsetOnAxis(sb_idxT[:, tau:tau + 1], 0)), 16)
                    ncast = 2 if tau < 12 else 1
                    for _ in range(ncast):
                        if cj < len(cast_jobs):
                            dst, src = cast_jobs[cj]
                            SCAST.inc(gp.dma_start(dst, src), 16)
                            cj += 1
                while cj < len(cast_jobs):
                    dst, src = cast_jobs[cj]
                    SCAST.inc(gp.dma_start(dst, src), 16)
                    cj += 1
                for s in SG:
                    gp.wait_ge(s.h, s.n)



            if not test_ffn_only:
                @blk.tensor
                def _(pe):
                    wait_setup(pe)
                    for c in range(NCH):
                        for tt in range(TPC):
                            tau = 4 * c + tt
                            pe.wait_ge(SG[tau % 3].h, 16 * (tau // 3 + 1))
                            for k in range(KT):
                                n = fidx(c, fj_xgt(tt, k))
                                if n >= 2:
                                    pe.wait_ge(V_POOL.h, n - 1)
                                PE_F.inc(pe.transpose(
                                    pool_ps[n % 2][:, 0:128],
                                    sb_xg[tau % 3][:, k * 128:(k + 1) * 128],
                                    sb_ident32[:]))
                        pe.wait_ge(V_POOL.h, fidx(c, 32))
                        for ft in range(16):
                            n = fidx(c, 32 + ft)
                            pe.wait_ge(V_POOL.h, n - 1)
                            for k in range(KT):
                                inst = pe.matmul(
                                    pool_ps[n % 2][:, :],
                                    sb_wqkvT[:, k * 3072 + ft * 128: k * 3072 + (ft + 1) * 128],
                                    sb_xgT[:, k * CH:(k + 1) * CH],
                                    start=(k == 0), stop=(k == KT - 1))
                            PE_F.inc(inst)
                        for seg in range(4):
                            for nh in range(2):
                                n = fidx(c, fj_vt(seg, nh))
                                pe.wait_ge(V_POOL.h, n - 1)
                                for k in range(KT):
                                    inst = pe.matmul(
                                        pool_ps[n % 2][0:64, :],
                                        sb_xgT[:, k * CH + seg * 128: k * CH + (seg + 1) * 128: 2],
                                        sb_wqkvT[:, k * 3072 + 2048 + nh * 512: k * 3072 + 2048 + (nh + 1) * 512],
                                        start=(k == 0), stop=(k == KT - 1))
                                PE_F.inc(inst)
                        pe.wait_ge(V_POOL.h, fidx(c, 56))
                        for sl in range(4):
                            for oc in range(2):
                                o = 8 * c + 2 * sl + oc
                                par = o % 2
                                if o >= 2:
                                    pe.wait_ge(V_SUB.h, o - 1)
                                for pg, ps_t in ((0, scE_ps[par]), (1, scO_ps[par])):
                                    for he in range(4):
                                        h = 8 * oc + 2 * he + pg
                                        pb = pg * 64
                                        qb = (h // 2) * CH + sl * 128
                                        inst = pe.matmul(
                                            ps_t[:, he * 64:(he + 1) * 64],
                                            sb_qT[pb:pb + 64, qb:qb + 128],
                                            sb_kT[pb:pb + 64, qb:qb + 128:2],
                                            start=True, stop=True)
                                PE_SC.inc(inst)
                                pe.wait_ge(V_ABF.h, o + 1)
                                def emit_T(hh):
                                    h = 8 * oc + hh
                                    blk_ = (h % 2) * 4 + (hh // 2)
                                    i = 8 * o + hh
                                    if i >= 2:
                                        pe.wait_ge(V_ATS.h, i - 1)
                                        pe.wait_ge(V_OTS.h, i - 1)
                                    PE_AT.inc(pe.transpose(
                                        atot_ps[i % 2][0:64, 0:128],
                                        sb_afp8[par][:, blk_ * 64:(blk_ + 1) * 64],
                                        sb_ident32[:]))
                                def emit_O(hh):
                                    h = 8 * oc + hh
                                    i = 8 * o + hh
                                    pe.wait_ge(V_ATS.h, i + 1)
                                    PE_OT.inc(pe.matmul(
                                        atot_ps[i % 2][0:64, 128:256],
                                        sb_vnat[0:64, sl * 1024 + h * 64: sl * 1024 + (h + 1) * 64],
                                        sb_atT[i % 2][0:64, :],
                                        start=True, stop=True))
                                for hh in range(8):
                                    emit_T(hh)
                                    emit_O(hh)
                        pe.wait_ge(V_OTS.h, 64 * (c + 1))
                        for tt in range(TPC):
                            tau = 4 * c + tt
                            for dh in range(2):
                                n = fidx(c, fj_wo(tt, dh))
                                pe.wait_ge(V_POOL.h, n - 1)
                                for k in range(KT):
                                    inst = pe.matmul(
                                        pool_ps[n % 2][:, :],
                                        sb_oT[:, k * CH + tt * 128: k * CH + (tt + 1) * 128],
                                        sb_woutT[:, k * 1024 + dh * 512: k * 1024 + (dh + 1) * 512],
                                        start=(k == 0), stop=(k == KT - 1))
                                PE_F.inc(inst)
                            pe.wait_ge(V_Y1.h, tau + 1)
                            for k in range(KT):
                                n = fidx(c, fj_y1t(tt, k))
                                pe.wait_ge(V_POOL.h, n - 1)
                                PE_F.inc(pe.transpose(
                                    pool_ps[n % 2][:, 0:128],
                                    sb_y1[tau % 2][:, k * 128:(k + 1) * 128],
                                    sb_ident32[:]))
                            n = fidx(c, fj_rt(tt))
                            pe.wait_ge(V_POOL.h, n)
                            for k in range(KT):
                                inst = pe.matmul(
                                    pool_ps[n % 2][:, 0:8],
                                    sb_y1T[:, k * CH + tt * 128: k * CH + (tt + 1) * 128],
                                    sb_wr[:, k * 8:(k + 1) * 8],
                                    start=(k == 0), stop=(k == KT - 1))
                            PE_F.inc(inst)
                            n = fidx(c, fj_gt(tt))
                            pe.wait_ge(V_POOL.h, n - 1)
                            pe.wait_ge(V_GATE.h, tau + 1)
                            PE_F.inc(pe.transpose(
                                pool_ps[n % 2][0:9, 0:128],
                                sb_gaug[:, tt * 9:(tt + 1) * 9],
                                sb_ident32[:]))

                @blk.vector
                def _(v):
                    wait_setup(v)
                    for tt in range(TPC):
                        v.memset(sb_gaug[:, tt * 9: tt * 9 + 1], 1.0)
                    for c in range(NCH):
                        for tt in range(TPC):
                            for k in range(KT):
                                n = fidx(c, fj_xgt(tt, k))
                                v.wait_ge(PE_F.h, n + 1)
                                V_POOL.inc(v.tensor_copy(
                                    sb_xgT[:, k * CH + tt * 128: k * CH + (tt + 1) * 128],
                                    pool_ps[n % 2][:, 0:128]))
                        for ft in range(8):
                            n = fidx(c, fj_qt(ft))
                            v.wait_ge(PE_F.h, n + 1)
                            V_POOL.inc(v.tensor_scalar(
                                sb_qT[:, ft * CH:(ft + 1) * CH], pool_ps[n % 2][:, :],
                                QSCALE, None, ALU.mult))
                        for ft in range(8):
                            n = fidx(c, fj_kt(ft))
                            v.wait_ge(PE_F.h, n + 1)
                            V_POOL.inc(v.tensor_copy(
                                sb_kT[:, ft * CH:(ft + 1) * CH], pool_ps[n % 2][:, :]))
                        for seg in range(4):
                            for nh in range(2):
                                n = fidx(c, fj_vt(seg, nh))
                                v.wait_ge(PE_F.h, n + 1)
                                V_POOL.inc(v.tensor_copy(
                                    sb_vnat[0:64, seg * 1024 + nh * 512: seg * 1024 + (nh + 1) * 512],
                                    pool_ps[n % 2][0:64, :]))
                        for sl in range(4):
                            for oc in range(2):
                                o = 8 * c + 2 * sl + oc
                                par = o % 2
                                v.wait_ge(PE_SC.h, o + 1)
                                first = True
                                for half, ps_t in ((0, scE_ps[par]), (1, scO_ps[par])):
                                    sc3 = ps_t[:, 0:256].rearrange("p (h k) -> p h k", h=4)
                                    ps3 = sb_psub[par][:, half * 256:(half + 1) * 256].rearrange(
                                        "p (h k) -> p h k", h=4)
                                    nm = sb_nm8[par][:, half * 4:(half + 1) * 4]
                                    nvs = VS.inc(v.tensor_reduce(nm, sc3, AX.X, ALU.max, negate=True))
                                    v.wait_ge(VS.h, nvs)
                                    if first and o >= 2:
                                        v.wait_ge(S_EXP.h, o - 1)
                                    first = False
                                    a, b = bass.broadcast_tensor_aps(
                                        sc3, nm.rearrange("p (h z) -> p h z", z=1))
                                    inst = v.tensor_tensor(ps3, a, b, ALU.add)
                                    if half == 1:
                                        V_SUB.inc(inst)
                                v.wait_ge(S_EXP.h, o + 1)
                                full3 = sb_psub[par][:].rearrange("p (h k) -> p h k", h=8)
                                nvs = VS.inc(v.tensor_reduce(sb_zs8[par][:], full3, AX.X, ALU.add))
                                v.wait_ge(VS.h, nvs)
                                nvs = VS.inc(v.reciprocal(sb_rz8[par][:], sb_zs8[par][:]))
                                v.wait_ge(VS.h, nvs)
                                if o >= 2:
                                    v.wait_ge(PE_AT.h, 8 * (o - 1))
                                a, b = bass.broadcast_tensor_aps(
                                    full3, sb_rz8[par][:].rearrange("p (h z) -> p h z", z=1))
                                V_ABF.inc(v.tensor_tensor(
                                    sb_afp8[par][:].rearrange("p (h k) -> p h k", h=8), a, b, ALU.mult))
                                for hh in range(8):
                                    h = 8 * oc + hh
                                    i = 8 * o + hh
                                    v.wait_ge(PE_AT.h, i + 1)
                                    V_ATS.inc(v.tensor_copy(sb_atT[i % 2][0:64, :], atot_ps[i % 2][0:64, 0:128]))
                                    v.wait_ge(PE_OT.h, i + 1)
                                    pb = (h % 2) * 64
                                    V_OTS.inc(v.tensor_copy(
                                        sb_oT[pb:pb + 64, (h // 2) * CH + sl * 128: (h // 2) * CH + (sl + 1) * 128],
                                        atot_ps[i % 2][0:64, 128:256]))
                        for tt in range(TPC):
                            tau = 4 * c + tt
                            lp = tau % 2
                            rt = sb_xr[lp]
                            v.wait_ge(SXR[tau % 2].h, 16 * (tau // 2 + 1))
                            if tau >= 2:
                                v.wait_ge(S_SQ.h, tau - 1)
                                v.wait_ge(S_RSTD.h, tau - 1)
                            for dh in range(2):
                                n = fidx(c, fj_wo(tt, dh))
                                v.wait_ge(PE_F.h, n + 1)
                                V_POOL.inc(v.tensor_tensor(
                                    rt[:, dh * 512:(dh + 1) * 512],
                                    pool_ps[n % 2][:, :],
                                    rt[:, dh * 512:(dh + 1) * 512], ALU.add))
                            v.wait_ge(V_POOL.h, V_POOL.n)
                            nvs = VS.inc(v.tensor_reduce(sb_lst[lp][:, 0:1], rt[:], AX.X, ALU.add))
                            v.wait_ge(VS.h, nvs)
                            nvs = VS.inc(v.tensor_scalar(
                                sb_lst[lp][:, 1:2], sb_lst[lp][:, 0:1], -1.0 / D, None, ALU.mult))
                            v.wait_ge(VS.h, nvs)
                            V_D.inc(v.tensor_scalar(
                                rt[:], rt[:], sb_lst[lp][:, 1:2], None, ALU.add))
                            v.wait_ge(S_RSTD.h, tau + 1)
                            if tau >= 2:
                                pe_prev = fidx((tau - 2) // 4, fj_y1t((tau - 2) % 4, 7)) + 1
                                v.wait_ge(PE_F.h, pe_prev)
                                v.wait_ge(SY1NW[tau % 2].h, 16 * (tau // 2))
                            V_Y1.inc(v.tensor_scalar(
                                sb_y1[tau % 2][:], rt[:], sb_lst[lp][:, 4:5], None, ALU.mult))
                            for k in range(KT):
                                n = fidx(c, fj_y1t(tt, k))
                                v.wait_ge(PE_F.h, n + 1)
                                V_POOL.inc(v.tensor_copy(
                                    sb_y1T[:, k * CH + tt * 128: k * CH + (tt + 1) * 128],
                                    pool_ps[n % 2][:, 0:128]))
                            n = fidx(c, fj_rt(tt))
                            v.wait_ge(PE_F.h, n + 1)
                            if tau >= 2:
                                v.wait_ge(S_EXPR.h, tau - 1)
                            V_POOL.inc(v.tensor_copy(sb_lg[lp][:], pool_ps[n % 2][:, 0:8]))
                            v.wait_ge(V_POOL.h, V_POOL.n)
                            V_NMR.inc(v.tensor_reduce(
                                sb_rst[lp][:, 0:1], sb_lg[lp][:], AX.X, ALU.max, negate=True))
                            v.wait_ge(S_EXPR.h, tau + 1)
                            et = sb_rexp[lp]
                            rtmp = sb_rtmp[lp]
                            nvs = VS.inc(v.tensor_reduce(sb_rst[lp][:, 1:2], et[:], AX.X, ALU.max))
                            v.wait_ge(VS.h, nvs)
                            nvs = VS.inc(v.tensor_scalar(
                                rtmp[:], et[:], sb_rst[lp][:, 1:2], None, ALU.is_lt))
                            v.wait_ge(VS.h, nvs)
                            nvs = VS.inc(v.tensor_tensor(rtmp[:], rtmp[:], et[:], ALU.mult))
                            v.wait_ge(VS.h, nvs)
                            nvs = VS.inc(v.tensor_reduce(sb_rst[lp][:, 2:3], rtmp[:], AX.X, ALU.max))
                            v.wait_ge(VS.h, nvs)
                            nvs = VS.inc(v.tensor_scalar(
                                rtmp[:], et[:], sb_rst[lp][:, 2:3], None, ALU.is_ge))
                            v.wait_ge(VS.h, nvs)
                            nvs = VS.inc(v.tensor_tensor(rtmp[:], rtmp[:], et[:], ALU.mult))
                            v.wait_ge(VS.h, nvs)
                            nvs = VS.inc(v.tensor_reduce(sb_rst[lp][:, 3:4], rtmp[:], AX.X, ALU.add))
                            v.wait_ge(VS.h, nvs)
                            nvs = VS.inc(v.reciprocal(sb_rst[lp][:, 4:5], sb_rst[lp][:, 3:4]))
                            v.wait_ge(VS.h, nvs)
                            nvs = VS.inc(v.tensor_scalar(
                                sb_gaten[:, tau * 8:(tau + 1) * 8], rtmp[:],
                                sb_rst[lp][:, 4:5], None, ALU.mult))
                            v.wait_ge(VS.h, nvs)
                            V_GATE.inc(v.tensor_copy(
                                sb_gaug[:, tt * 9 + 1:(tt + 1) * 9],
                                sb_gaten[:, tau * 8:(tau + 1) * 8]))
                            n = fidx(c, fj_gt(tt))
                            v.wait_ge(PE_F.h, n + 1)
                            if c + 1 < NCH:
                                pass
                            V_POOL.inc(v.tensor_copy(
                                sb_gateT[0:9, c * CH + tt * 128: c * CH + (tt + 1) * 128],
                                pool_ps[n % 2][0:9, 0:128]))
                        if c + 1 < NCH:
                            v.wait_ge(SY1TW.h, 16 * (c + 1))

                @blk.scalar
                def _(s):
                    wait_setup(s)
                    for c in range(NCH):
                        for sl in range(4):
                            for oc in range(2):
                                o = 8 * c + 2 * sl + oc
                                par = o % 2
                                s.wait_ge(V_SUB.h, o + 1)
                                if o >= 2:
                                    s.wait_ge(V_ABF.h, o - 1)
                                S_EXP.inc(s.activation(
                                    sb_psub[par][:], sb_psub[par][:], AF.Exp, scale=1.0))
                        for tt in range(TPC):
                            tau = 4 * c + tt
                            lp = tau % 2
                            s.wait_ge(V_D.h, tau + 1)
                            S_SQ.inc(s.activation(
                                sb_junk[:], sb_xr[lp][:], AF.Square,
                                accum_out=sb_lst[lp][:, 2:3]))
                            s.wait_ge(S_SQ.h, S_SQ.n)
                            nss = SS.inc(s.activation(
                                sb_lst[lp][:, 3:4], sb_lst[lp][:, 2:3], AF.Ln,
                                bias=sb_cst[:, 0:1], scale=sb_cst[:, 2:3]))
                            s.wait_ge(SS.h, nss)
                            S_RSTD.inc(s.activation(
                                sb_lst[lp][:, 4:5], sb_lst[lp][:, 3:4], AF.Exp,
                                scale=sb_cst[:, 1:2]))
                            s.wait_ge(V_NMR.h, tau + 1)
                            if tau >= 2:
                                s.wait_ge(V_GATE.h, tau - 1)
                            S_EXPR.inc(s.activation(
                                sb_rexp[lp][:], sb_lg[lp][:], AF.Exp,
                                bias=sb_rst[lp][:, 0:1], scale=1.0))

    nc.all_engine_barrier()

    CAP = 2560
    SGPE = CAP // CH
    NRT = 8 * SGPE
    NU_FULL = NCH + NRT
    BIGIDX = 1 << 20

    mask_ready = [0] * NTAU
    maskev_done = [0] * NTAU
    scan_done = [0] * NTAU
    posev_done = [0] * NTAU
    b2_done = [0] * NTAU
    units = [("sh", g, 0) for g in range(NCH)] + [("rt", e, sg) for e in range(1, 9) for sg in range(SGPE)]
    if nu_rt is not None:
        units = units[:NCH + nu_rt]
    if nsh is not None:
        units = units[:nsh]
    NRT_ACT = len(units) - NCH

    def unit_event_list(u):
        kind = units[u][0]
        ev = []
        if kind == "rt":
            ev += [("T", t) for t in range(32)]
        ev += [("S1", m) for m in range(32)]
        ev += [("S2", j) for j in range(8)]
        return ev

    NU = len(units)
    b_eventbase = []
    _acc = 0
    for _u in range(NU):
        b_eventbase.append(_acc)
        _acc += len(unit_event_list(_u))
    b2c = {}
    evac_sched = []
    cnt = {"V_TE": 0, "S_GELU": 0, "V_ACC": 0, "S_MUL": 0}
    for u in range(NU):
        kind = units[u][0]
        for what, idx in unit_event_list(u):
            if what == "T":
                cnt["V_TE"] += 1
                evac_sched.append(("V_TE", cnt["V_TE"]))
            elif what == "S1":
                cnt["S_GELU"] += 1
                evac_sched.append(("S_GELU", cnt["S_GELU"]))
            else:
                if kind == "sh":
                    cnt["V_ACC"] += 1
                    evac_sched.append(("V_ACC", cnt["V_ACC"]))
                else:
                    tile_done = cnt["S_MUL"] + 1
                    evac_sched.append(("S_MUL", tile_done))
                    if idx % 2 == 1:
                        cnt["S_MUL"] += 1

    with contextlib.ExitStack() as stB:
        sb_y1g = [stB.enter_context(nc.sbuf_tensor(f"sb_y1g{i}", [128, KT * CH], BF16)) for i in range(2)]
        sb_h = stB.enter_context(nc.sbuf_tensor("sb_h", [128, MT * CH], BF16))
        ring1 = [stB.enter_context(nc.sbuf_tensor(f"ring1_{i}", [128, 4 * KT * 128], BF16)) for i in range(3)]
        ring2 = [stB.enter_context(nc.sbuf_tensor(f"ring2_{i}", [128, 4 * 1024], BF16)) for i in range(4)]
        sb_acc2 = [stB.enter_context(nc.sbuf_tensor(f"sb_acc2_{i}", [128, D], F32)) for i in range(2)]
        sb_maskn = stB.enter_context(nc.sbuf_tensor("sb_maskn", [128, NTAU * 8], F32))
        sb_maskT = stB.enter_context(nc.sbuf_tensor("sb_maskT", [8, 1024], F32))
        sb_posT = stB.enter_context(nc.sbuf_tensor("sb_posT", [8, 1024], F32))
        sb_zrow = stB.enter_context(nc.sbuf_tensor("sb_zrow", [8, 1024], F32))
        sb_carry = stB.enter_context(nc.sbuf_tensor("sb_carry", [8, 2], F32))
        sb_b2t = [stB.enter_context(nc.sbuf_tensor(f"sb_b2t{i}", [128, 8], F32)) for i in range(4)]
        sb_big = stB.enter_context(nc.sbuf_tensor("sb_big", [128, 64], I32))
        sb_eoff = stB.enter_context(nc.sbuf_tensor("sb_eoff", [128, 8], F32))

        with contextlib.ExitStack() as stB3:
            sb_tstage = [stB3.enter_context(nc.sbuf_tensor(f"sb_tstage{i}", [128, D + 8], F32)) for i in range(4)]
            sb_y1te = stB3.enter_context(nc.sbuf_tensor("sb_y1te", [128, KT * CH], BF16))
            sb_routed = [stB3.enter_context(nc.sbuf_tensor(f"sb_routed{i}", [128, D], F32)) for i in range(2)]
            sb_gcol = [stB3.enter_context(nc.sbuf_tensor(f"sb_gcol{i}", [128, 4], F32)) for i in range(2)]
            ff_ps = [stB3.enter_context(nc.psum_tensor(f"ff_ps{i}", [128, 512], F32)) for i in range(8)]

            sems = {"V_TE": V_TE, "S_GELU": S_GELU, "V_ACC": V_ACC, "S_MUL": S_MUL}

            with nc.Block() as blk:
                @blk.vector
                def _(v):
                    wait_setup(v)
                    if skip_b1:
                        b2c["total"] = 0
                        return
                    if skip_b2:
                        b2c["total"] = 0
                    V_B2.inc(v.memset(sb_big[:], BIGIDX))
                    for e8 in range(8):
                        v.memset(sb_eoff[:, e8:e8 + 1], float(e8 * T))
                    if skip_b2:
                        _skip = True
                    V_B2.inc(v.memset(sb_zrow[:], 0.0))
                    if not skip_b2:
                        V_B2.inc(v.tensor_scalar(sb_maskn[:], sb_gaten[:], 0.0, None, ALU.is_gt))
                        v.wait_ge(V_B2.h, V_B2.n)
                        V_B2.inc(v.memset(sb_carry[:, 0:1], 0.0))
                        for tau in range(NTAU):
                            mask_ready[tau] = V_B2.n
                        for ch in range(8):
                            if ch >= 1:
                                v.wait_ge(PE_TT.h, 16 * ch)
                            for tl in range(8):
                                tau = 8 * ch + tl
                                v.wait_ge(PE_TT.h, 16 * ch + tl + 1)
                                V_B2.inc(v.tensor_copy(
                                    sb_maskT[:, tl * 128:(tl + 1) * 128],
                                    ff_ps[tau % 2][0:8, 0:128]))
                                maskev_done[tau] = V_B2.n
                            v.wait_ge(V_B2.h, V_B2.n)
                            V_B2.inc(v.tensor_tensor_scan(
                                sb_posT[:], sb_maskT[:], sb_zrow[:],
                                sb_carry[:, 0:1], ALU.add, ALU.add))
                            v.wait_ge(V_B2.h, V_B2.n)
                            V_B2.inc(v.tensor_copy(sb_carry[:, 0:1], sb_posT[:, 1023:1024]))
                            for tl in range(8):
                                tau = 8 * ch + tl
                                scan_done[tau] = V_B2.n
                            for tl in range(8):
                                tau = 8 * ch + tl
                                v.wait_ge(PE_TT.h, 16 * (tau // 8) + 8 + (tau % 8) + 1)
                                v.wait_ge(PE_TT.h, 16 * (tau // 8) + 8 + (tau % 8) + 1)
                                t0 = sb_b2t[0]
                                t1 = sb_b2t[1]
                                t2 = sb_b2t[2]
                                pos1 = sb_b2t[3]
                                V_B2.inc(v.tensor_copy(pos1[:], ff_ps[2 + tau % 2][0:128, 0:8]))
                                posev_done[tau] = V_B2.n
                                v.wait_ge(V_B2.h, V_B2.n)
                                V_B2.inc(v.tensor_scalar(pos1[:], pos1[:], -1.0, None, ALU.add))
                                v.wait_ge(V_B2.h, V_B2.n)
                                V_B2.inc(v.tensor_scalar(t0[:], pos1[:], 1.0 / 128, None, ALU.mult))
                                v.wait_ge(V_B2.h, V_B2.n)
                                ti = sb_sidx[:, tau * 8:(tau + 1) * 8]
                                V_B2.inc(v.tensor_copy(ti, t0[:]))
                                v.wait_ge(V_B2.h, V_B2.n)
                                V_B2.inc(v.tensor_copy(t0[:], ti))
                                v.wait_ge(V_B2.h, V_B2.n)
                                V_B2.inc(v.scalar_tensor_tensor(t1[:], t0[:], -128.0, pos1[:], ALU.mult, ALU.add))
                                v.wait_ge(V_B2.h, V_B2.n)
                                V_B2.inc(v.tensor_scalar(t2[:], t1[:], 0.0, None, ALU.is_lt))
                                v.wait_ge(V_B2.h, V_B2.n)
                                V_B2.inc(v.scalar_tensor_tensor(t1[:], t2[:], 128.0, t1[:], ALU.mult, ALU.add))
                                v.wait_ge(V_B2.h, V_B2.n)
                                V_B2.inc(v.tensor_tensor(t0[:], t0[:], t2[:], ALU.subtract))
                                v.wait_ge(V_B2.h, V_B2.n)
                                V_B2.inc(v.scalar_tensor_tensor(t2[:], t1[:], 64.0, t0[:], ALU.mult, ALU.add))
                                v.wait_ge(V_B2.h, V_B2.n)
                                V_B2.inc(v.tensor_tensor(t2[:], t2[:], sb_eoff[:], ALU.add))
                                v.wait_ge(V_B2.h, V_B2.n)
                                V_B2.inc(v.tensor_scalar(t0[:], sb_maskn[:, tau * 8:(tau + 1) * 8],
                                                         -float(BIGIDX), float(BIGIDX), ALU.mult, ALU.add))
                                v.wait_ge(V_B2.h, V_B2.n)
                                V_B2.inc(v.tensor_tensor(t2[:], t2[:], t0[:], ALU.add))
                                v.wait_ge(V_B2.h, V_B2.n)
                                V_B2.inc(v.tensor_copy(ti, t2[:]))
                                b2_done[tau] = V_B2.n
                        b2c["total"] = V_B2.n
                    for u in range(NU) if not nomm else []:
                        kind, a1, a2 = units[u]
                        if kind == "sh":
                            g = a1
                            for j in range(8) if not nos2 else []:
                                tt, dh = j // 2, j % 2
                                v.wait_ge(PE_S2.h, 8 * u + j + 1)
                                sig = 4 * g + tt
                                if dh == 0 and sig >= 2:
                                    v.wait_ge(SACCW[sig % 2].h, 16 * (sig // 2))
                                V_ACC.inc(v.tensor_copy(
                                    sb_acc2[sig % 2][:, dh * 512:(dh + 1) * 512], ff_ps[j][:, :]))
                        else:
                            e = a1
                            sg = a2
                            urt = u - NCH
                            first = True
                            for tl in range(4):
                                gam = 4 * urt + tl
                                for k in range(KT):
                                    tord = 32 * urt + 8 * tl + k
                                    v.wait_ge(PE_TT.h, 2 * NTAU + tord + 1)
                                    if first:
                                        v.wait_ge(PE_S1.h, 32 * u)
                                        first = False
                                    bank = (b_eventbase[u] + 8 * tl + k) % 8
                                    V_TE.inc(v.tensor_copy(
                                        sb_y1te[:, k * CH + tl * 128: k * CH + (tl + 1) * 128],
                                        ff_ps[bank][:, 0:128]))
                                v.wait_ge(SGATH[gam % 4].h, 16 * (gam // 4 + 1))
                                if urt >= 2:
                                    v.wait_ge(S_MUL.h, 4 * (urt - 1))
                                V_GC.inc(v.tensor_copy(
                                    sb_gcol[urt % 2][:, tl:tl + 1],
                                    sb_tstage[gam % 4][:, D + e - 1: D + e]))

                @blk.gpsimd
                def _(gp):
                    if skip_b1:
                        return
                    gp.wait_ge(SCAST.h, 16 * 4 * NE)
                    if not noreg:
                        bc_list = gp.to_reg(8 * T)
                        bc_tok = gp.to_reg(T)
                    if not (skip_b2 or skip_slsc):
                        gp.wait_ge(V_B2.h, 1)
                        for e8 in range(8):
                            SLINIT.inc(gp.dma_start(
                                lists[e8].rearrange("(p c) -> p c", p=128), sb_big[:]), 16)
                        gp.wait_ge(SLINIT.h, 16 * 8)
                        for tau in range(NTAU):
                            gp.wait_ge(V_B2.h, b2_done[tau])
                            for e8 in range(8):
                                if SLSC.n > 0:
                                    gp.wait_ge(SLSC.h, SLSC.n)
                                SLSC.inc(gp.indirect_dma_start(
                                    lists.rearrange("e (n o) -> (e n) o", o=1),
                                    IndirectOffsetOnAxis(sb_sidx[:, tau * 8 + e8: tau * 8 + e8 + 1], 0),
                                    sb_tokid[:, tau:tau + 1], None,
                                    bounds_check=bc_list, oob_is_err=False), 16)
                    def emit_scatters(us):
                        kindp, ep, sgp = units[us]
                        urtp = us - NCH
                        for tt in range(TPC):
                            sig = 4 * urtp + tt
                            if sgp == 0 and tt == 0:
                                if ep == 1:
                                    gp.wait_ge(SACCW[0].h, 16 * (NTAU // 2))
                                    gp.wait_ge(SACCW[1].h, 16 * (NTAU // 2))
                                else:
                                    gp.wait_ge(SCAT.h, 16 * 4 * SGPE * (ep - 1))
                            gp.wait_ge(S_MUL.h, sig + 1)
                            if SCAT.n > 0:
                                gp.wait_ge(SCAT.h, SCAT.n)
                            SCAT.inc(gp.indirect_dma_start(
                                acc_hbm,
                                IndirectOffsetOnAxis(
                                    sb_listE[ep % 2][:, sgp * 4 + tt: sgp * 4 + tt + 1], 0),
                                sb_routed[tt % 2][:], None, compute_op=ALU.add,
                                bounds_check=bc_tok, oob_is_err=False), 16)

                    for u in range(NU):
                        kind, a1, a2 = units[u]
                        if u >= 1 and units[u - 1][0] == "rt":
                            emit_scatters(u - 1)
                        pass
                        if u + 1 <= NU - 1 and units[u + 1][0] == "rt":
                            _, e, sg = units[u + 1]
                            urt = (u + 1) - NCH
                            for tl in range(4):
                                gam = 4 * urt + tl
                                if gam >= 4:
                                    gp.wait_ge(PE_TT.h, 128 + 8 * (gam - 3))
                                    gp.wait_ge(V_GC.h, gam - 3)
                                gp.wait_ge(SLIST[e % 2].h, 16 * ((e + 1) // 2))
                                SGATH[gam % 4].inc(gp.indirect_dma_start(
                                    sb_tstage[gam % 4][:], None, y1g8,
                                    IndirectOffsetOnAxis(
                                        sb_listE[e % 2][:, sg * 4 + tl: sg * 4 + tl + 1], 0),
                                    bounds_check=bc_tok, oob_is_err=False), 16)
                    if units[NU - 1][0] == "rt":
                        emit_scatters(NU - 1)
                    for s in SGATH + [SLSC, SLINIT]:
                        if s.n > 0:
                            gp.wait_ge(s.h, s.n)

                @blk.sync
                def _(sy):
                    if skip_b1:
                        return
                    def emit_w1_loads(u, we, mc0, mc1):
                        for mc in range(mc0, mc1):
                            w1seq = 8 * u + mc
                            if nomm:
                                if w1seq >= 3:
                                    sy.wait_ge(SW1[w1seq % 3].h, 16 * (w1seq // 3))
                            elif w1seq >= 3:
                                sy.wait_ge(PE_S1.h, 4 * (w1seq - 3) + 4)
                            SW1[w1seq % 3].inc(sy.dma_start(
                                ring1[w1seq % 3][:].rearrange("p (m kc) -> p m kc", m=4),
                                w1_chunk4(we, mc)), 16)

                    for u in range(NU):
                        kind, a1, a2 = units[u]
                        if kind == "sh":
                            g = a1
                            if g >= 2 and not nomm:
                                sy.wait_ge(PE_S1.h, 32 * (g - 1))
                            if not test_ffn_only:
                                sy.wait_ge(SY1TW.h, 16 * (g + 1))
                            SY1G[g % 2].inc(sy.dma_start(sb_y1g[g % 2][:], y1Ts[g]), 16)
                            if g == 0:
                                sy.wait_ge(SCAST.h, 16 * 4 * NE)
                            if g >= 1:
                                for tt in range(TPC):
                                    sig = 4 * (g - 1) + tt
                                    sy.wait_ge(V_ACC.h, 8 * (g - 1) + 2 * tt + 2)
                                    SACCW[sig % 2].inc(sy.dma_start(
                                        acc_hbm[sig * 128:(sig + 1) * 128, :],
                                        sb_acc2[sig % 2][:]), 16)
                            emit_w1_loads(u, 0, 0, 8)
                        else:
                            e, sg = a1, a2
                            if sg == 0:
                                sy.wait_ge(SLSC.h, 16 * 512)
                                if e >= 3:
                                    gl = 4 * SGPE * (e - 2) - 1
                                    for s_ in range(4):
                                        if gl >= s_:
                                            sy.wait_ge(SGATH[s_].h, 16 * ((gl - s_) // 4 + 1))
                                    sy.wait_ge(SCAT.h, 16 * 4 * SGPE * (e - 2))
                                SLIST[e % 2].inc(sy.dma_start(
                                    sb_listE[e % 2][:, 0:CAP // 128],
                                    lists[e - 1].rearrange("(p c) -> p c", p=128)[:, 0:CAP // 128]), 16)
                            if u == NCH:
                                for tt in range(TPC):
                                    sig = 4 * (NCH - 1) + tt
                                    sy.wait_ge(V_ACC.h, 8 * (NCH - 1) + 2 * tt + 2)
                                    SACCW[(4 * (NCH - 1) + tt) % 2].inc(sy.dma_start(
                                        acc_hbm[sig * 128:(sig + 1) * 128, :],
                                        sb_acc2[sig % 2][:]), 16)
                            emit_w1_loads(u, e, 0, 8)
                    if units[NU - 1][0] == "sh" and not nomm and NU == NCH:
                        for tt in range(TPC):
                            sig = 4 * (NCH - 1) + tt
                            sy.wait_ge(V_ACC.h, 8 * (NCH - 1) + 2 * tt + 2)
                            SACCW[sig % 2].inc(sy.dma_start(
                                acc_hbm[sig * 128:(sig + 1) * 128, :],
                                sb_acc2[sig % 2][:]), 16)

                @blk.tensor
                def _(pe):
                    wait_setup(pe)
                    if skip_b1:
                        return
                    for ch in range(8) if not skip_b2 else []:
                        for tl in range(8):
                            tau = 8 * ch + tl
                            pe.wait_ge(V_B2.h, mask_ready[tau])
                            if tau >= 2:
                                pe.wait_ge(V_B2.h, maskev_done[tau - 2])
                            PE_TT.inc(pe.transpose(
                                ff_ps[tau % 2][0:8, 0:128],
                                sb_maskn[:, tau * 8:(tau + 1) * 8], sb_ident32[:]))
                        for tl in range(8):
                            tau = 8 * ch + tl
                            pe.wait_ge(V_B2.h, scan_done[tau])
                            if tau >= 2:
                                pe.wait_ge(V_B2.h, posev_done[tau - 2])
                            PE_TT.inc(pe.transpose(
                                ff_ps[2 + tau % 2][0:128, 0:8],
                                sb_posT[0:8, tl * 128:(tl + 1) * 128],
                                sb_ident32[0:8, 0:8]))
                    if not skip_b2:
                        pe.wait_ge(V_B2.h, b2c["total"])
                    evg = 0
                    for u in range(NU) if not nomm else []:
                        kind, a1, a2 = units[u]
                        ev_list = unit_event_list(u)
                        for what, idx in ev_list:
                            n = evg
                            if n >= 8:
                                snm, val = evac_sched[n - 8]
                                pe.wait_ge(sems[snm].h, val)
                            bank = n % 8
                            if what == "T":
                                urt = u - NCH
                                tl, k = idx // 8, idx % 8
                                gam = 4 * urt + tl
                                if k == 0:
                                    pe.wait_ge(SGATH[gam % 4].h, 16 * (gam // 4 + 1))
                                PE_TT.inc(pe.transpose(
                                    ff_ps[bank][:, 0:128],
                                    sb_tstage[gam % 4][:, k * 128:(k + 1) * 128],
                                    sb_ident32[:]))
                            elif what == "S1":
                                m = idx
                                if m % 4 == 0:
                                    w1seq = 8 * u + m // 4
                                    pe.wait_ge(SW1[w1seq % 3].h, 16 * (w1seq // 3 + 1))
                                if kind == "sh":
                                    if m == 0:
                                        g = a1
                                        pe.wait_ge(SY1G[g % 2].h, 16 * (g // 2 + 1))
                                    rhs_t = sb_y1g[a1 % 2]
                                else:
                                    if m == 0:
                                        pe.wait_ge(V_TE.h, 32 * (u - NCH + 1))
                                    rhs_t = sb_y1te
                                w1seq = 8 * u + m // 4
                                for k in range(KT):
                                    inst = pe.matmul(
                                        ff_ps[bank][:, :],
                                        ring1[w1seq % 3][:, (m % 4) * KT * 128 + k * 128: (m % 4) * KT * 128 + (k + 1) * 128],
                                        rhs_t[:, k * CH:(k + 1) * CH],
                                        start=(k == 0), stop=(k == KT - 1))
                                    if m == 0 and k == 0 and u >= 1:
                                        W2FREE.inc(inst)
                                PE_S1.inc(inst)
                            else:
                                if idx == 0 and not nos2:
                                    pe.wait_ge(S_GELU.h, 32 * (u + 1))
                                    if kind == "sh" and not nobias:
                                        g = a1
                                        for j in range(8):
                                            tt, dh = j // 2, j % 2
                                            pe.matmul(
                                                ff_ps[j][:, :],
                                                sb_gateT[0:9, g * CH + tt * 128: g * CH + (tt + 1) * 128],
                                                sb_bprojT[0:9, dh * 512:(dh + 1) * 512],
                                                start=True, stop=False)
                                    for k in range(32):
                                        q, kk = k // 4, k % 4
                                        if kk == 0:
                                            w2seq = 8 * u + q
                                            pe.wait_ge(SW2[w2seq % 4].h, 16 * (w2seq // 4 + 1))
                                        for j in range(8):
                                            tt, dh = j // 2, j % 2
                                            inst = pe.matmul(
                                                ff_ps[j][:, :],
                                                sb_h[:, k * CH + tt * 128: k * CH + (tt + 1) * 128],
                                                ring2[(8 * u + q) % 4][:, kk * 1024 + dh * 512: kk * 1024 + (dh + 1) * 512],
                                                start=((kind == "rt" or nobias) and k == 0), stop=(k == 31))
                                            if k == 31:
                                                PE_S2.inc(inst)
                                            elif kk == 0 and j == 0 and k > 0:
                                                W2FREE.inc(inst)
                            evg += 1

                @blk.scalar
                def _(s):
                    wait_setup(s)
                    if skip_b1 or nomm:
                        return
                    s.wait_ge(SCAST.h, 16 * 4 * NE)

                    def emit_w2(u_, q0, q1):
                        kindw, aw, _ = units[u_]
                        we = 0 if kindw == "sh" else aw
                        for q in range(q0, q1):
                            w2seq = 8 * u_ + q
                            if w2seq >= 4:
                                s.wait_ge(W2FREE.h, w2seq - 3)
                            SW2[w2seq % 4].inc(s.dma_start(ring2[w2seq % 4][:], w2_slice(we, q)), 16)

                    for u in range(NU):
                        kind, a1, a2 = units[u]
                        e_eff = 0 if kind == "sh" else a1
                        base_ev = 32 if kind == "rt" else 0
                        if not nos2:
                            emit_w2(u, 0, 4)
                        for m in range(MT):
                            s.wait_ge(PE_S1.h, 32 * u + m + 1)
                            bank = (b_eventbase[u] + base_ev + m) % 8
                            S_GELU.inc(s.activation(
                                sb_h[:, m * CH:(m + 1) * CH], ff_ps[bank][:, :],
                                AF.Gelu_apprx_tanh,
                                bias=sb_bfcT[:, e_eff * MT + m: e_eff * MT + m + 1], scale=1.0))
                        if not nos2:
                            emit_w2(u, 4, 8)
                        if kind == "rt":
                            urt = u - NCH
                            for tt in range(TPC):
                                sig = 4 * urt + tt
                                s.wait_ge(PE_S2.h, 8 * u + 2 * tt + 2)
                                s.wait_ge(V_GC.h, 4 * urt + tt + 1)
                                if sig >= 2:
                                    s.wait_ge(SCAT.h, 16 * (sig - 1))
                                s.mul(sb_routed[tt % 2][:, 0:512], ff_ps[2 * tt][:, :],
                                      sb_gcol[urt % 2][:, tt:tt + 1])
                                S_MUL.inc(s.mul(
                                    sb_routed[tt % 2][:, 512:1024], ff_ps[2 * tt + 1][:, :],
                                    sb_gcol[urt % 2][:, tt:tt + 1]))

        nc.all_engine_barrier()

        with contextlib.ExitStack() as stB4:
            sb_a4 = [stB4.enter_context(nc.sbuf_tensor(f"sb_a4_{i}", [128, D], F32)) for i in range(2)]
            sb_y4 = [stB4.enter_context(nc.sbuf_tensor(f"sb_y4_{i}", [128, D], F32)) for i in range(2)]
            sb_rf = [stB4.enter_context(nc.sbuf_tensor(f"sb_rf{i}", [128, D], F32)) for i in range(2)]
            sb_outt = [stB4.enter_context(nc.sbuf_tensor(f"sb_outt{i}", [128, D], F32)) for i in range(2)]
            sb_junk2 = stB4.enter_context(nc.sbuf_tensor("sb_junk2", [128, D], F32))
            sb_lst2 = [stB4.enter_context(nc.sbuf_tensor(f"sb_lst2_{i}", [128, 8], F32)) for i in range(2)]

            if skip_b4:
                with nc.Block() as blk:
                    @blk.sync
                    def _(sy):
                        sy.wait_ge(SCAT.h, 16 * 4 * NRT_ACT)
                        for s_ in [SIDX, SLIST[0], SLIST[1], SY1G[0], SY1G[1]] + SW1 + SW2 + SACCW:
                            if s_.n > 0:
                                sy.wait_ge(s_.h, s_.n)
                        for tau in range(NTAU):
                            SB4A.inc(sy.dma_start(sb_a4[0][:], acc_hbm[tau * 128:(tau + 1) * 128, :]), 16)
                            sy.wait_ge(SB4A.h, SB4A.n)
                            SOUT[0].inc(sy.dma_start(out[tau * 128:(tau + 1) * 128, :], sb_a4[0][:]), 16)
                            sy.wait_ge(SOUT[0].h, SOUT[0].n)
                return nc

            with nc.Block() as blk:
                @blk.sync
                def _(sy):
                    sy.wait_ge(SCAT.h, 16 * 4 * NRT_ACT)
                    for tau in range(NTAU):
                        if tau >= 2:
                            sy.wait_ge(V_FIN.h, tau - 1)
                        SB4A.inc(sy.dma_start(sb_a4[tau % 2][:], acc_hbm[tau * 128:(tau + 1) * 128, :]), 16)
                        SB4Y.inc(sy.dma_start(sb_y4[tau % 2][:], y1g8[tau * 128:(tau + 1) * 128, 0:D]), 16)
                        sy.wait_ge(V_OUTR.h, tau + 1)
                        SOUT[tau % 2].inc(sy.dma_start(
                            out[tau * 128:(tau + 1) * 128, :], sb_outt[tau % 2][:]), 16)
                    sy.wait_ge(SOUT[0].h, 16 * (NTAU // 2))
                    sy.wait_ge(SOUT[1].h, 16 * (NTAU // 2))
                    for s_ in [SIDX, SB4A, SB4Y, SLIST[0], SLIST[1], SY1G[0], SY1G[1]]:
                        if s_.n > 0:
                            sy.wait_ge(s_.h, s_.n)

                @blk.scalar
                def _(s):
                    for tau in range(NTAU):
                        lp = tau % 2
                        s.wait_ge(V_D2.h, tau + 1)
                        S_SQ2.inc(s.activation(
                            sb_junk2[:], sb_rf[lp][:], AF.Square,
                            accum_out=sb_lst2[lp][:, 2:3]))
                        s.wait_ge(S_SQ2.h, S_SQ2.n)
                        nss = SS.inc(s.activation(
                            sb_lst2[lp][:, 3:4], sb_lst2[lp][:, 2:3], AF.Ln,
                            bias=sb_cst[:, 0:1], scale=sb_cst[:, 2:3]))
                        s.wait_ge(SS.h, nss)
                        S_RSTD2.inc(s.activation(
                            sb_lst2[lp][:, 4:5], sb_lst2[lp][:, 3:4], AF.Exp,
                            scale=sb_cst[:, 1:2]))

                @blk.vector
                def _(v):
                    for tau in range(NTAU):
                        lp = tau % 2
                        rf = sb_rf[lp]
                        v.wait_ge(SB4A.h, 16 * (tau + 1))
                        v.wait_ge(SB4Y.h, 16 * (tau + 1))
                        if tau >= 2:
                            v.wait_ge(S_SQ2.h, tau - 1)
                            v.wait_ge(S_RSTD2.h, tau - 1)
                            v.wait_ge(SOUT[tau % 2].h, 16 * (tau // 2))
                        nvs = VS.inc(v.tensor_tensor(
                            rf[:, 0:512], sb_a4[lp][:, 0:512], sb_y4[lp][:, 0:512], ALU.add))
                        V_FIN.inc(v.tensor_tensor(
                            rf[:, 512:1024], sb_a4[lp][:, 512:1024], sb_y4[lp][:, 512:1024], ALU.add))
                        v.wait_ge(VS.h, nvs)
                        v.wait_ge(V_FIN.h, V_FIN.n)
                        nvs = VS.inc(v.tensor_reduce(sb_lst2[lp][:, 0:1], rf[:], AX.X, ALU.add))
                        v.wait_ge(VS.h, nvs)
                        nvs = VS.inc(v.tensor_scalar(
                            sb_lst2[lp][:, 1:2], sb_lst2[lp][:, 0:1], -1.0 / D, None, ALU.mult))
                        v.wait_ge(VS.h, nvs)
                        V_D2.inc(v.tensor_scalar(
                            rf[:], rf[:], sb_lst2[lp][:, 1:2], None, ALU.add))
                        v.wait_ge(S_RSTD2.h, tau + 1)
                        V_OUTR.inc(v.tensor_scalar(
                            sb_outt[tau % 2][:], rf[:], sb_lst2[lp][:, 4:5], None, ALU.mult))

    return nc


def kernel(L0_attn_x, L0_attn_w_qkv, L0_attn_w_out, L0_ffn_Wfc_s, L0_ffn_bfc_s,
           L0_ffn_Wproj_s, L0_ffn_bproj_s, L0_ffn_Wr, L0_ffn_Wfc, L0_ffn_bfc,
           L0_ffn_Wproj, L0_ffn_bproj):
    xin = np.ascontiguousarray(np.asarray(L0_attn_x, dtype=np.float32))
    w_qkv = np.asarray(L0_attn_w_qkv, dtype=np.float32)
    w_out = np.asarray(L0_attn_w_out, dtype=np.float32)
    Wfc_s = np.asarray(L0_ffn_Wfc_s, dtype=np.float32)
    bfc_s = np.asarray(L0_ffn_bfc_s, dtype=np.float32)
    Wproj_s = np.asarray(L0_ffn_Wproj_s, dtype=np.float32)
    bproj_s = np.asarray(L0_ffn_bproj_s, dtype=np.float32)
    Wr = np.asarray(L0_ffn_Wr, dtype=np.float32)
    Wfc = np.asarray(L0_ffn_Wfc, dtype=np.float32)
    bfc = np.asarray(L0_ffn_bfc, dtype=np.float32)
    Wproj = np.asarray(L0_ffn_Wproj, dtype=np.float32)
    bproj = np.asarray(L0_ffn_bproj, dtype=np.float32)

    perm = hilbert_perm(T)
    idxT = np.ascontiguousarray(perm.reshape(NTAU, 128).T)
    tokid_np = np.ascontiguousarray(
        np.arange(T, dtype=np.int32).reshape(NTAU, 128).T)
    ident32 = np.eye(128, dtype=np.float32)
    identbf = np.eye(128, dtype=BF)
    wqkvT32 = np.ascontiguousarray(
        w_qkv.T.reshape(KT, 128, 3072).transpose(1, 0, 2).reshape(128, KT * 3072))
    woutT32 = np.ascontiguousarray(
        w_out.T.reshape(KT, 128, 1024).transpose(1, 0, 2).reshape(128, KT * 1024))
    wrT32 = np.ascontiguousarray(
        Wr.reshape(KT, 128, 8).transpose(1, 0, 2).reshape(128, KT * 8))
    bfc_all = np.concatenate([bfc_s[None], bfc], axis=0)
    bfcT32 = np.ascontiguousarray(
        bfc_all.reshape(NE, MT, 128).transpose(2, 0, 1).reshape(128, NE * MT))
    bprojT32 = np.ascontiguousarray(np.concatenate([bproj_s[None], bproj], axis=0))
    W1_all = np.concatenate([Wfc_s[None], Wfc], axis=0)
    w1t = np.ascontiguousarray(
        W1_all.reshape(NE, KT, 128, MT, 128).transpose(0, 3, 2, 1, 4)).reshape(NE, 2, D * HID // 2)
    W2_all = np.concatenate([Wproj_s[None], Wproj], axis=0)
    w2t = np.ascontiguousarray(
        W2_all.reshape(NE, 8, 4, 128, 1024).transpose(0, 1, 3, 2, 4)).reshape(NE, 2, D * HID // 2)

    nc = build_program()
    cst = np.zeros((128, 8), dtype=np.float32)
    cst[:, 0] = EPS
    cst[:, 1] = -0.5
    cst[:, 2] = 1.0 / D
    shared = {
        "idxT": idxT, "tokid": tokid_np,
        "ident32": ident32, "identbf": identbf, "cst32": cst,
        "wqkvT32": wqkvT32, "woutT32": woutT32, "wrT32": wrT32,
        "bfcT32": bfcT32, "bprojT32": bprojT32, "w1t": w1t, "w2t": w2t,
    }
    in_maps = []
    for b in range(B):
        m = dict(shared)
        m["x"] = np.ascontiguousarray(xin[b])
        in_maps.append(m)
    res = run_bass_kernel_spmd(nc, in_maps, core_ids=list(range(B)))
    return np.stack([r["out"] for r in res.results], axis=0)
